# version 08
# baseline (speedup 1.0000x reference)
import contextlib
import numpy as np
import concourse.bass as bass
import concourse.mybir as mybir
from concourse.bass_utils import run_bass_kernel_spmd

F32 = mybir.dt.float32
BF16 = mybir.dt.bfloat16
I32 = mybir.dt.int32
AF = mybir.ActivationFunctionType
ALU = mybir.AluOpType
ET = mybir.EngineType

D = 2048
S = 2048
NB = 4
H = 16
QL = 1536
KVL = 512
DN = 128
DR = 64
DV = 128
FH = 8192
TQ = 1024
SCALE = float((DN + DR) ** -0.5)
MAGIC = float(1.5 * 2 ** 23)
INV2PI = float(1.0 / (2.0 * np.pi))
TWOPI = float(2.0 * np.pi)
ROPE_ES = float(-np.log(10000.0) / 32.0)
EPS_LN = 1e-5
EPS_RMS = 1e-6
NT = 256
NTILES = S // NT

ENGS = ("tensor", "vector", "scalar", "gpsimd", "sync")



def _pack_lhsT(w, mb=128):
    Fd, Dk = w.shape
    a = w.reshape(Fd // mb, mb, Dk // 128, 128).transpose(0, 3, 2, 1)
    return np.ascontiguousarray(a.reshape(Fd // mb, 128, -1), dtype=np.float32)


_DEINT = np.concatenate([np.arange(0, 64, 2), np.arange(1, 64, 2)])


def _stage_weights(inp):
    qa = np.asarray(inp["L0_attn_q_a_w"], np.float32)
    qb = np.asarray(inp["L0_attn_q_b_w"], np.float32)
    kva = np.asarray(inp["L0_attn_kv_a_w"], np.float32)
    kvb = np.asarray(inp["L0_attn_kv_b_w"], np.float32)
    d = {
        "qa_pack": _pack_lhsT(qa, 128),
        "qaln_pack": np.ascontiguousarray(
            np.asarray(inp["L0_attn_q_a_ln_w"], np.float32).reshape(12, 128)),
        "kva_ckv": _pack_lhsT(kva[0:KVL, :], 128),
        "kva_pe": _pack_lhsT(kva[KVL + _DEINT, :], 64)[0],
        "kvaln_pack": np.ascontiguousarray(
            np.asarray(inp["L0_attn_kv_a_ln_w"], np.float32).reshape(4, 128)),
        "o_pack": _pack_lhsT(np.asarray(inp["L0_attn_o_w"], np.float32), 128),
        "g_pack": _pack_lhsT(np.asarray(inp["L0_ffn_w_gate"], np.float32), 128),
        "u_pack": _pack_lhsT(np.asarray(inp["L0_ffn_w_up"], np.float32), 128),
        "d_pack": _pack_lhsT(np.asarray(inp["L0_ffn_w_down"], np.float32), 128),
    }
    hw = np.empty((H, 128, 3328), np.float32)
    for h in range(H):
        r0 = h * (DN + DR)
        hw[h, :, 0:1536] = _pack_lhsT(qb[r0:r0 + DN, :], 128)[0]
        hw[h, :, 1536:2304] = _pack_lhsT(qb[r0 + DN + _DEINT, :], 64)[0]
        r1 = h * (DN + DV)
        hw[h, :, 2304:2816] = _pack_lhsT(kvb[r1:r1 + DN, :], 128)[0]
        hw[h, :, 2816:3328] = _pack_lhsT(kvb[r1 + DN:r1 + DN + DV, :], 128)[0]
    d["headw"] = hw
    return d



class Prog:
    def __init__(self, nc, es):
        self.nc = nc
        self.es = es
        self.q = {e: [] for e in ENGS}
        self.sems = {}
        self.cnt = {}
        self.enabled = True
        self.fol = {}

    def sem(self, name):
        self.sems[name] = self.es.enter_context(self.nc.semaphore(name))
        self.cnt[name] = 0

    def emit(self, eng, build_fn, waits=(), inc=None, inc_by=1, fol=True):
        if not self.enabled:
            return None
        tick = None
        if inc is not None:
            self.cnt[inc] += inc_by
            tick = (inc, self.cnt[inc])
        waits = tuple(w for w in waits if w is not None)
        sems = self.sems
        fol = self.fol.get(eng) if (inc is not None and fol) else None

        def fn(E, _w=waits, _i=inc, _b=build_fn, _n=inc_by, _f=fol):
            for (s, v) in _w:
                E.wait_ge(sems[s], v)
            ins = _b(E)
            if _f is not None:
                ins = _f(E)
            if _i is not None:
                ins.then_inc(sems[_i], _n)

        self.q[eng].append(fn)
        return tick

    def wait(self, eng, *ticks):
        if not self.enabled:
            return
        ticks = tuple(t for t in ticks if t is not None)
        if not ticks:
            return
        sems = self.sems

        def fn(E, _t=ticks):
            for (s, v) in _t:
                E.wait_ge(sems[s], v)

        self.q[eng].append(fn)

    def barrier(self):
        if not self.enabled:
            return None
        for e in ("tensor", "vector", "scalar"):
            self.emit(e, lambda E: E.nop(), inc="bar")
        t = ("bar", self.cnt["bar"])
        for e in ENGS:
            self.wait(e, t)
        return t



def build_nc(dbg=False, phases=4):
    nc = bass.Bass("TRN2", target_bir_lowering=False)

    xT = nc.dram_tensor("xT", [D, S], F32, kind="ExternalInput").ap()
    qa_pk = nc.dram_tensor("qa_pack", [12, 128, 2048], F32, kind="ExternalInput").ap()
    qaln_pk = nc.dram_tensor("qaln_pack", [12, 128], F32, kind="ExternalInput").ap()
    kva_ckv_pk = nc.dram_tensor("kva_ckv", [4, 128, 2048], F32, kind="ExternalInput").ap()
    kva_pe_pk = nc.dram_tensor("kva_pe", [128, 1024], F32, kind="ExternalInput").ap()
    kvaln_pk = nc.dram_tensor("kvaln_pack", [4, 128], F32, kind="ExternalInput").ap()
    hw_pk = nc.dram_tensor("headw", [H, 128, 3328], F32, kind="ExternalInput").ap()
    o_pk = nc.dram_tensor("o_pack", [16, 128, 2048], F32, kind="ExternalInput").ap()
    g_pk = nc.dram_tensor("g_pack", [64, 128, 2048], F32, kind="ExternalInput").ap()
    u_pk = nc.dram_tensor("u_pack", [64, 128, 2048], F32, kind="ExternalInput").ap()
    d_pk = nc.dram_tensor("d_pack", [16, 128, 8192], F32, kind="ExternalInput").ap()
    y_out = nc.dram_tensor("y", [D, TQ], F32, kind="ExternalOutput").ap()
    x1_scr = nc.dram_tensor("x1_scr", [16, 128, TQ], F32, kind="Internal").ap()
    brow_scr = nc.dram_tensor("brow_scr", [1, TQ], F32, kind="Internal").ap()

    dbg_out = {}
    if dbg:
        for nm, shp, dt in [("d_qmid", [128, 12 * TQ], F32),
                            ("d_ckv", [128, 4 * S], F32),
                            ("d_kpe", [65, S], F32),
                            ("d_pe2", [1, S], F32),
                            ("d_cat", [128, H * TQ], F32),
                            ("d_h2", [128, 16 * TQ], F32),
                            ("d_h1", [128, 16 * TQ], F32),
                            ("d_cos", [64, TQ], F32),
                            ("d_sin", [64, TQ], F32),
                            ("d_rt", [64, 64], F32),
                            ("d_lnw", [128, 16], F32),
                            ("d_qraw", [128, TQ], F32),
                            ("d_posk", [1, S], F32),
                            ("d_f2pi", [64, 1], F32),
                            ("d_qabuf", [128, 2048], F32),
                            ("d_qn0", [128, TQ], F32),
                            ("d_kn0", [128, S], F32),
                            ("d_qaug0", [65, TQ], F32),
                            ("d_vt0", [128, 2048], F32),
                            ("d_et0", [128, TQ], F32),
                            ("d_rbc0", [128, TQ], F32),
                            ("d_rows0", [1, 6144], F32),
                            ("d_brow0", [1, TQ], F32)]:
            dbg_out[nm] = nc.dram_tensor(nm, shp, dt, kind="ExternalOutput").ap()

    es_root = contextlib.ExitStack()
    with es_root as es:
        p = Prog(nc, es)
        for s in ("pe", "dv", "ac", "gp", "bar",
                  "dwk", "dwq0", "dwq1", "dwh0", "dwh1", "dwo0", "dwo1",
                  "dwg0", "dwg1", "dwu0", "dwu1", "dwd0", "dwd1", "dwb",
                  "dxx0", "dxx1", "dxc0", "dxc1", "dx1w0", "dx1w1",
                  "dx1r0", "dx1r1", "dlnw", "dxb", "dso"):
            p.sem(s)

        A = es.enter_context

        ones_bf = A(nc.sbuf_tensor("ones_bf", [128, 1], BF16))
        ones_f = A(nc.sbuf_tensor("ones_f", [1, 128], F32))
        zerob = A(nc.sbuf_tensor("zerob", [128, 1], F32))
        ident_f = A(nc.sbuf_tensor("ident_f", [128, 128], F32))
        iota_bc = A(nc.sbuf_tensor("iota_bc", [128, 128], F32))
        pidx128 = A(nc.sbuf_tensor("pidx128", [128, 1], F32))
        rt_bf = A(nc.sbuf_tensor("rt_bf", [64, 64], BF16))
        f2pi = A(nc.sbuf_tensor("f2pi", [64, 1], F32))
        pid_i = A(nc.sbuf_tensor("pid_i", [1, 1], I32))
        pid_f = A(nc.sbuf_tensor("pid_f", [1, 1], F32))
        halfv = A(nc.sbuf_tensor("halfv", [1, 1], F32))
        obase = A(nc.sbuf_tensor("obase", [1, 1], F32))
        scr64a = A(nc.sbuf_tensor("scr64a", [64, 64], F32))
        scr64b = A(nc.sbuf_tensor("scr64b", [64, 64], F32))
        scr64c = A(nc.sbuf_tensor("scr64c", [64, 64], F32))
        scr64d = A(nc.sbuf_tensor("scr64d", [64, 64], F32))
        qaln_sb = A(nc.sbuf_tensor("qaln_sb", [128, 12], F32))
        kvaln_sb = A(nc.sbuf_tensor("kvaln_sb", [128, 4], F32))
        lnw_raw = A(nc.sbuf_tensor("lnw_raw", [12, 128], F32))
        rows = A(nc.sbuf_tensor("rows", [1, 6144], F32))
        km2 = A(nc.sbuf_tensor("km2", [1, 1], F32))
        folv = A(nc.sbuf_tensor("folv", [1, 256], F32))
        fols = A(nc.sbuf_tensor("fols", [1, 256], F32))
        p.fol["vector"] = lambda E: E.tensor_copy(folv[0:1, 0:8], folv[0:1, 0:8])
        p.fol["scalar"] = lambda E: E.copy(fols[0:1, 0:8], fols[0:1, 0:8])

        def SPC(E):
            return E.tensor_copy(folv[0:1, 0:256], folv[0:1, 0:256])

        def R(name):
            a, b = {"pe2": (0, 2048), "kt2": (2048, 4096),
                    "qn2": (4096, 5120), "bias": (5120, 6144),
                    "sum": (4096, 5120), "rsum": (5120, 6144),
                    "crow": (0, 2048)}[name]
            return rows[0:1, a:b]

        ab1_scope = es.enter_context(contextlib.ExitStack())
        AB1 = ab1_scope.enter_context
        cos_own = AB1(nc.sbuf_tensor("cos_own", [64, TQ], F32))
        sin_own = AB1(nc.sbuf_tensor("sin_own", [64, TQ], F32))
        ckv_bf = AB1(nc.sbuf_tensor("ckv_bf", [128, 4 * S], BF16))
        kpe_aug = AB1(nc.sbuf_tensor("kpe_aug", [65, S], BF16))

        h1r_scope = es.enter_context(contextlib.ExitStack())
        h1_own = h1r_scope.enter_context(
            nc.sbuf_tensor("h1_own", [128, 16 * TQ], BF16, side="right"))

        akv_scope = es.enter_context(contextlib.ExitStack())
        AKV = akv_scope.enter_context
        cos_k = AKV(nc.sbuf_tensor("cos_k", [64, S], F32))
        sin_k = AKV(nc.sbuf_tensor("sin_k", [64, S], F32))
        kva_ckv_bf = AKV(nc.sbuf_tensor("kva_ckv_bf", [128, 4 * 2048], BF16))
        kva_pe_bf = AKV(nc.sbuf_tensor("kva_pe_bf", [128, 1024], BF16))
        kpe_bf_t = AKV(nc.sbuf_tensor("kpe_bf_t", [64, NT], BF16))
        xt_buf = [AKV(nc.sbuf_tensor(f"xt{i}", [128, 16 * NT], F32))
                  for i in range(2)]
        sq_buf = AKV(nc.sbuf_tensor("sq_buf", [128, 16 * NT], BF16))
        h1_tmp = AKV(nc.sbuf_tensor("h1_tmp", [128, 16 * NT], BF16))
        tmp_f = AKV(nc.sbuf_tensor("tmp_f", [128, 1024], F32))
        abrow = AKV(nc.sbuf_tensor("abrow", [1, 1024], F32))

        aps_scope = es.enter_context(contextlib.ExitStack())
        APS = aps_scope.enter_context
        ps_main = APS(nc.psum_tensor("ps_main", [128, 1024], F32))
        ps_bc = APS(nc.psum_tensor("ps_bc", [128, 1024], F32))
        ps_pe = APS(nc.psum_tensor("ps_pe", [64, 512], F32))
        ps_r1 = APS(nc.psum_tensor("ps_r1", [1, 512], F32))
        ps_r2 = APS(nc.psum_tensor("ps_r2", [1, 512], F32))

        pre_scope = contextlib.ExitStack()
        PRE = pre_scope.enter_context
        iota_row = PRE(nc.sbuf_tensor("iota_row", [1, 2048], F32))
        posk_row = PRE(nc.sbuf_tensor("posk_row", [1, 2048], F32))
        rope_t1 = PRE(nc.sbuf_tensor("rope_t1", [64, 1024], F32))
        rope_t2 = PRE(nc.sbuf_tensor("rope_t2", [64, 1024], F32))

        p.emit("gpsimd", lambda E: E.iota(iota_bc[:, :], pattern=[[1, 128]], base=0,
                                          channel_multiplier=0,
                                          allow_small_or_imprecise_dtypes=True))
        p.emit("gpsimd", lambda E: E.iota(pidx128[:, :], pattern=[[0, 1]], base=0,
                                          channel_multiplier=1,
                                          allow_small_or_imprecise_dtypes=True))
        p.emit("gpsimd",
               lambda E: E.iota(iota_row[0:1, :], pattern=[[1, 2048]], base=0,
                                channel_multiplier=0,
                                allow_small_or_imprecise_dtypes=True))
        t_gp = p.emit("gpsimd",
                      lambda E: E.iota(folv[0:1, 0:8], pattern=[[1, 8]], base=0,
                                       channel_multiplier=0,
                                       allow_small_or_imprecise_dtypes=True),
                      inc="gp")

        def _pre_a(E):
            E.store(pid_i[0:1, 0:1], nc.partition_id(engines=[ET.DVE]))
            E.memset(ones_bf[:, :], 1.0)
            E.memset(ones_f[:, :], 1.0)
            E.tensor_copy(pid_f[:, :], pid_i[:, :])
            E.memset(zerob[:, :], 0.0)
            E.memset(kpe_aug[64:65, :], -1.0)
            E.memset(scr64b[0:1, 0:8], 0.0)
        p.emit("vector", _pre_a)
        p.wait("vector", t_gp)

        def PP(vfn, prev=[None]):
            t = p.emit("vector", vfn, waits=[prev[0]], inc="dv")
            prev[0] = p.emit("scalar",
                             lambda E: E.copy(fols[0:1, 0:8], fols[0:1, 0:8]),
                             waits=[t], inc="ac")
            return prev[0]

        def _v1(E):
            E.tensor_scalar(ident_f[:, :], iota_bc[:, :], pidx128[:, 0:1], None,
                            ALU.is_equal)
            E.tensor_scalar(scr64a[:, :], iota_bc[0:64, 0:64],
                            pidx128[0:64, 0:1], None, ALU.subtract)
            E.tensor_scalar(scr64b[0:1, 1:2], pid_f[:, :], 1.0, None, ALU.is_equal)
            E.tensor_scalar(scr64b[0:1, 3:4], pid_f[:, :], 3.0, None, ALU.is_equal)
            E.tensor_scalar(scr64b[0:1, 5:6], pid_f[:, :], 5.0, None, ALU.is_equal)
            E.tensor_scalar(scr64b[0:1, 7:8], pid_f[:, :], 7.0, None, ALU.is_equal)
            return E.tensor_scalar(scr64d[0:64, 0:1], pidx128[0:64, 0:1], 32.0,
                                   None, ALU.is_ge)
        PP(_v1)

        def _v2(E):
            E.tensor_scalar(scr64c[:, :], scr64a[:, :], 32.0, None, ALU.is_equal)
            E.tensor_add(scr64b[0:1, 0:1], scr64b[0:1, 1:2], scr64b[0:1, 3:4])
            return E.tensor_scalar(scr64d[0:64, 1:2], scr64d[0:64, 0:1], 32.0,
                                   None, ALU.mult)
        PP(_v2)

        def _v3(E):
            E.tensor_scalar(scr64a[:, :], scr64a[:, :], -32.0, None, ALU.is_equal)
            E.tensor_add(scr64b[0:1, 2:3], scr64b[0:1, 5:6], scr64b[0:1, 7:8])
            return E.tensor_sub(f2pi[:, :], pidx128[0:64, 0:1],
                                scr64d[0:64, 1:2])
        PP(_v3)

        def _v4(E):
            E.tensor_sub(scr64d[:, :], scr64c[:, :], scr64a[:, :])
            return E.tensor_add(halfv[:, :], scr64b[0:1, 0:1], scr64b[0:1, 2:3])
        PP(_v4)

        def _v5(E):
            E.tensor_copy(rt_bf[:, :], scr64d[:, :])
            return E.tensor_scalar(halfv[:, :], halfv[:, :], 1024.0, None,
                                   ALU.mult)
        PP(_v5)

        def _v6(E):
            E.tensor_scalar(obase[:, :], halfv[:, :], -1.0, None, ALU.mult)
            return E.tensor_scalar(posk_row[0:1, 0:1024], iota_row[0:1, 0:1024],
                                   halfv[0:1, 0:1], None, ALU.add)
        PP(_v6)

        def _v7(E):
            return E.tensor_scalar(obase[:, :], obase[:, :], 1024.0, None,
                                   ALU.add)
        PP(_v7)

        def _v8(E):
            return E.tensor_scalar(posk_row[0:1, 1024:2048],
                                   iota_row[0:1, 0:1024], obase[0:1, 0:1], None,
                                   ALU.add)
        t_prev2 = PP(_v8)

        p.emit("scalar", lambda E: E.activation(f2pi[:, :], f2pi[:, :], AF.Exp,
                                                bias=zerob[0:64, 0:1],
                                                scale=ROPE_ES))
        p.emit("scalar", lambda E: E.copy(fols[0:1, 0:256], fols[0:1, 0:256]))
        t_f2pi = p.emit("scalar", lambda E: E.mul(f2pi[:, :], f2pi[:, :], INV2PI),
                        inc="ac")

        t_d = p.emit("sync", lambda E: E.dma_start(lnw_raw[0:12, :], qaln_pk),
                     inc="dlnw", inc_by=16)
        p.wait("tensor", t_d, t_prev2)
        t_pe = p.emit("tensor", lambda E: E.transpose(ps_main[0:128, 0:12],
                                                      lnw_raw[0:12, :],
                                                      ident_f[0:12, 0:12]),
                      inc="pe")
        p.wait("vector", t_pe)
        t_dv = p.emit("vector", lambda E: E.tensor_copy(qaln_sb[:, :],
                                                        ps_main[0:128, 0:12]),
                      inc="dv")
        t_d = p.emit("sync", lambda E: E.dma_start(lnw_raw[0:4, :], kvaln_pk),
                     waits=[t_dv], inc="dlnw", inc_by=16)
        p.wait("tensor", t_d)
        t_pe = p.emit("tensor", lambda E: E.transpose(ps_main[0:128, 16:20],
                                                      lnw_raw[0:4, :],
                                                      ident_f[0:4, 0:4]),
                      inc="pe")
        p.wait("vector", t_pe)
        t_kvln = p.emit("vector", lambda E: E.tensor_copy(kvaln_sb[:, :],
                                                          ps_main[0:128, 16:20]),
                        inc="dv")

        p.wait("tensor", t_prev2)
        t_tabv = None
        for tb in range(2):
            p.wait("tensor", t_tabv)
            for j in range(2):
                p.emit("tensor",
                       lambda E, tb=tb, j=j: E.matmul(
                           ps_bc[0:64, j * 512:(j + 1) * 512], ones_f[0:1, 0:64],
                           posk_row[0:1, tb * 1024 + j * 512:
                                    tb * 1024 + (j + 1) * 512],
                           start=True, stop=True),
                       inc="pe" if j == 1 else None)
            t_pb = ("pe", p.cnt["pe"])
            p.wait("vector", t_pb, t_f2pi)

            def _tab_v(E, tb=tb):
                E.tensor_scalar(rope_t1[:, :], ps_bc[0:64, :], f2pi[:, 0:1], None,
                                ALU.mult)
                SPC(E)
                E.tensor_scalar(rope_t2[:, :], rope_t1[:, :], MAGIC, -MAGIC,
                                ALU.add, ALU.add)
                SPC(E)
                E.tensor_sub(rope_t2[:, :], rope_t1[:, :], rope_t2[:, :])
                SPC(E)
                E.tensor_scalar(sin_k[:, tb * 1024:(tb + 1) * 1024],
                                rope_t2[:, :], TWOPI, None, ALU.mult)
                E.tensor_scalar(rope_t1[:, :], rope_t1[:, :], 0.25, None, ALU.add)
                SPC(E)
                E.tensor_scalar(rope_t2[:, :], rope_t1[:, :], MAGIC, -MAGIC,
                                ALU.add, ALU.add)
                SPC(E)
                E.tensor_sub(rope_t2[:, :], rope_t1[:, :], rope_t2[:, :])
                SPC(E)
                return E.tensor_scalar(cos_k[:, tb * 1024:(tb + 1) * 1024],
                                rope_t2[:, :], TWOPI, None, ALU.mult)
            t_tabv = p.emit("vector", _tab_v, inc="dv")
        p.wait("scalar", t_tabv)
        p.emit("scalar", lambda E: E.activation(sin_k[:, :], sin_k[:, :], AF.Sin,
                                                bias=zerob[0:64, 0:1], scale=1.0))
        t_tabs = p.emit("scalar", lambda E: E.activation(cos_k[:, :], cos_k[:, :],
                                                         AF.Sin,
                                                         bias=zerob[0:64, 0:1],
                                                         scale=1.0), inc="ac")
        p.wait("vector", t_tabs)

        def _tab_cp(E):
            E.tensor_copy(cos_own[:, :], cos_k[:, 0:TQ])
            return E.tensor_copy(sin_own[:, :], sin_k[:, 0:TQ])
        t_tabcp = p.emit("vector", _tab_cp, inc="dv")
        pre_scope.close()

        for mb in range(4):
            p.emit("gpsimd",
                   lambda E, mb=mb: E.dma_start(
                       kva_ckv_bf[:, mb * 2048:(mb + 1) * 2048], kva_ckv_pk[mb]),
                   inc="dwk", inc_by=16)
        p.emit("gpsimd", lambda E: E.dma_start(kva_pe_bf[:, :], kva_pe_pk),
               inc="dwk", inc_by=16)
        t_kva = ("dwk", p.cnt["dwk"])

        xT_r = xT.rearrange("(kc p) s -> p kc s", p=128)
        t_xload = [None, None]
        t_xfree = [None, None]
        t_sqfree = [None, None]

        def h1sl(tt, kc):
            if tt < 4:
                return h1_own[:, kc * TQ + tt * NT: kc * TQ + tt * NT + NT]
            return h1_tmp[:, kc * NT:(kc + 1) * NT]

        t_tile_vlast = None
        t_kv_prev = None
        t_pm_prev = None
        t_ck_prev = None
        t_stat = None
        for tt in range(NTILES):
            par = tt % 2
            sem = f"dxx{par}"
            t_xload[par] = p.emit(
                "sync",
                lambda E, par=par, tt=tt: E.dma_start(
                    xt_buf[par][:, :].rearrange("p (kc s) -> p kc s", kc=16),
                    xT_r[:, :, tt * NT:(tt + 1) * NT]),
                waits=[t_xfree[par], t_sqfree[par]], inc=sem, inc_by=16)

            p.wait("vector", t_xload[par],
                   t_tabcp if tt == 0 else t_kv_prev)
            t_cast = p.emit("vector",
                            lambda E, par=par: E.tensor_copy(h1_tmp[:, :],
                                                             xt_buf[par][:, :]),
                            inc="dv")
            p.wait("scalar", t_xload[par], t_pm_prev)
            t_sq = p.emit("scalar",
                          lambda E, par=par: E.activation(sq_buf[:, :],
                                                          xt_buf[par][:, :],
                                                          AF.Square,
                                                          bias=zerob[:, 0:1],
                                                          scale=1.0), inc="ac")
            t_sqfree[par] = t_sq
            p.wait("tensor", t_cast, t_sq,
                   t_tile_vlast if tt >= 1 else ("dv", p.cnt["dv"]))
            for kc in range(16):
                p.emit("tensor",
                       lambda E, kc=kc: E.matmul(ps_r1[0:1, 0:NT], ones_bf[:, 0:1],
                                                 h1_tmp[:, kc * NT:(kc + 1) * NT],
                                                 start=(kc == 0), stop=(kc == 15)))
            for kc in range(16):
                p.emit("tensor",
                       lambda E, kc=kc: E.matmul(ps_r2[0:1, 0:NT], ones_bf[:, 0:1],
                                                 sq_buf[:, kc * NT:(kc + 1) * NT],
                                                 start=(kc == 0), stop=(kc == 15)),
                       inc="pe" if kc == 15 else None)
            t_stat = ("pe", p.cnt["pe"])

            p.wait("vector", t_stat)

            def _ln_rows(E):
                E.tensor_scalar(abrow[0:1, 0:NT], ps_r1[0:1, 0:NT], 1.0 / D, None,
                                ALU.mult)
                E.tensor_scalar(abrow[0:1, NT:2 * NT], ps_r2[0:1, 0:NT], 1.0 / D,
                                None, ALU.mult)
                E.tensor_mul(tmp_f[0:1, 0:NT], abrow[0:1, 0:NT], abrow[0:1, 0:NT])
                SPC(E)
                E.tensor_sub(abrow[0:1, NT:2 * NT], abrow[0:1, NT:2 * NT],
                             tmp_f[0:1, 0:NT])
                SPC(E)
                return E.tensor_scalar(abrow[0:1, NT:2 * NT], abrow[0:1, NT:2 * NT],
                                EPS_LN, None, ALU.add)
            t_r1 = p.emit("vector", _ln_rows, inc="dv")
            p.wait("scalar", t_r1)
            p.emit("scalar", lambda E: E.activation(abrow[0:1, NT:2 * NT],
                                                    abrow[0:1, NT:2 * NT], AF.Ln,
                                                    bias=zerob[0:1, 0:1],
                                                    scale=1.0))
            t_r2 = p.emit("scalar", lambda E: E.activation(abrow[0:1, NT:2 * NT],
                                                           abrow[0:1, NT:2 * NT],
                                                           AF.Exp,
                                                           bias=zerob[0:1, 0:1],
                                                           scale=-0.5), inc="ac")
            p.wait("vector", t_r2)

            def _brow(E):
                E.tensor_mul(abrow[0:1, 0:NT], abrow[0:1, 0:NT],
                             abrow[0:1, NT:2 * NT])
                SPC(E)
                return E.tensor_scalar(abrow[0:1, 0:NT], abrow[0:1, 0:NT], -1.0, None,
                                ALU.mult)
            t_r3 = p.emit("vector", _brow, inc="dv")
            p.wait("tensor", t_r3)
            t_bc = p.emit("tensor",
                          lambda E: E.matmul(ps_bc[:, 0:512], ones_f[0:1, :],
                                             abrow[0:1, 0:512], start=True,
                                             stop=True), inc="pe")
            p.wait("vector", t_bc)

            def _h1_apply(E, par=par, tt=tt):
                ins = None
                for kc in range(16):
                    sl = kc % 2
                    E.tensor_mul(tmp_f[:, sl * NT:(sl + 1) * NT],
                                 xt_buf[par][:, kc * NT:(kc + 1) * NT],
                                 ps_bc[:, NT:2 * NT])
                    if kc > 0:
                        psl = (kc - 1) % 2
                        ins = E.tensor_tensor(h1sl(tt, kc - 1),
                                              tmp_f[:, psl * NT:(psl + 1) * NT],
                                              ps_bc[:, 0:NT], ALU.add)
                SPC(E)
                return E.tensor_tensor(h1sl(tt, 15), tmp_f[:, (15 % 2) * NT:
                                                           (15 % 2 + 1) * NT],
                                       ps_bc[:, 0:NT], ALU.add)
            p.emit("vector", _h1_apply, inc="dv")
            t_h1 = ("dv", p.cnt["dv"])
            t_xfree[par] = t_h1

            p.wait("tensor", t_h1, t_kva if tt == 0 else None, t_ck_prev)
            for cc in range(4):
                for kc in range(16):
                    p.emit("tensor",
                           lambda E, cc=cc, kc=kc, tt=tt: E.matmul(
                               ps_main[:, cc * NT:(cc + 1) * NT],
                               kva_ckv_bf[:, cc * 2048 + kc * 128:
                                          cc * 2048 + (kc + 1) * 128],
                               h1sl(tt, kc), start=(kc == 0), stop=(kc == 15)))
            for kc in range(16):
                p.emit("tensor",
                       lambda E, kc=kc, tt=tt: E.matmul(
                           ps_pe[0:64, 0:NT],
                           kva_pe_bf[:, kc * 64:(kc + 1) * 64],
                           h1sl(tt, kc), start=(kc == 0), stop=(kc == 15)),
                       inc="pe" if kc == 15 else None)
            t_kv = ("pe", p.cnt["pe"])
            t_kv_prev = t_kv

            p.wait("scalar", t_kv)
            for cc in range(4):
                p.emit("scalar",
                       lambda E, cc=cc: E.activation(
                           sq_buf[:, cc * NT:(cc + 1) * NT],
                           ps_main[:, cc * NT:(cc + 1) * NT], AF.Square,
                           bias=zerob[:, 0:1], scale=1.0),
                       inc="ac" if cc == 3 else None)
            t_csq = ("ac", p.cnt["ac"])
            p.wait("tensor", t_csq)
            for cc in range(4):
                p.emit("tensor",
                       lambda E, cc=cc: E.matmul(ps_r1[0:1, 0:NT], ones_bf[:, 0:1],
                                                 sq_buf[:, cc * NT:(cc + 1) * NT],
                                                 start=(cc == 0), stop=(cc == 3)),
                       inc="pe" if cc == 3 else None)
            t_cs2 = ("pe", p.cnt["pe"])
            p.wait("vector", t_cs2)
            t_rv = p.emit("vector",
                          lambda E: E.tensor_scalar(abrow[0:1, 0:NT],
                                                    ps_r1[0:1, 0:NT], 1.0 / KVL,
                                                    EPS_RMS, ALU.mult, ALU.add),
                          inc="dv")
            p.wait("scalar", t_rv)
            p.emit("scalar", lambda E: E.activation(abrow[0:1, 0:NT],
                                                    abrow[0:1, 0:NT], AF.Ln,
                                                    bias=zerob[0:1, 0:1],
                                                    scale=1.0))
            t_rs = p.emit("scalar", lambda E: E.activation(abrow[0:1, 0:NT],
                                                           abrow[0:1, 0:NT],
                                                           AF.Exp,
                                                           bias=zerob[0:1, 0:1],
                                                           scale=-0.5), inc="ac")
            p.wait("tensor", t_rs)
            t_rbc = p.emit("tensor",
                           lambda E: E.matmul(ps_bc[:, 512:768], ones_f[0:1, :],
                                              abrow[0:1, 0:NT], start=True,
                                              stop=True), inc="pe")
            p.wait("vector", t_rbc)

            def _ckv_norm(E, tt=tt):
                for cc in range(4):
                    sl = cc % 2
                    E.tensor_scalar(tmp_f[:, sl * NT:(sl + 1) * NT],
                                    ps_main[:, cc * NT:(cc + 1) * NT],
                                    kvaln_sb[:, cc:cc + 1], None, ALU.mult)
                    if cc > 0:
                        psl = (cc - 1) % 2
                        E.tensor_tensor(
                            ckv_bf[:, (cc - 1) * S + tt * NT:
                                   (cc - 1) * S + (tt + 1) * NT],
                            tmp_f[:, psl * NT:(psl + 1) * NT],
                            ps_bc[:, 512:768], ALU.mult)
                SPC(E)
                return E.tensor_tensor(
                    ckv_bf[:, 3 * S + tt * NT: 3 * S + (tt + 1) * NT],
                    tmp_f[:, NT:2 * NT], ps_bc[:, 512:768], ALU.mult)
            p.emit("vector", _ckv_norm, inc="dv")
            t_ck_prev = ("dv", p.cnt["dv"])

            t_kpev = p.emit("vector",
                            lambda E: E.tensor_copy(kpe_bf_t[:, :],
                                                    ps_pe[0:64, 0:NT]), inc="dv")
            p.wait("tensor", t_kpev)
            t_rx = p.emit("tensor",
                          lambda E: E.matmul(ps_pe[0:64, NT:2 * NT],
                                             rt_bf[:, :], kpe_bf_t[:, :],
                                             start=True, stop=True), inc="pe")
            p.wait("vector", t_rx)

            def _krope(E, tt=tt):
                E.tensor_mul(tmp_f[0:64, 0:NT], ps_pe[0:64, 0:NT],
                             cos_k[:, tt * NT:(tt + 1) * NT])
                E.tensor_mul(tmp_f[0:64, NT:2 * NT], ps_pe[0:64, NT:2 * NT],
                             sin_k[:, tt * NT:(tt + 1) * NT])
                SPC(E)
                return E.tensor_tensor(kpe_aug[0:64, tt * NT:(tt + 1) * NT],
                                tmp_f[0:64, 0:NT], tmp_f[0:64, NT:2 * NT], ALU.add)
            t_kr = p.emit("vector", _krope, inc="dv")
            p.wait("scalar", t_kr)
            t_ps = p.emit("scalar",
                          lambda E, tt=tt: E.activation(
                              sq_buf[0:64, 0:NT],
                              kpe_aug[0:64, tt * NT:(tt + 1) * NT], AF.Square,
                              bias=zerob[0:64, 0:1], scale=1.0), inc="ac")
            p.wait("tensor", t_ps)
            t_pm = p.emit("tensor",
                          lambda E: E.matmul(ps_r2[0:1, 256:256 + NT],
                                             ones_bf[0:64, 0:1],
                                             sq_buf[0:64, 0:NT],
                                             start=True, stop=True), inc="pe")
            t_pm_prev = t_pm
            p.wait("vector", t_pm)
            t_tile_vlast = p.emit(
                "vector",
                lambda E, tt=tt: E.tensor_copy(
                    R("pe2")[0:1, tt * NT:(tt + 1) * NT],
                    ps_r2[0:1, 256:256 + NT]),
                inc="dv")

        p.barrier()
        akv_scope.close()

        ab2_scope = es.enter_context(contextlib.ExitStack())
        AB2 = ab2_scope.enter_context
        qmid_bf = AB2(nc.sbuf_tensor("qmid_bf", [128, 12 * TQ], BF16))
        hw_buf = [AB2(nc.sbuf_tensor(f"hw_buf{i}", [128, 3328], BF16))
                  for i in range(2)]

        aq_scope = es.enter_context(contextlib.ExitStack())
        AQ = aq_scope.enter_context
        qa_buf = [AQ(nc.sbuf_tensor(f"qa_buf{i}", [128, 2048], BF16))
                  for i in range(2)]
        qmid_raw = AQ(nc.sbuf_tensor("qmid_raw", [128, 12 * TQ], F32))
        qsq_bf = AQ(nc.sbuf_tensor("qsq_bf", [128, TQ], BF16))
        qtmp_f = AQ(nc.sbuf_tensor("qtmp_f", [128, TQ], F32))
        qtmp2_f = AQ(nc.sbuf_tensor("qtmp2_f", [128, TQ], F32))
        qrow = AQ(nc.sbuf_tensor("qrow", [1, 1024], F32))

        t_qa_load = [None, None]
        t_qa_used = [None, None]
        t_qev = [None, None]
        t_qnmm_prev = None
        for mc in range(12):
            par = mc % 2
            PSQ = ps_main if par == 0 else ps_bc
            t_qa_load[par] = p.emit(
                "gpsimd",
                lambda E, mc=mc, par=par: E.dma_start(qa_buf[par][:, :], qa_pk[mc]),
                waits=[t_qa_used[par]], inc=f"dwq{par}", inc_by=16)
            p.wait("tensor", t_qa_load[par], t_qev[par],
                   ("dv", p.cnt["dv"]) if mc <= 1 else None)
            for kc in range(16):
                for nh in range(2):
                    p.emit("tensor",
                           lambda E, kc=kc, nh=nh, par=par, PSQ=PSQ: E.matmul(
                               PSQ[:, nh * 512:(nh + 1) * 512],
                               qa_buf[par][:, kc * 128:(kc + 1) * 128],
                               h1_own[:, kc * TQ + nh * 512:
                                      kc * TQ + (nh + 1) * 512],
                               start=(kc == 0), stop=(kc == 15)),
                           inc="pe" if (kc == 15 and nh == 1) else None)
            t_qa_used[par] = ("pe", p.cnt["pe"])
            p.wait("vector", t_qa_used[par])
            t_ev = p.emit("vector",
                          lambda E, mc=mc, PSQ=PSQ: E.tensor_copy(
                              qmid_raw[:, mc * TQ:(mc + 1) * TQ], PSQ[:, :]),
                          inc="dv")
            t_qev[par] = t_ev
            p.wait("scalar", t_ev, t_qnmm_prev if mc > 0 else t_stat)
            t_qsq = p.emit("scalar",
                           lambda E, mc=mc: E.activation(
                               qsq_bf[:, :], qmid_raw[:, mc * TQ:(mc + 1) * TQ],
                               AF.Square, bias=zerob[:, 0:1], scale=1.0), inc="ac")
            p.wait("tensor", t_qsq)
            for nh in range(2):
                p.emit("tensor",
                       lambda E, nh=nh, mc=mc: E.matmul(
                           (ps_r1 if nh == 0 else ps_r2)[0:1, 0:512],
                           ones_bf[:, 0:1], qsq_bf[:, nh * 512:(nh + 1) * 512],
                           start=(mc == 0), stop=(mc == 11)),
                       inc="pe" if nh == 1 else None)
            t_qnmm_prev = ("pe", p.cnt["pe"])
        t_qs2 = ("pe", p.cnt["pe"])
        p.wait("vector", t_qs2)

        def _qrms(E):
            E.tensor_scalar(qrow[0:1, 0:512], ps_r1[0:1, 0:512], 1.0 / QL,
                            EPS_RMS, ALU.mult, ALU.add)
            return E.tensor_scalar(qrow[0:1, 512:1024], ps_r2[0:1, 0:512], 1.0 / QL,
                            EPS_RMS, ALU.mult, ALU.add)
        t_qr = p.emit("vector", _qrms, inc="dv")
        p.wait("scalar", t_qr)
        p.emit("scalar", lambda E: E.activation(qrow[0:1, :], qrow[0:1, :],
                                                AF.Ln, bias=zerob[0:1, 0:1],
                                                scale=1.0))
        t_qr2 = p.emit("scalar", lambda E: E.activation(qrow[0:1, :],
                                                        qrow[0:1, :], AF.Exp,
                                                        bias=zerob[0:1, 0:1],
                                                        scale=-0.5), inc="ac")
        p.wait("tensor", t_qr2)
        for nh in range(2):
            p.emit("tensor",
                   lambda E, nh=nh: E.matmul(ps_bc[:, nh * 512:(nh + 1) * 512],
                                             ones_f[0:1, :],
                                             qrow[0:1, nh * 512:(nh + 1) * 512],
                                             start=True, stop=True),
                   inc="pe" if nh == 1 else None)
        t_qbc = ("pe", p.cnt["pe"])
        p.wait("vector", t_qbc)
        t_qapply = []

        def _q_apply_pair(E, mc):
            tmps = [qtmp_f, qtmp2_f]
            E.tensor_mul(tmps[mc % 2][:, :],
                         qmid_raw[:, mc * TQ:(mc + 1) * TQ], ps_bc[:, :])
            if mc > 0:
                return E.tensor_scalar(qmid_bf[:, (mc - 1) * TQ:mc * TQ],
                                       tmps[(mc - 1) % 2][:, :],
                                       qaln_sb[:, mc - 1:mc], None, ALU.mult)
            return SPC(E)

        for mc in range(12):
            t = p.emit("vector", lambda E, mc=mc: _q_apply_pair(E, mc), inc="dv")
            if mc > 0:
                t_qapply.append(t)
        t = p.emit("vector",
                   lambda E: (SPC(E),
                              E.tensor_scalar(qmid_bf[:, 11 * TQ:12 * TQ],
                                              qtmp2_f[:, :], qaln_sb[:, 11:12],
                                              None, ALU.mult))[-1], inc="dv")
        t_qapply.append(t)

        t_hw = [None] * H
        for h in range(2):
            t_hw[h] = p.emit("gpsimd",
                             lambda E, h=h: E.dma_start(hw_buf[h % 2][:, :],
                                                        hw_pk[h]),
                             inc=f"dwh{h % 2}", inc_by=16)

        if dbg:
            t_all = ("dv", p.cnt["dv"])
            p.wait("gpsimd", t_all)
            p.emit("gpsimd", lambda E: E.dma_start(dbg_out["d_qmid"], qmid_bf[:, :]),
                   inc="dso", inc_by=16)
            p.emit("gpsimd", lambda E: E.dma_start(dbg_out["d_ckv"], ckv_bf[:, :]),
                   inc="dso", inc_by=16)
            p.emit("gpsimd", lambda E: E.dma_start(dbg_out["d_kpe"], kpe_aug[:, :]),
                   inc="dso", inc_by=16)
            p.emit("gpsimd", lambda E: E.dma_start(dbg_out["d_pe2"], R("pe2")),
                   inc="dso", inc_by=16)
            p.emit("gpsimd", lambda E: E.dma_start(dbg_out["d_h1"], h1_own[:, :]),
                   inc="dso", inc_by=16)
            p.emit("gpsimd", lambda E: E.dma_start(dbg_out["d_cos"], cos_own[:, :]),
                   inc="dso", inc_by=16)
            p.emit("gpsimd", lambda E: E.dma_start(dbg_out["d_sin"], sin_own[:, :]),
                   inc="dso", inc_by=16)
            p.emit("gpsimd", lambda E: E.dma_start(dbg_out["d_rt"], rt_bf[:, :]),
                   inc="dso", inc_by=16)
            p.emit("gpsimd", lambda E: E.dma_start(dbg_out["d_lnw"][:, 0:12],
                                                   qaln_sb[:, :]),
                   inc="dso", inc_by=16)
            p.emit("gpsimd", lambda E: E.dma_start(dbg_out["d_lnw"][:, 12:16],
                                                   kvaln_sb[:, :]),
                   inc="dso", inc_by=16)
            p.emit("gpsimd", lambda E: E.dma_start(dbg_out["d_qraw"],
                                                   qmid_raw[:, 0:TQ]),
                   inc="dso", inc_by=16)
            p.emit("gpsimd", lambda E: E.dma_start(dbg_out["d_f2pi"], f2pi[:, :]),
                   inc="dso", inc_by=16)
            p.emit("gpsimd", lambda E: E.dma_start(dbg_out["d_qabuf"],
                                                   qa_buf[0][:, :]),
                   inc="dso", inc_by=16)

        p.enabled = phases >= 2
        aq_scope.close()
        aps_scope.close()
        h1r_scope.close()

        h2_bf = A(nc.sbuf_tensor("h2_bf", [128, 16 * TQ], BF16, side="right"))
        cat_scope = es.enter_context(contextlib.ExitStack())
        attn_cat = cat_scope.enter_context(
            nc.sbuf_tensor("attn_cat", [128, H * TQ], BF16, side="right"))

        b_scope = es.enter_context(contextlib.ExitStack())
        BS = b_scope.enter_context
        et_buf = [BS(nc.sbuf_tensor(f"et{i}", [128, TQ], BF16)) for i in range(2)]
        qn_bf = BS(nc.sbuf_tensor("qn_bf", [128, TQ], BF16))
        qpe_bf = BS(nc.sbuf_tensor("qpe_bf", [64, TQ], BF16))
        q_aug = BS(nc.sbuf_tensor("q_aug", [65, TQ], BF16))
        kn_bf = BS(nc.sbuf_tensor("kn_bf", [128, S], BF16))
        vt_bf = BS(nc.sbuf_tensor("vt_bf", [128, 16 * 128], BF16))
        sqb = BS(nc.sbuf_tensor("sqb", [128, 4096], BF16))
        rbc_sb = BS(nc.sbuf_tensor("rbc_sb", [128, TQ], F32))
        btmp1 = BS(nc.sbuf_tensor("btmp1", [64, TQ], F32))

        bps_scope = es.enter_context(contextlib.ExitStack())
        BPS = bps_scope.enter_context
        ps_proj = BPS(nc.psum_tensor("ps_proj", [128, 1024], F32))
        ps_sA = BPS(nc.psum_tensor("ps_sA", [128, 1024], F32))
        ps_sB = BPS(nc.psum_tensor("ps_sB", [128, 1024], F32))
        ps_av = BPS(nc.psum_tensor("ps_av", [128, 1024], F32))

        t_cat_prev = None
        t_hw_used = [None, None]
        t_rr_prev = None
        t_rbv_prev = None
        t_av_prev = None
        for h in range(H):
            par = h % 2
            HB = hw_buf[par]
            if h >= 2:
                t_hw[h] = p.emit("gpsimd",
                                 lambda E, h=h: E.dma_start(hw_buf[h % 2][:, :],
                                                            hw_pk[h]),
                                 waits=[t_hw_used[par]], inc=f"dwh{par}", inc_by=16)

            p.wait("tensor", t_hw[h], t_rr_prev)
            for kc in range(12):
                if h == 0:
                    p.wait("tensor", t_qapply[kc])
                for nh in range(2):
                    p.emit("tensor",
                           lambda E, kc=kc, nh=nh, HB=HB: E.matmul(
                               ps_proj[:, nh * 512:(nh + 1) * 512],
                               HB[:, kc * 128:(kc + 1) * 128],
                               qmid_bf[:, kc * TQ + nh * 512:kc * TQ + (nh + 1) * 512],
                               start=(kc == 0), stop=(kc == 11)),
                           inc="pe" if (kc == 11 and nh == 1) else None)
            t_qn = ("pe", p.cnt["pe"])
            p.wait("vector", t_qn)
            t_qnev = p.emit("vector",
                            lambda E: E.tensor_copy(qn_bf[:, :], ps_proj[:, :]),
                            inc="dv")
            p.wait("tensor", t_rbv_prev)
            for kc in range(12):
                for nh in range(2):
                    p.emit("tensor",
                           lambda E, kc=kc, nh=nh, HB=HB: E.matmul(
                               ps_sA[0:64, nh * 512:(nh + 1) * 512],
                               HB[:, 1536 + kc * 64:1536 + (kc + 1) * 64],
                               qmid_bf[:, kc * TQ + nh * 512:kc * TQ + (nh + 1) * 512],
                               start=(kc == 0), stop=(kc == 11)),
                           inc="pe" if (kc == 11 and nh == 1) else None)
            t_qpe = ("pe", p.cnt["pe"])
            p.wait("vector", t_qpe)
            t_qpev = p.emit("vector",
                            lambda E: E.tensor_copy(qpe_bf[:, :], ps_sA[0:64, :]),
                            inc="dv")
            p.wait("tensor", t_cat_prev)
            for half, PSK in ((0, ps_av), (1, ps_proj)):
                if half == 1:
                    p.wait("tensor", t_qnev)
                for kc in range(4):
                    for nh in range(2):
                        p.emit("tensor",
                               lambda E, kc=kc, nh=nh, half=half, PSK=PSK, HB=HB:
                               E.matmul(
                                   PSK[:, nh * 512:(nh + 1) * 512],
                                   HB[:, 2304 + kc * 128:2304 + (kc + 1) * 128],
                                   ckv_bf[:, kc * S + half * TQ + nh * 512:
                                          kc * S + half * TQ + (nh + 1) * 512],
                                   start=(kc == 0), stop=(kc == 3)),
                               inc="pe" if (kc == 3 and nh == 1) else None)
            p.wait("tensor", t_qpev)
            for nh in range(2):
                p.emit("tensor",
                       lambda E, nh=nh: E.matmul(
                           ps_sB[0:64, nh * 512:(nh + 1) * 512], rt_bf[:, :],
                           qpe_bf[:, nh * 512:(nh + 1) * 512],
                           start=True, stop=True),
                       inc="pe" if nh == 1 else None)
            t_rx = ("pe", p.cnt["pe"])
            p.wait("vector", t_rx)
            def _qrope(E):
                E.tensor_mul(q_aug[0:64, :], ps_sB[0:64, :], sin_own[:, :])
                E.tensor_mul(btmp1[:, :], ps_sA[0:64, :], cos_own[:, :])
                SPC(E)
                return E.tensor_tensor(q_aug[0:64, :], q_aug[0:64, :], btmp1[:, :],
                                ALU.add)
            t_qrope = p.emit("vector", _qrope, inc="dv")


            t_k = ("pe", p.cnt["pe"])
            p.wait("vector", t_k)
            t_kev = p.emit("vector",
                           lambda E: (E.tensor_copy(kn_bf[:, 0:TQ], ps_av[:, :]),
                                      E.tensor_copy(kn_bf[:, TQ:S],
                                                    ps_proj[:, :]))[-1],
                           inc="dv")

            p.wait("scalar", t_kev, t_qrope)
            t_sqq = p.emit("scalar",
                           lambda E: (E.activation(sqb[:, 2048:3072], qn_bf[:, :],
                                                   AF.Square, bias=zerob[:, 0:1],
                                                   scale=1.0),
                                      E.activation(sqb[0:64, 3072:4096],
                                                   q_aug[0:64, :], AF.Square,
                                                   bias=zerob[0:64, 0:1],
                                                   scale=1.0))[-1], inc="ac")
            t_sqk = p.emit("scalar",
                           lambda E: E.activation(sqb[:, 0:S], kn_bf[:, :],
                                                  AF.Square, bias=zerob[:, 0:1],
                                                  scale=1.0), inc="ac")
            p.wait("tensor", t_sqq)
            for nh in range(2):
                p.emit("tensor",
                       lambda E, nh=nh: E.matmul(
                           ps_sA[0:1, nh * 512:(nh + 1) * 512], ones_bf[:, 0:1],
                           sqb[:, 2048 + nh * 512:2048 + (nh + 1) * 512],
                           start=True, stop=True))
            for nh in range(2):
                p.emit("tensor",
                       lambda E, nh=nh: E.matmul(
                           ps_sA[0:1, nh * 512:(nh + 1) * 512],
                           ones_bf[0:64, 0:1],
                           sqb[0:64, 3072 + nh * 512:3072 + (nh + 1) * 512],
                           start=False, stop=True),
                       inc="pe" if nh == 1 else None)
            t_qn2 = ("pe", p.cnt["pe"])
            p.wait("tensor", t_sqk)
            for half, PSK in ((0, ps_av), (1, ps_sB)):
                for nh in range(2):
                    p.emit("tensor",
                           lambda E, half=half, nh=nh, PSK=PSK: E.matmul(
                               PSK[0:1, nh * 512:(nh + 1) * 512], ones_bf[:, 0:1],
                               sqb[:, half * TQ + nh * 512:
                                   half * TQ + (nh + 1) * 512],
                               start=True, stop=True),
                           inc="pe" if (half == 1 and nh == 1) else None)
            t_kn2 = ("pe", p.cnt["pe"])
            p.wait("vector", t_kn2)

            def _stats2(E):
                E.tensor_copy(R("qn2"), ps_sA[0:1, :])
                E.tensor_copy(R("kt2")[0:1, 0:TQ], ps_av[0:1, :])
                E.tensor_copy(R("kt2")[0:1, TQ:S], ps_sB[0:1, :])
                SPC(E)
                E.tensor_add(R("kt2"), R("kt2"), R("pe2"))
                SPC(E)
                SPC(E)
                E.tensor_reduce(km2[0:1, 0:1], R("kt2"), mybir.AxisListType.X,
                                ALU.max)
                SPC(E)
                return SPC(E)
            t_b1 = p.emit("vector", _stats2, inc="dv")
            p.wait("scalar", t_b1)
            p.emit("scalar", lambda E: E.activation(R("bias"), R("qn2"), AF.Ln,
                                                    bias=zerob[0:1, 0:1],
                                                    scale=km2[0:1, 0:1]))
            p.emit("scalar", lambda E: E.copy(fols[0:1, 0:256], fols[0:1, 0:256]))
            t_b3 = p.emit("scalar", lambda E: E.activation(R("bias"), R("bias"),
                                                           AF.Exp,
                                                           bias=zerob[0:1, 0:1],
                                                           scale=0.5), inc="ac")
            t_b4 = p.emit("vector", SPC, waits=[t_b3], inc="dv")
            t_bld = p.emit("gpsimd", lambda E: E.dma_start(q_aug[64:65, :],
                                                           R("bias")),
                           waits=[t_b4], inc="dwb", inc_by=16)
            if dbg and h == 0:
                p.wait("gpsimd", t_bld)
                p.emit("gpsimd", lambda E: E.dma_start(dbg_out["d_brow0"],
                                                       brow_scr),
                       inc="dso", inc_by=16)

            p.wait("tensor", t_kev)
            t_vt_ev_prev = [None, None]
            for tc in range(16):
                vpar = tc % 2
                po = vpar * 512
                p.wait("tensor", t_vt_ev_prev[vpar])
                for cc in range(4):
                    p.emit("tensor",
                           lambda E, tc=tc, cc=cc, po=po, HB=HB: E.matmul(
                               ps_proj[:, po:po + 128],
                               ckv_bf[:, cc * S + tc * 128:cc * S + (tc + 1) * 128],
                               HB[:, 2816 + cc * 128:2816 + (cc + 1) * 128],
                               start=(cc == 0), stop=(cc == 3)),
                           inc="pe" if cc == 3 else None)
                t_vt = ("pe", p.cnt["pe"])
                p.wait("vector", t_vt)
                t_vt_ev_prev[vpar] = p.emit(
                    "vector",
                    lambda E, tc=tc, po=po: E.tensor_copy(
                        vt_bf[:, tc * 128:(tc + 1) * 128], ps_proj[:, po:po + 128]),
                    inc="dv")

            t_exp = [None, None]
            t_savc = [t_av_prev, t_av_prev]
            p.wait("tensor", t_bld)
            for tc in range(16):
                spar = tc % 2
                PSS = ps_sA if spar == 0 else ps_sB
                p.wait("tensor", t_exp[spar])
                for nh in range(2):
                    p.emit("tensor",
                           lambda E, tc=tc, nh=nh, PSS=PSS: E.matmul(
                               PSS[:, nh * 512:(nh + 1) * 512],
                               kn_bf[:, tc * 128:(tc + 1) * 128],
                               qn_bf[:, nh * 512:(nh + 1) * 512],
                               start=True, stop=False))
                    p.emit("tensor",
                           lambda E, tc=tc, nh=nh, PSS=PSS: E.matmul(
                               PSS[:, nh * 512:(nh + 1) * 512],
                               kpe_aug[:, tc * 128:(tc + 1) * 128],
                               q_aug[:, nh * 512:(nh + 1) * 512],
                               start=False, stop=True),
                           inc="pe" if nh == 1 else None)
                t_sc = ("pe", p.cnt["pe"])
                p.wait("scalar", t_sc, t_savc[spar])
                t_exp[spar] = p.emit("scalar",
                                     lambda E, spar=spar, PSS=PSS:
                                     E.activation(et_buf[spar][:, :], PSS[:, :],
                                                  AF.Exp, bias=zerob[:, 0:1],
                                                  scale=SCALE), inc="ac")
                p.wait("tensor", t_exp[spar])
                for nh in range(2):
                    p.emit("tensor",
                           lambda E, nh=nh, spar=spar, tc=tc: E.matmul(
                               ps_proj[0:1, nh * 512:(nh + 1) * 512],
                               ones_bf[:, 0:1],
                               et_buf[spar][:, nh * 512:(nh + 1) * 512],
                               start=(tc == 0), stop=(tc == 15)))
                for nh in range(2):
                    p.emit("tensor",
                           lambda E, tc=tc, nh=nh, spar=spar: E.matmul(
                               ps_av[:, nh * 512:(nh + 1) * 512],
                               vt_bf[:, tc * 128:(tc + 1) * 128],
                               et_buf[spar][:, nh * 512:(nh + 1) * 512],
                               start=(tc == 0), stop=(tc == 15)),
                           inc="pe" if nh == 1 else None)
                t_savc[spar] = ("pe", p.cnt["pe"])

            t_sums = t_savc[1]
            p.wait("vector", t_sums)

            def _rsum(E):
                E.tensor_copy(R("sum"), ps_proj[0:1, :])
                SPC(E)
                return E.reciprocal(R("rsum"), R("sum"))
            t_rr = p.emit("vector", _rsum, inc="dv")
            t_rr_prev2 = t_rr
            p.wait("tensor", t_rr)
            for nh in range(2):
                p.emit("tensor",
                       lambda E, nh=nh: E.matmul(
                           ps_sA[:, nh * 512:(nh + 1) * 512], ones_f[0:1, :],
                           R("rsum")[0:1, nh * 512:(nh + 1) * 512],
                           start=True, stop=True),
                       inc="pe" if nh == 1 else None)
            t_rbc2 = ("pe", p.cnt["pe"])
            p.wait("vector", t_rbc2)
            t_rbv = p.emit("vector",
                           lambda E: E.tensor_copy(rbc_sb[:, :], ps_sA[:, :]),
                           inc="dv")
            t_cat_prev = p.emit("vector",
                                lambda E, h=h: E.tensor_mul(
                                    attn_cat[:, h * TQ:(h + 1) * TQ],
                                    ps_av[:, :], rbc_sb[:, :]),
                                inc="dv")
            t_hw_used[par] = t_savc[1]
            t_rr_prev = t_rr_prev2
            t_rbv_prev = t_rbv
            t_av_prev = t_savc[1]
            if dbg and h == 0:
                p.wait("gpsimd", t_cat_prev)
                for nm, src_ap in [("d_qn0", qn_bf[:, :]), ("d_kn0", kn_bf[:, :]),
                                   ("d_qaug0", q_aug[:, :]), ("d_vt0", vt_bf[:, :]),
                                   ("d_et0", et_buf[1][:, :]),
                                   ("d_rbc0", rbc_sb[:, :])]:
                    p.emit("gpsimd",
                           lambda E, nm=nm, src_ap=src_ap: E.dma_start(
                               dbg_out[nm], src_ap), inc="dso", inc_by=16)

        if dbg:
            p.wait("gpsimd", t_cat_prev)
            p.emit("gpsimd", lambda E: E.dma_start(dbg_out["d_cat"], attn_cat[:, :]),
                   inc="dso", inc_by=16)

        p.enabled = phases >= 3
        p.barrier()
        b_scope.close()
        bps_scope.close()
        ab2_scope.close()
        ab1_scope.close()

        cd_scope = es.enter_context(contextlib.ExitStack())
        CD = cd_scope.enter_context
        x1r_buf = [CD(nc.sbuf_tensor(f"x1r{i}", [128, TQ], F32)) for i in range(2)]
        o_buf = [CD(nc.sbuf_tensor(f"o_buf{i}", [128, 2048], BF16))
                 for i in range(2)]
        g_buf = [CD(nc.sbuf_tensor(f"g_buf{i}", [128, 2048], BF16))
                 for i in range(2)]
        u_buf = [CD(nc.sbuf_tensor(f"u_buf{i}", [128, 2048], BF16))
                 for i in range(2)]

        c_scope = es.enter_context(contextlib.ExitStack())
        CS = c_scope.enter_context
        xc_buf = [CS(nc.sbuf_tensor(f"xc{i}", [128, TQ], F32)) for i in range(2)]
        x1_buf = [CS(nc.sbuf_tensor(f"x1b{i}", [128, TQ], F32)) for i in range(2)]
        cb_bf = CS(nc.sbuf_tensor("cb_bf", [128, TQ], BF16))
        csq_bf = CS(nc.sbuf_tensor("csq_bf", [128, TQ], BF16))
        ctmp = CS(nc.sbuf_tensor("ctmp", [128, TQ], F32))

        cps_scope = es.enter_context(contextlib.ExitStack())
        CPS = cps_scope.enter_context
        ps_oA = CPS(nc.psum_tensor("ps_oA", [128, 1024], F32))
        ps_oB = CPS(nc.psum_tensor("ps_oB", [128, 1024], F32))
        ps_c1 = CPS(nc.psum_tensor("ps_c1", [1, 512], F32))
        ps_c2 = CPS(nc.psum_tensor("ps_c2", [1, 512], F32))
        ps_c3 = CPS(nc.psum_tensor("ps_c3", [1, 512], F32))
        ps_c4 = CPS(nc.psum_tensor("ps_c4", [1, 512], F32))

        t_o_load = [None, None]
        t_o_used = [None, None]
        t_xc_load = [None, None]
        t_x1_st = [None, None]
        t_oev = [None, None]
        t_cst_prev = None
        for mc in range(16):
            par = mc % 2
            PSO = ps_oA if par == 0 else ps_oB
            t_o_load[par] = p.emit("gpsimd",
                                   lambda E, mc=mc, par=par: E.dma_start(
                                       o_buf[par][:, :], o_pk[mc]),
                                   waits=[t_o_used[par]],
                                   inc=f"dwo{par}", inc_by=16)
            t_xc_load[par] = p.emit("sync",
                                    lambda E, mc=mc, par=par: E.dma_start(
                                        xc_buf[par][:, :], xT_r[:, mc, 0:TQ]),
                                    waits=[t_oev[par]], inc=f"dxc{par}", inc_by=16)
            p.wait("tensor", t_o_load[par], t_oev[par])
            for kc in range(16):
                for nh in range(2):
                    p.emit("tensor",
                           lambda E, kc=kc, nh=nh, par=par, PSO=PSO: E.matmul(
                               PSO[:, nh * 512:(nh + 1) * 512],
                               o_buf[par][:, kc * 128:(kc + 1) * 128],
                               attn_cat[:, kc * TQ + nh * 512:
                                        kc * TQ + (nh + 1) * 512],
                               start=(kc == 0), stop=(kc == 15)),
                           inc="pe" if (kc == 15 and nh == 1) else None)
            t_o_used[par] = ("pe", p.cnt["pe"])
            p.wait("vector", t_o_used[par], t_xc_load[par], t_x1_st[par])
            t_x1 = p.emit("vector",
                          lambda E, par=par, PSO=PSO: E.tensor_tensor(
                              x1_buf[par][:, :], PSO[:, :], xc_buf[par][:, :],
                              ALU.add), inc="dv")
            t_oev[par] = t_x1
            t_x1_st[par] = p.emit("sync",
                                  lambda E, mc=mc, par=par: E.dma_start(
                                      x1_scr[mc], x1_buf[par][:, :]),
                                  waits=[t_x1], inc=f"dx1w{par}", inc_by=16)
            p.wait("vector", t_cst_prev)
            t_cb = p.emit("vector",
                          lambda E, par=par: E.tensor_copy(cb_bf[:, :],
                                                           x1_buf[par][:, :]),
                          inc="dv")
            p.wait("scalar", t_x1, t_cst_prev)
            t_cs = p.emit("scalar",
                          lambda E, par=par: E.activation(csq_bf[:, :],
                                                          x1_buf[par][:, :],
                                                          AF.Square,
                                                          bias=zerob[:, 0:1],
                                                          scale=1.0), inc="ac")
            p.wait("tensor", t_cb, t_cs)
            for nh in range(2):
                p.emit("tensor",
                       lambda E, nh=nh, mc=mc: E.matmul(
                           (ps_c1 if nh == 0 else ps_c2)[0:1, 0:512],
                           ones_bf[:, 0:1], cb_bf[:, nh * 512:(nh + 1) * 512],
                           start=(mc == 0), stop=(mc == 15)))
            for nh in range(2):
                p.emit("tensor",
                       lambda E, nh=nh, mc=mc: E.matmul(
                           (ps_c3 if nh == 0 else ps_c4)[0:1, 0:512],
                           ones_bf[:, 0:1], csq_bf[:, nh * 512:(nh + 1) * 512],
                           start=(mc == 0), stop=(mc == 15)),
                       inc="pe" if nh == 1 else None)
            t_cst_prev = ("pe", p.cnt["pe"])
        t_cstat = ("pe", p.cnt["pe"])

        p.wait("vector", t_cstat)

        def _ln2rows(E):
            cr = R("crow")
            E.tensor_scalar(cr[0:1, 0:512], ps_c1[0:1, :], 1.0 / D, None, ALU.mult)
            E.tensor_scalar(cr[0:1, 512:1024], ps_c2[0:1, :], 1.0 / D, None,
                            ALU.mult)
            E.tensor_scalar(cr[0:1, 1024:1536], ps_c3[0:1, :], 1.0 / D, None,
                            ALU.mult)
            E.tensor_scalar(cr[0:1, 1536:2048], ps_c4[0:1, :], 1.0 / D, None,
                            ALU.mult)
            E.tensor_mul(ctmp[0:1, 0:1024], cr[0:1, 0:1024], cr[0:1, 0:1024])
            SPC(E)
            E.tensor_sub(cr[0:1, 1024:2048], cr[0:1, 1024:2048],
                         ctmp[0:1, 0:1024])
            SPC(E)
            return E.tensor_scalar(cr[0:1, 1024:2048], cr[0:1, 1024:2048], EPS_LN,
                            None, ALU.add)
        t_c1 = p.emit("vector", _ln2rows, inc="dv")
        p.wait("scalar", t_c1)
        p.emit("scalar", lambda E: E.activation(R("crow")[0:1, 1024:2048],
                                                R("crow")[0:1, 1024:2048], AF.Ln,
                                                bias=zerob[0:1, 0:1], scale=1.0))
        t_c2 = p.emit("scalar", lambda E: E.activation(R("crow")[0:1, 1024:2048],
                                                       R("crow")[0:1, 1024:2048],
                                                       AF.Exp,
                                                       bias=zerob[0:1, 0:1],
                                                       scale=-0.5), inc="ac")
        p.wait("vector", t_c2)

        def _c_b(E):
            cr = R("crow")
            E.tensor_mul(cr[0:1, 0:1024], cr[0:1, 0:1024], cr[0:1, 1024:2048])
            SPC(E)
            return E.tensor_scalar(cr[0:1, 0:1024], cr[0:1, 0:1024], -1.0, None,
                            ALU.mult)
        t_c3 = p.emit("vector", _c_b, inc="dv")
        p.wait("tensor", t_c3)
        for nh in range(2):
            p.emit("tensor",
                   lambda E, nh=nh: E.matmul(ps_oA[:, nh * 512:(nh + 1) * 512],
                                             ones_f[0:1, :],
                                             R("crow")[0:1, 1024 + nh * 512:
                                                       1024 + (nh + 1) * 512],
                                             start=True, stop=True))
        for nh in range(2):
            p.emit("tensor",
                   lambda E, nh=nh: E.matmul(ps_oB[:, nh * 512:(nh + 1) * 512],
                                             ones_f[0:1, :],
                                             R("crow")[0:1, nh * 512:
                                                       (nh + 1) * 512],
                                             start=True, stop=True),
                   inc="pe" if nh == 1 else None)
        t_cbc = ("pe", p.cnt["pe"])

        t_x1r = [None, None]
        t_h2v = [None, None]
        t_h2chunk = []
        for mc in range(16):
            par = mc % 2
            t_x1r[par] = p.emit("sync",
                                lambda E, mc=mc, par=par: E.dma_start(
                                    x1r_buf[par][:, :], x1_scr[mc]),
                                waits=[("dx1w0", p.cnt["dx1w0"]),
                                       ("dx1w1", p.cnt["dx1w1"]), t_h2v[par]],
                                inc=f"dx1r{par}", inc_by=16)
            p.wait("vector", t_x1r[par], t_cbc if mc == 0 else None)
            t_h2v[par] = p.emit(
                "vector",
                lambda E, mc=mc, par=par: (
                    E.tensor_mul(ctmp[:, :], x1r_buf[par][:, :], ps_oA[:, :]),
                    SPC(E),
                    E.tensor_tensor(h2_bf[:, mc * TQ:(mc + 1) * TQ], ctmp[:, :],
                                    ps_oB[:, :], ALU.add))[-1],
                inc="dv")
            t_h2chunk.append(t_h2v[par])

        if dbg:
            p.wait("gpsimd", t_h2v[1])
            p.emit("gpsimd", lambda E: E.dma_start(dbg_out["d_h2"], h2_bf[:, :]),
                   inc="dso", inc_by=16)

        p.enabled = phases >= 4
        if p.enabled:
            t_h2_all = ("dv", p.cnt["dv"])
            p.wait("gpsimd", t_h2_all)
            p.wait("sync", t_h2_all)
        cat_scope.close()
        c_scope.close()
        cps_scope.close()

        NOCT = 8
        OCT = 64 // NOCT
        d_scope = es.enter_context(contextlib.ExitStack())
        DS = d_scope.enter_context
        gated = DS(nc.sbuf_tensor("gated", [128, OCT * TQ], BF16))
        silu_bf = [DS(nc.sbuf_tensor(f"silu{i}", [128, TQ], BF16)) for i in range(2)]
        d_buf = [DS(nc.sbuf_tensor(f"d_buf{i}", [128, OCT * 128], BF16))
                 for i in range(2)]
        dtmp = [DS(nc.sbuf_tensor(f"dtmp{i}", [128, TQ], F32)) for i in range(2)]
        ffn_acc = DS(nc.sbuf_tensor("ffn_acc", [128, 16 * TQ], F32, side="right"))

        dps_scope = es.enter_context(contextlib.ExitStack())
        DPS = dps_scope.enter_context
        ps_d = [DPS(nc.psum_tensor(f"ps_d{i}", [128, 1024], F32))
                for i in range(2)]
        ps_g = DPS(nc.psum_tensor("ps_g", [128, 1024], F32))
        ps_u = DPS(nc.psum_tensor("ps_u", [128, 1024], F32))

        t_g_load = [None, None]
        t_u_load = [None, None]
        t_d_load = [None, None]
        t_d_used = [None, None]
        t_sil_prev = None
        t_mul_prev = None
        t_gated_wr = [None, None]
        t_gmm_prev = [None, None]
        t_umm_prev = [None, None]
        t_dp_done = None
        t_dev = [None, None]
        t_yst = [None, None]
        t_x1r2 = [None, None]

        for oo in range(NOCT):
            for hl in range(OCT):
                hc = oo * OCT + hl
                par = hc % 2
                t_g_load[par] = p.emit("gpsimd",
                                       lambda E, hc=hc, par=par: E.dma_start(
                                           g_buf[par][:, :], g_pk[hc]),
                                       waits=[t_gmm_prev[par]],
                                       inc=f"dwg{par}", inc_by=16)
                t_u_load[par] = p.emit("gpsimd",
                                       lambda E, hc=hc, par=par: E.dma_start(
                                           u_buf[par][:, :], u_pk[hc]),
                                       waits=[t_umm_prev[par]],
                                       inc=f"dwu{par}", inc_by=16)
                p.wait("tensor", t_g_load[par], t_sil_prev)
                for kc in range(16):
                    if hc == 0:
                        p.wait("tensor", t_h2chunk[kc])
                    for nh in range(2):
                        p.emit("tensor",
                               lambda E, kc=kc, nh=nh, par=par: E.matmul(
                                   ps_g[:, nh * 512:(nh + 1) * 512],
                                   g_buf[par][:, kc * 128:(kc + 1) * 128],
                                   h2_bf[:, kc * TQ + nh * 512:
                                         kc * TQ + (nh + 1) * 512],
                                   start=(kc == 0), stop=(kc == 15)),
                               inc="pe" if (kc == 15 and nh == 1) else None)
                t_gmm = ("pe", p.cnt["pe"])
                t_gmm_prev[par] = t_gmm
                p.wait("tensor", t_u_load[par], t_mul_prev)
                for kc in range(16):
                    for nh in range(2):
                        p.emit("tensor",
                               lambda E, kc=kc, nh=nh, par=par: E.matmul(
                                   ps_u[:, nh * 512:(nh + 1) * 512],
                                   u_buf[par][:, kc * 128:(kc + 1) * 128],
                                   h2_bf[:, kc * TQ + nh * 512:
                                         kc * TQ + (nh + 1) * 512],
                                   start=(kc == 0), stop=(kc == 15)),
                               inc="pe" if (kc == 15 and nh == 1) else None)
                t_umm = ("pe", p.cnt["pe"])
                t_umm_prev[par] = t_umm
                p.wait("scalar", t_gmm, t_gated_wr[par])
                t_sil = p.emit("scalar",
                               lambda E, par=par: E.activation(
                                   silu_bf[par][:, :], ps_g[:, :], AF.Silu,
                                   bias=zerob[:, 0:1], scale=1.0), inc="ac")
                t_sil_prev = t_sil
                p.wait("vector", t_sil, t_umm,
                       t_dp_done if (oo > 0 and hl < 2) else None)
                t_gated_wr[par] = p.emit(
                    "vector",
                    lambda E, hl=hl, par=par: E.tensor_mul(
                        gated[:, hl * TQ:(hl + 1) * TQ], silu_bf[par][:, :],
                        ps_u[:, :]), inc="dv")
                t_mul_prev = t_gated_wr[par]

            t_gated_all = ("dv", p.cnt["dv"])

            p.wait("tensor", t_gated_all)
            for mc in range(16):
                par = mc % 2
                t_d_load[par] = p.emit("gpsimd",
                                       lambda E, oo=oo, mc=mc, par=par:
                                       E.dma_start(
                                           d_buf[par][:, :],
                                           d_pk[mc][:, oo * OCT * 128:
                                                    (oo + 1) * OCT * 128]),
                                       waits=[t_d_used[par]],
                                       inc=f"dwd{par}", inc_by=16)
                if oo == NOCT - 1:
                    t_x1r2[par] = p.emit("sync",
                                         lambda E, mc=mc, par=par: E.dma_start(
                                             x1r_buf[par][:, :], x1_scr[mc]),
                                         waits=[t_yst[par]],
                                         inc=f"dx1r{par}", inc_by=16)
                p.wait("tensor", t_d_load[par], t_dev[par])
                for kc in range(OCT):
                    for nh in range(2):
                        p.emit("tensor",
                               lambda E, kc=kc, nh=nh, par=par: E.matmul(
                                   ps_d[par][:, nh * 512:(nh + 1) * 512],
                                   d_buf[par][:, kc * 128:(kc + 1) * 128],
                                   gated[:, kc * TQ + nh * 512:
                                         kc * TQ + (nh + 1) * 512],
                                   start=(kc == 0), stop=(kc == OCT - 1)),
                               inc="pe" if (kc == OCT - 1 and nh == 1) else None)
                t_dmm = ("pe", p.cnt["pe"])
                t_d_used[par] = t_dmm
                p.wait("vector", t_dmm)
                if oo == 0:
                    t_dev[par] = p.emit("vector",
                                        lambda E, mc=mc, par=par: E.tensor_copy(
                                            ffn_acc[:, mc * TQ:(mc + 1) * TQ],
                                            ps_d[par][:, :]), inc="dv")
                elif oo < NOCT - 1:
                    t_dev[par] = p.emit("vector",
                                        lambda E, mc=mc, par=par:
                                        E.tensor_tensor(
                                            ffn_acc[:, mc * TQ:(mc + 1) * TQ],
                                            ffn_acc[:, mc * TQ:(mc + 1) * TQ],
                                            ps_d[par][:, :], ALU.add), inc="dv")
                else:
                    p.wait("vector", t_x1r2[par])
                    t_dev[par] = p.emit(
                        "vector",
                        lambda E, mc=mc, par=par: (
                            E.tensor_tensor(dtmp[par][:, :], ps_d[par][:, :],
                                            ffn_acc[:, mc * TQ:(mc + 1) * TQ],
                                            ALU.add),
                            SPC(E),
                            E.tensor_tensor(dtmp[par][:, :], dtmp[par][:, :],
                                            x1r_buf[par][:, :], ALU.add))[-1],
                        inc="dv")
                    t_yst[par] = p.emit("sync",
                                        lambda E, mc=mc, par=par: E.dma_start(
                                            y_out[mc * 128:(mc + 1) * 128, :],
                                            dtmp[par][:, :]),
                                        waits=[t_dev[par]], inc="dso", inc_by=16)
            t_dp_done = ("pe", p.cnt["pe"])

        p.wait("sync", ("dso", p.cnt["dso"]))

        with nc.Block() as block:
            @block.tensor
            def _(E):
                for fn in p.q["tensor"]:
                    fn(E)

            @block.vector
            def _(E):
                for fn in p.q["vector"]:
                    fn(E)

            @block.scalar
            def _(E):
                for fn in p.q["scalar"]:
                    fn(E)

            @block.gpsimd
            def _(E):
                for fn in p.q["gpsimd"]:
                    fn(E)

            @block.sync
            def _(E):
                for fn in p.q["sync"]:
                    fn(E)

    return nc



_NC_CACHE = {}


def _get_nc(dbg=False):
    if dbg not in _NC_CACHE:
        _NC_CACHE[dbg] = build_nc(dbg)
    return _NC_CACHE[dbg]


def kernel(**inputs):
    x = np.asarray(inputs["L0_attn_x"], np.float32)
    w = _stage_weights(inputs)
    in_maps = []
    for c in range(8):
        b, half = c // 2, c % 2
        own = x[b, half * TQ:(half + 1) * TQ, :]
        other = x[b, (1 - half) * TQ:(2 - half) * TQ, :]
        xT = np.ascontiguousarray(np.concatenate([own, other], axis=0).T)
        m = {"xT": xT}
        m.update(w)
        in_maps.append(m)
    nc = _get_nc(False)
    res = run_bass_kernel_spmd(nc, in_maps, core_ids=list(range(8)))
    out = np.empty((NB, S, D), np.float32)
    for c in range(8):
        b, half = c // 2, c % 2
        out[b, half * TQ:(half + 1) * TQ, :] = res.results[c]["y"].T
    return out


# version 09
# speedup vs baseline: 1.0387x; 1.0387x over previous
import contextlib
import numpy as np
import concourse.bass as bass
import concourse.mybir as mybir
from concourse.bass_utils import run_bass_kernel_spmd

F32 = mybir.dt.float32
BF16 = mybir.dt.bfloat16
I32 = mybir.dt.int32
AF = mybir.ActivationFunctionType
ALU = mybir.AluOpType
ET = mybir.EngineType

D = 2048
S = 2048
NB = 4
H = 16
QL = 1536
KVL = 512
DN = 128
DR = 64
DV = 128
FH = 8192
TQ = 1024
SCALE = float((DN + DR) ** -0.5)
MAGIC = float(1.5 * 2 ** 23)
INV2PI = float(1.0 / (2.0 * np.pi))
TWOPI = float(2.0 * np.pi)
ROPE_ES = float(-np.log(10000.0) / 32.0)
EPS_LN = 1e-5
EPS_RMS = 1e-6
NT = 256
NTILES = S // NT

ENGS = ("tensor", "vector", "scalar", "gpsimd", "sync")



def _pack_lhsT(w, mb=128):
    Fd, Dk = w.shape
    a = w.reshape(Fd // mb, mb, Dk // 128, 128).transpose(0, 3, 2, 1)
    return np.ascontiguousarray(a.reshape(Fd // mb, 128, -1), dtype=np.float32)


_DEINT = np.concatenate([np.arange(0, 64, 2), np.arange(1, 64, 2)])


def _stage_weights(inp):
    qa = np.asarray(inp["L0_attn_q_a_w"], np.float32)
    qb = np.asarray(inp["L0_attn_q_b_w"], np.float32)
    kva = np.asarray(inp["L0_attn_kv_a_w"], np.float32)
    kvb = np.asarray(inp["L0_attn_kv_b_w"], np.float32)
    d = {
        "qa_pack": _pack_lhsT(qa, 128),
        "qaln_pack": np.ascontiguousarray(
            np.asarray(inp["L0_attn_q_a_ln_w"], np.float32).reshape(12, 128)),
        "kva_ckv": _pack_lhsT(kva[0:KVL, :], 128),
        "kva_pe": _pack_lhsT(kva[KVL + _DEINT, :], 64)[0],
        "kvaln_pack": np.ascontiguousarray(
            np.asarray(inp["L0_attn_kv_a_ln_w"], np.float32).reshape(4, 128)),
        "o_pack": _pack_lhsT(np.asarray(inp["L0_attn_o_w"], np.float32), 128),
        "g_pack": _pack_lhsT(np.asarray(inp["L0_ffn_w_gate"], np.float32), 128),
        "u_pack": _pack_lhsT(np.asarray(inp["L0_ffn_w_up"], np.float32), 128),
        "d_pack": _pack_lhsT(np.asarray(inp["L0_ffn_w_down"], np.float32), 128),
    }
    hw = np.empty((H, 128, 3328), np.float32)
    for h in range(H):
        r0 = h * (DN + DR)
        hw[h, :, 0:1536] = _pack_lhsT(qb[r0:r0 + DN, :], 128)[0]
        hw[h, :, 1536:2304] = _pack_lhsT(qb[r0 + DN + _DEINT, :], 64)[0]
        r1 = h * (DN + DV)
        hw[h, :, 2304:2816] = _pack_lhsT(kvb[r1:r1 + DN, :], 128)[0]
        hw[h, :, 2816:3328] = _pack_lhsT(kvb[r1 + DN:r1 + DN + DV, :], 128)[0]
    d["headw"] = hw
    return d



class Prog:
    def __init__(self, nc, es):
        self.nc = nc
        self.es = es
        self.q = {e: [] for e in ENGS}
        self.sems = {}
        self.cnt = {}
        self.enabled = True
        self.fol = {}

    def sem(self, name):
        self.sems[name] = self.es.enter_context(self.nc.semaphore(name))
        self.cnt[name] = 0

    def emit(self, eng, build_fn, waits=(), inc=None, inc_by=1, fol=True):
        if not self.enabled:
            return None
        tick = None
        if inc is not None:
            self.cnt[inc] += inc_by
            tick = (inc, self.cnt[inc])
        waits = tuple(w for w in waits if w is not None)
        sems = self.sems
        fol = self.fol.get(eng) if (inc is not None and fol) else None

        def fn(E, _w=waits, _i=inc, _b=build_fn, _n=inc_by, _f=fol):
            for (s, v) in _w:
                E.wait_ge(sems[s], v)
            ins = _b(E)
            if _f is not None:
                ins = _f(E)
            if _i is not None:
                ins.then_inc(sems[_i], _n)

        self.q[eng].append(fn)
        return tick

    def wait(self, eng, *ticks):
        if not self.enabled:
            return
        ticks = tuple(t for t in ticks if t is not None)
        if not ticks:
            return
        sems = self.sems

        def fn(E, _t=ticks):
            for (s, v) in _t:
                E.wait_ge(sems[s], v)

        self.q[eng].append(fn)

    def barrier(self):
        if not self.enabled:
            return None
        for e in ("tensor", "vector", "scalar"):
            self.emit(e, lambda E: E.nop(), inc="bar")
        t = ("bar", self.cnt["bar"])
        for e in ENGS:
            self.wait(e, t)
        return t



def build_nc(dbg=False, phases=4):
    nc = bass.Bass("TRN2", target_bir_lowering=False)

    xT = nc.dram_tensor("xT", [D, S], F32, kind="ExternalInput").ap()
    qa_pk = nc.dram_tensor("qa_pack", [12, 128, 2048], F32, kind="ExternalInput").ap()
    qaln_pk = nc.dram_tensor("qaln_pack", [12, 128], F32, kind="ExternalInput").ap()
    kva_ckv_pk = nc.dram_tensor("kva_ckv", [4, 128, 2048], F32, kind="ExternalInput").ap()
    kva_pe_pk = nc.dram_tensor("kva_pe", [128, 1024], F32, kind="ExternalInput").ap()
    kvaln_pk = nc.dram_tensor("kvaln_pack", [4, 128], F32, kind="ExternalInput").ap()
    hw_pk = nc.dram_tensor("headw", [H, 128, 3328], F32, kind="ExternalInput").ap()
    o_pk = nc.dram_tensor("o_pack", [16, 128, 2048], F32, kind="ExternalInput").ap()
    g_pk = nc.dram_tensor("g_pack", [64, 128, 2048], F32, kind="ExternalInput").ap()
    u_pk = nc.dram_tensor("u_pack", [64, 128, 2048], F32, kind="ExternalInput").ap()
    d_pk = nc.dram_tensor("d_pack", [16, 128, 8192], F32, kind="ExternalInput").ap()
    y_out = nc.dram_tensor("y", [D, TQ], F32, kind="ExternalOutput").ap()
    x1_scr = nc.dram_tensor("x1_scr", [16, 128, TQ], F32, kind="Internal").ap()
    brow_scr = nc.dram_tensor("brow_scr", [1, TQ], F32, kind="Internal").ap()

    dbg_out = {}
    if dbg:
        for nm, shp, dt in [("d_qmid", [128, 12 * TQ], F32),
                            ("d_ckv", [128, 4 * S], F32),
                            ("d_kpe", [65, S], F32),
                            ("d_pe2", [1, S], F32),
                            ("d_cat", [128, H * TQ], F32),
                            ("d_h2", [128, 16 * TQ], F32),
                            ("d_h1", [128, 16 * TQ], F32),
                            ("d_cos", [64, TQ], F32),
                            ("d_sin", [64, TQ], F32),
                            ("d_rt", [64, 64], F32),
                            ("d_lnw", [128, 16], F32),
                            ("d_qraw", [128, TQ], F32),
                            ("d_posk", [1, S], F32),
                            ("d_f2pi", [64, 1], F32),
                            ("d_qabuf", [128, 2048], F32),
                            ("d_qn0", [128, TQ], F32),
                            ("d_kn0", [128, S], F32),
                            ("d_qaug0", [65, TQ], F32),
                            ("d_vt0", [128, 2048], F32),
                            ("d_et0", [128, TQ], F32),
                            ("d_rbc0", [128, TQ], F32),
                            ("d_rows0", [1, 6144], F32),
                            ("d_brow0", [1, TQ], F32)]:
            dbg_out[nm] = nc.dram_tensor(nm, shp, dt, kind="ExternalOutput").ap()

    es_root = contextlib.ExitStack()
    with es_root as es:
        p = Prog(nc, es)
        for s in ("pe", "dv", "ac", "gp", "bar",
                  "dwk", "dwq0", "dwq1", "dwh0", "dwh1", "dwo0", "dwo1",
                  "dwg0", "dwg1", "dwu0", "dwu1", "dwd0", "dwd1", "dwb",
                  "dxx0", "dxx1", "dxc0", "dxc1", "dx1w0", "dx1w1",
                  "dx1r0", "dx1r1", "dlnw", "dxb", "dso"):
            p.sem(s)

        A = es.enter_context

        ones_bf = A(nc.sbuf_tensor("ones_bf", [128, 1], BF16))
        ones_f = A(nc.sbuf_tensor("ones_f", [1, 128], F32))
        zerob = A(nc.sbuf_tensor("zerob", [128, 1], F32))
        ident_f = A(nc.sbuf_tensor("ident_f", [128, 128], F32))
        iota_bc = A(nc.sbuf_tensor("iota_bc", [128, 128], F32))
        pidx128 = A(nc.sbuf_tensor("pidx128", [128, 1], F32))
        rt_bf = A(nc.sbuf_tensor("rt_bf", [64, 64], BF16))
        f2pi = A(nc.sbuf_tensor("f2pi", [64, 1], F32))
        pid_i = A(nc.sbuf_tensor("pid_i", [1, 1], I32))
        pid_f = A(nc.sbuf_tensor("pid_f", [1, 1], F32))
        halfv = A(nc.sbuf_tensor("halfv", [1, 1], F32))
        obase = A(nc.sbuf_tensor("obase", [1, 1], F32))
        scr64a = A(nc.sbuf_tensor("scr64a", [64, 64], F32))
        scr64b = A(nc.sbuf_tensor("scr64b", [64, 64], F32))
        scr64c = A(nc.sbuf_tensor("scr64c", [64, 64], F32))
        scr64d = A(nc.sbuf_tensor("scr64d", [64, 64], F32))
        qaln_sb = A(nc.sbuf_tensor("qaln_sb", [128, 12], F32))
        kvaln_sb = A(nc.sbuf_tensor("kvaln_sb", [128, 4], F32))
        lnw_raw = A(nc.sbuf_tensor("lnw_raw", [12, 128], F32))
        rows = A(nc.sbuf_tensor("rows", [1, 6144], F32))
        km2 = A(nc.sbuf_tensor("km2", [1, 1], F32))
        folv = A(nc.sbuf_tensor("folv", [1, 256], F32))
        fols = A(nc.sbuf_tensor("fols", [1, 256], F32))
        p.fol["vector"] = lambda E: E.tensor_copy(folv[0:1, 0:8], folv[0:1, 0:8])
        p.fol["scalar"] = lambda E: E.copy(fols[0:1, 0:8], fols[0:1, 0:8])

        def SPC(E):
            return E.tensor_copy(folv[0:1, 0:256], folv[0:1, 0:256])

        def R(name):
            a, b = {"pe2": (0, 2048), "kt2": (2048, 4096),
                    "qn2": (4096, 5120), "bias": (5120, 6144),
                    "sum": (4096, 5120), "rsum": (5120, 6144),
                    "crow": (0, 2048)}[name]
            return rows[0:1, a:b]

        ab1_scope = es.enter_context(contextlib.ExitStack())
        AB1 = ab1_scope.enter_context
        cos_own = AB1(nc.sbuf_tensor("cos_own", [64, TQ], F32))
        sin_own = AB1(nc.sbuf_tensor("sin_own", [64, TQ], F32))
        ckv_bf = AB1(nc.sbuf_tensor("ckv_bf", [128, 4 * S], BF16))
        kpe_aug = AB1(nc.sbuf_tensor("kpe_aug", [65, S], BF16))

        h1r_scope = es.enter_context(contextlib.ExitStack())
        h1_own = h1r_scope.enter_context(
            nc.sbuf_tensor("h1_own", [128, 16 * TQ], BF16, side="right"))

        akv_scope = es.enter_context(contextlib.ExitStack())
        AKV = akv_scope.enter_context
        cos_k = AKV(nc.sbuf_tensor("cos_k", [64, S], F32))
        sin_k = AKV(nc.sbuf_tensor("sin_k", [64, S], F32))
        kva_ckv_bf = AKV(nc.sbuf_tensor("kva_ckv_bf", [128, 4 * 2048], BF16))
        kva_pe_bf = AKV(nc.sbuf_tensor("kva_pe_bf", [128, 1024], BF16))
        kpe_bf_t = AKV(nc.sbuf_tensor("kpe_bf_t", [64, NT], BF16))
        xt_buf = [AKV(nc.sbuf_tensor(f"xt{i}", [128, 16 * NT], F32))
                  for i in range(2)]
        sq_buf = AKV(nc.sbuf_tensor("sq_buf", [128, 16 * NT], BF16))
        h1_tmp = AKV(nc.sbuf_tensor("h1_tmp", [128, 16 * NT], BF16))
        tmp_f = AKV(nc.sbuf_tensor("tmp_f", [128, 1024], F32))
        abrow = AKV(nc.sbuf_tensor("abrow", [1, 1024], F32))

        aps_scope = es.enter_context(contextlib.ExitStack())
        APS = aps_scope.enter_context
        ps_main = APS(nc.psum_tensor("ps_main", [128, 1024], F32))
        ps_bc = APS(nc.psum_tensor("ps_bc", [128, 1024], F32))
        ps_pe = APS(nc.psum_tensor("ps_pe", [64, 512], F32))
        ps_r1 = APS(nc.psum_tensor("ps_r1", [1, 512], F32))
        ps_r2 = APS(nc.psum_tensor("ps_r2", [1, 512], F32))

        pre_scope = contextlib.ExitStack()
        PRE = pre_scope.enter_context
        iota_row = PRE(nc.sbuf_tensor("iota_row", [1, 2048], F32))
        posk_row = PRE(nc.sbuf_tensor("posk_row", [1, 2048], F32))
        rope_t1 = PRE(nc.sbuf_tensor("rope_t1", [64, 1024], F32))
        rope_t2 = PRE(nc.sbuf_tensor("rope_t2", [64, 1024], F32))

        p.emit("gpsimd", lambda E: E.iota(iota_bc[:, :], pattern=[[1, 128]], base=0,
                                          channel_multiplier=0,
                                          allow_small_or_imprecise_dtypes=True))
        p.emit("gpsimd", lambda E: E.iota(pidx128[:, :], pattern=[[0, 1]], base=0,
                                          channel_multiplier=1,
                                          allow_small_or_imprecise_dtypes=True))
        p.emit("gpsimd",
               lambda E: E.iota(iota_row[0:1, :], pattern=[[1, 2048]], base=0,
                                channel_multiplier=0,
                                allow_small_or_imprecise_dtypes=True))
        t_gp = p.emit("gpsimd",
                      lambda E: E.iota(folv[0:1, 0:8], pattern=[[1, 8]], base=0,
                                       channel_multiplier=0,
                                       allow_small_or_imprecise_dtypes=True),
                      inc="gp")

        def _pre_a(E):
            E.store(pid_i[0:1, 0:1], nc.partition_id(engines=[ET.DVE]))
            E.memset(ones_bf[:, :], 1.0)
            E.memset(ones_f[:, :], 1.0)
            E.tensor_copy(pid_f[:, :], pid_i[:, :])
            E.memset(zerob[:, :], 0.0)
            E.memset(kpe_aug[64:65, :], -1.0)
            E.memset(scr64b[0:1, 0:8], 0.0)
        p.emit("vector", _pre_a)
        p.wait("vector", t_gp)

        def PP(vfn, prev=[None]):
            t = p.emit("vector", vfn, waits=[prev[0]], inc="dv")
            prev[0] = p.emit("scalar",
                             lambda E: E.copy(fols[0:1, 0:8], fols[0:1, 0:8]),
                             waits=[t], inc="ac")
            return prev[0]

        def _v1(E):
            E.tensor_scalar(ident_f[:, :], iota_bc[:, :], pidx128[:, 0:1], None,
                            ALU.is_equal)
            E.tensor_scalar(scr64a[:, :], iota_bc[0:64, 0:64],
                            pidx128[0:64, 0:1], None, ALU.subtract)
            E.tensor_scalar(scr64b[0:1, 1:2], pid_f[:, :], 1.0, None, ALU.is_equal)
            E.tensor_scalar(scr64b[0:1, 3:4], pid_f[:, :], 3.0, None, ALU.is_equal)
            E.tensor_scalar(scr64b[0:1, 5:6], pid_f[:, :], 5.0, None, ALU.is_equal)
            E.tensor_scalar(scr64b[0:1, 7:8], pid_f[:, :], 7.0, None, ALU.is_equal)
            return E.tensor_scalar(scr64d[0:64, 0:1], pidx128[0:64, 0:1], 32.0,
                                   None, ALU.is_ge)
        PP(_v1)

        def _v2(E):
            E.tensor_scalar(scr64c[:, :], scr64a[:, :], 32.0, None, ALU.is_equal)
            E.tensor_add(scr64b[0:1, 0:1], scr64b[0:1, 1:2], scr64b[0:1, 3:4])
            return E.tensor_scalar(scr64d[0:64, 1:2], scr64d[0:64, 0:1], 32.0,
                                   None, ALU.mult)
        PP(_v2)

        def _v3(E):
            E.tensor_scalar(scr64a[:, :], scr64a[:, :], -32.0, None, ALU.is_equal)
            E.tensor_add(scr64b[0:1, 2:3], scr64b[0:1, 5:6], scr64b[0:1, 7:8])
            return E.tensor_sub(f2pi[:, :], pidx128[0:64, 0:1],
                                scr64d[0:64, 1:2])
        PP(_v3)

        def _v4(E):
            E.tensor_sub(scr64d[:, :], scr64c[:, :], scr64a[:, :])
            return E.tensor_add(halfv[:, :], scr64b[0:1, 0:1], scr64b[0:1, 2:3])
        PP(_v4)

        def _v5(E):
            E.tensor_copy(rt_bf[:, :], scr64d[:, :])
            return E.tensor_scalar(halfv[:, :], halfv[:, :], 1024.0, None,
                                   ALU.mult)
        PP(_v5)

        def _v6(E):
            E.tensor_scalar(obase[:, :], halfv[:, :], -1.0, None, ALU.mult)
            return E.tensor_scalar(posk_row[0:1, 0:1024], iota_row[0:1, 0:1024],
                                   halfv[0:1, 0:1], None, ALU.add)
        PP(_v6)

        def _v7(E):
            return E.tensor_scalar(obase[:, :], obase[:, :], 1024.0, None,
                                   ALU.add)
        PP(_v7)

        def _v8(E):
            return E.tensor_scalar(posk_row[0:1, 1024:2048],
                                   iota_row[0:1, 0:1024], obase[0:1, 0:1], None,
                                   ALU.add)
        t_prev2 = PP(_v8)

        p.emit("scalar", lambda E: E.activation(f2pi[:, :], f2pi[:, :], AF.Exp,
                                                bias=zerob[0:64, 0:1],
                                                scale=ROPE_ES))
        p.emit("scalar", lambda E: E.copy(fols[0:1, 0:256], fols[0:1, 0:256]))
        t_f2pi = p.emit("scalar", lambda E: E.mul(f2pi[:, :], f2pi[:, :], INV2PI),
                        inc="ac")

        t_d = p.emit("sync", lambda E: E.dma_start(lnw_raw[0:12, :], qaln_pk),
                     inc="dlnw", inc_by=16)
        p.wait("tensor", t_d, t_prev2)
        t_pe = p.emit("tensor", lambda E: E.transpose(ps_main[0:128, 0:12],
                                                      lnw_raw[0:12, :],
                                                      ident_f[0:12, 0:12]),
                      inc="pe")
        p.wait("vector", t_pe)
        t_dv = p.emit("vector", lambda E: E.tensor_copy(qaln_sb[:, :],
                                                        ps_main[0:128, 0:12]),
                      inc="dv")
        t_d = p.emit("sync", lambda E: E.dma_start(lnw_raw[0:4, :], kvaln_pk),
                     waits=[t_dv], inc="dlnw", inc_by=16)
        p.wait("tensor", t_d)
        t_pe = p.emit("tensor", lambda E: E.transpose(ps_main[0:128, 16:20],
                                                      lnw_raw[0:4, :],
                                                      ident_f[0:4, 0:4]),
                      inc="pe")
        p.wait("vector", t_pe)
        t_kvln = p.emit("vector", lambda E: E.tensor_copy(kvaln_sb[:, :],
                                                          ps_main[0:128, 16:20]),
                        inc="dv")

        p.wait("tensor", t_prev2)
        t_tabv = None
        for tb in range(2):
            p.wait("tensor", t_tabv)
            for j in range(2):
                p.emit("tensor",
                       lambda E, tb=tb, j=j: E.matmul(
                           ps_bc[0:64, j * 512:(j + 1) * 512], ones_f[0:1, 0:64],
                           posk_row[0:1, tb * 1024 + j * 512:
                                    tb * 1024 + (j + 1) * 512],
                           start=True, stop=True),
                       inc="pe" if j == 1 else None)
            t_pb = ("pe", p.cnt["pe"])
            p.wait("vector", t_pb, t_f2pi)

            def _tab_v(E, tb=tb):
                E.tensor_scalar(rope_t1[:, :], ps_bc[0:64, :], f2pi[:, 0:1], None,
                                ALU.mult)
                SPC(E)
                E.tensor_scalar(rope_t2[:, :], rope_t1[:, :], MAGIC, -MAGIC,
                                ALU.add, ALU.add)
                SPC(E)
                E.tensor_sub(rope_t2[:, :], rope_t1[:, :], rope_t2[:, :])
                SPC(E)
                E.tensor_scalar(sin_k[:, tb * 1024:(tb + 1) * 1024],
                                rope_t2[:, :], TWOPI, None, ALU.mult)
                E.tensor_scalar(rope_t1[:, :], rope_t1[:, :], 0.25, None, ALU.add)
                SPC(E)
                E.tensor_scalar(rope_t2[:, :], rope_t1[:, :], MAGIC, -MAGIC,
                                ALU.add, ALU.add)
                SPC(E)
                E.tensor_sub(rope_t2[:, :], rope_t1[:, :], rope_t2[:, :])
                SPC(E)
                return E.tensor_scalar(cos_k[:, tb * 1024:(tb + 1) * 1024],
                                rope_t2[:, :], TWOPI, None, ALU.mult)
            t_tabv = p.emit("vector", _tab_v, inc="dv")
        p.wait("scalar", t_tabv)
        p.emit("scalar", lambda E: E.activation(sin_k[:, :], sin_k[:, :], AF.Sin,
                                                bias=zerob[0:64, 0:1], scale=1.0))
        t_tabs = p.emit("scalar", lambda E: E.activation(cos_k[:, :], cos_k[:, :],
                                                         AF.Sin,
                                                         bias=zerob[0:64, 0:1],
                                                         scale=1.0), inc="ac")
        p.wait("vector", t_tabs)

        def _tab_cp(E):
            E.tensor_copy(cos_own[:, :], cos_k[:, 0:TQ])
            return E.tensor_copy(sin_own[:, :], sin_k[:, 0:TQ])
        t_tabcp = p.emit("vector", _tab_cp, inc="dv")
        pre_scope.close()

        for mb in range(4):
            p.emit("gpsimd",
                   lambda E, mb=mb: E.dma_start(
                       kva_ckv_bf[:, mb * 2048:(mb + 1) * 2048], kva_ckv_pk[mb]),
                   inc="dwk", inc_by=16)
        p.emit("gpsimd", lambda E: E.dma_start(kva_pe_bf[:, :], kva_pe_pk),
               inc="dwk", inc_by=16)
        t_kva = ("dwk", p.cnt["dwk"])

        xT_r = xT.rearrange("(kc p) s -> p kc s", p=128)
        t_xload = [None, None]
        t_xfree = [None, None]
        t_sqfree = [None, None]

        def h1sl(tt, kc):
            if tt < 4:
                return h1_own[:, kc * TQ + tt * NT: kc * TQ + tt * NT + NT]
            return h1_tmp[:, kc * NT:(kc + 1) * NT]

        t_tile_vlast = None
        t_kv_prev = None
        t_pm_prev = None
        t_ck_prev = None
        t_stat = None
        for tt in range(NTILES):
            par = tt % 2
            sem = f"dxx{par}"
            t_xload[par] = p.emit(
                "sync",
                lambda E, par=par, tt=tt: E.dma_start(
                    xt_buf[par][:, :].rearrange("p (kc s) -> p kc s", kc=16),
                    xT_r[:, :, tt * NT:(tt + 1) * NT]),
                waits=[t_xfree[par], t_sqfree[par]], inc=sem, inc_by=16)

            p.wait("vector", t_xload[par],
                   t_tabcp if tt == 0 else t_kv_prev)
            t_cast = p.emit("vector",
                            lambda E, par=par: E.tensor_copy(h1_tmp[:, :],
                                                             xt_buf[par][:, :]),
                            inc="dv")
            p.wait("scalar", t_xload[par], t_pm_prev)
            t_sq = p.emit("scalar",
                          lambda E, par=par: E.activation(sq_buf[:, :],
                                                          xt_buf[par][:, :],
                                                          AF.Square,
                                                          bias=zerob[:, 0:1],
                                                          scale=1.0), inc="ac")
            t_sqfree[par] = t_sq
            p.wait("tensor", t_cast, t_sq,
                   t_tile_vlast if tt >= 1 else ("dv", p.cnt["dv"]))
            for kc in range(16):
                p.emit("tensor",
                       lambda E, kc=kc: E.matmul(ps_r1[0:1, 0:NT], ones_bf[:, 0:1],
                                                 h1_tmp[:, kc * NT:(kc + 1) * NT],
                                                 start=(kc == 0), stop=(kc == 15)))
            for kc in range(16):
                p.emit("tensor",
                       lambda E, kc=kc: E.matmul(ps_r2[0:1, 0:NT], ones_bf[:, 0:1],
                                                 sq_buf[:, kc * NT:(kc + 1) * NT],
                                                 start=(kc == 0), stop=(kc == 15)),
                       inc="pe" if kc == 15 else None)
            t_stat = ("pe", p.cnt["pe"])

            p.wait("vector", t_stat)

            def _ln_rows(E):
                E.tensor_scalar(abrow[0:1, 0:NT], ps_r1[0:1, 0:NT], 1.0 / D, None,
                                ALU.mult)
                E.tensor_scalar(abrow[0:1, NT:2 * NT], ps_r2[0:1, 0:NT], 1.0 / D,
                                None, ALU.mult)
                E.tensor_mul(tmp_f[0:1, 0:NT], abrow[0:1, 0:NT], abrow[0:1, 0:NT])
                SPC(E)
                E.tensor_sub(abrow[0:1, NT:2 * NT], abrow[0:1, NT:2 * NT],
                             tmp_f[0:1, 0:NT])
                SPC(E)
                return E.tensor_scalar(abrow[0:1, NT:2 * NT], abrow[0:1, NT:2 * NT],
                                EPS_LN, None, ALU.add)
            t_r1 = p.emit("vector", _ln_rows, inc="dv")
            p.wait("scalar", t_r1)
            p.emit("scalar", lambda E: E.activation(abrow[0:1, NT:2 * NT],
                                                    abrow[0:1, NT:2 * NT], AF.Ln,
                                                    bias=zerob[0:1, 0:1],
                                                    scale=1.0))
            t_r2 = p.emit("scalar", lambda E: E.activation(abrow[0:1, NT:2 * NT],
                                                           abrow[0:1, NT:2 * NT],
                                                           AF.Exp,
                                                           bias=zerob[0:1, 0:1],
                                                           scale=-0.5), inc="ac")
            p.wait("vector", t_r2)

            def _brow(E):
                E.tensor_mul(abrow[0:1, 0:NT], abrow[0:1, 0:NT],
                             abrow[0:1, NT:2 * NT])
                SPC(E)
                return E.tensor_scalar(abrow[0:1, 0:NT], abrow[0:1, 0:NT], -1.0, None,
                                ALU.mult)
            t_r3 = p.emit("vector", _brow, inc="dv")
            p.wait("tensor", t_r3)
            t_bc = p.emit("tensor",
                          lambda E: E.matmul(ps_bc[:, 0:512], ones_f[0:1, :],
                                             abrow[0:1, 0:512], start=True,
                                             stop=True), inc="pe")
            p.wait("vector", t_bc)

            def _h1_apply(E, par=par, tt=tt):
                ins = None
                for kc in range(16):
                    sl = kc % 2
                    E.tensor_mul(tmp_f[:, sl * NT:(sl + 1) * NT],
                                 xt_buf[par][:, kc * NT:(kc + 1) * NT],
                                 ps_bc[:, NT:2 * NT])
                    if kc > 0:
                        psl = (kc - 1) % 2
                        ins = E.tensor_tensor(h1sl(tt, kc - 1),
                                              tmp_f[:, psl * NT:(psl + 1) * NT],
                                              ps_bc[:, 0:NT], ALU.add)
                SPC(E)
                return E.tensor_tensor(h1sl(tt, 15), tmp_f[:, (15 % 2) * NT:
                                                           (15 % 2 + 1) * NT],
                                       ps_bc[:, 0:NT], ALU.add)
            p.emit("vector", _h1_apply, inc="dv")
            t_h1 = ("dv", p.cnt["dv"])
            t_xfree[par] = t_h1

            p.wait("tensor", t_h1, t_kva if tt == 0 else None, t_ck_prev)
            for cc in range(4):
                for kc in range(16):
                    p.emit("tensor",
                           lambda E, cc=cc, kc=kc, tt=tt: E.matmul(
                               ps_main[:, cc * NT:(cc + 1) * NT],
                               kva_ckv_bf[:, cc * 2048 + kc * 128:
                                          cc * 2048 + (kc + 1) * 128],
                               h1sl(tt, kc), start=(kc == 0), stop=(kc == 15)))
            for kc in range(16):
                p.emit("tensor",
                       lambda E, kc=kc, tt=tt: E.matmul(
                           ps_pe[0:64, 0:NT],
                           kva_pe_bf[:, kc * 64:(kc + 1) * 64],
                           h1sl(tt, kc), start=(kc == 0), stop=(kc == 15)),
                       inc="pe" if kc == 15 else None)
            t_kv = ("pe", p.cnt["pe"])
            t_kv_prev = t_kv

            p.wait("scalar", t_kv)
            for cc in range(4):
                p.emit("scalar",
                       lambda E, cc=cc: E.activation(
                           sq_buf[:, cc * NT:(cc + 1) * NT],
                           ps_main[:, cc * NT:(cc + 1) * NT], AF.Square,
                           bias=zerob[:, 0:1], scale=1.0),
                       inc="ac" if cc == 3 else None)
            t_csq = ("ac", p.cnt["ac"])
            p.wait("tensor", t_csq)
            for cc in range(4):
                p.emit("tensor",
                       lambda E, cc=cc: E.matmul(ps_r1[0:1, 0:NT], ones_bf[:, 0:1],
                                                 sq_buf[:, cc * NT:(cc + 1) * NT],
                                                 start=(cc == 0), stop=(cc == 3)),
                       inc="pe" if cc == 3 else None)
            t_cs2 = ("pe", p.cnt["pe"])
            p.wait("vector", t_cs2)
            t_rv = p.emit("vector",
                          lambda E: E.tensor_scalar(abrow[0:1, 0:NT],
                                                    ps_r1[0:1, 0:NT], 1.0 / KVL,
                                                    EPS_RMS, ALU.mult, ALU.add),
                          inc="dv")
            p.wait("scalar", t_rv)
            p.emit("scalar", lambda E: E.activation(abrow[0:1, 0:NT],
                                                    abrow[0:1, 0:NT], AF.Ln,
                                                    bias=zerob[0:1, 0:1],
                                                    scale=1.0))
            t_rs = p.emit("scalar", lambda E: E.activation(abrow[0:1, 0:NT],
                                                           abrow[0:1, 0:NT],
                                                           AF.Exp,
                                                           bias=zerob[0:1, 0:1],
                                                           scale=-0.5), inc="ac")
            p.wait("tensor", t_rs)
            t_rbc = p.emit("tensor",
                           lambda E: E.matmul(ps_bc[:, 512:768], ones_f[0:1, :],
                                              abrow[0:1, 0:NT], start=True,
                                              stop=True), inc="pe")
            p.wait("vector", t_rbc)

            def _ckv_norm(E, tt=tt):
                for cc in range(4):
                    sl = cc % 2
                    E.tensor_scalar(tmp_f[:, sl * NT:(sl + 1) * NT],
                                    ps_main[:, cc * NT:(cc + 1) * NT],
                                    kvaln_sb[:, cc:cc + 1], None, ALU.mult)
                    if cc > 0:
                        psl = (cc - 1) % 2
                        E.tensor_tensor(
                            ckv_bf[:, (cc - 1) * S + tt * NT:
                                   (cc - 1) * S + (tt + 1) * NT],
                            tmp_f[:, psl * NT:(psl + 1) * NT],
                            ps_bc[:, 512:768], ALU.mult)
                SPC(E)
                return E.tensor_tensor(
                    ckv_bf[:, 3 * S + tt * NT: 3 * S + (tt + 1) * NT],
                    tmp_f[:, NT:2 * NT], ps_bc[:, 512:768], ALU.mult)
            p.emit("vector", _ckv_norm, inc="dv")
            t_ck_prev = ("dv", p.cnt["dv"])

            t_kpev = p.emit("vector",
                            lambda E: E.tensor_copy(kpe_bf_t[:, :],
                                                    ps_pe[0:64, 0:NT]), inc="dv")
            p.wait("tensor", t_kpev)
            t_rx = p.emit("tensor",
                          lambda E: E.matmul(ps_pe[0:64, NT:2 * NT],
                                             rt_bf[:, :], kpe_bf_t[:, :],
                                             start=True, stop=True), inc="pe")
            p.wait("vector", t_rx)

            def _krope(E, tt=tt):
                E.tensor_mul(tmp_f[0:64, 0:NT], ps_pe[0:64, 0:NT],
                             cos_k[:, tt * NT:(tt + 1) * NT])
                E.tensor_mul(tmp_f[0:64, NT:2 * NT], ps_pe[0:64, NT:2 * NT],
                             sin_k[:, tt * NT:(tt + 1) * NT])
                SPC(E)
                return E.tensor_tensor(kpe_aug[0:64, tt * NT:(tt + 1) * NT],
                                tmp_f[0:64, 0:NT], tmp_f[0:64, NT:2 * NT], ALU.add)
            t_kr = p.emit("vector", _krope, inc="dv")
            p.wait("scalar", t_kr)
            t_ps = p.emit("scalar",
                          lambda E, tt=tt: E.activation(
                              sq_buf[0:64, 0:NT],
                              kpe_aug[0:64, tt * NT:(tt + 1) * NT], AF.Square,
                              bias=zerob[0:64, 0:1], scale=1.0), inc="ac")
            p.wait("tensor", t_ps)
            t_pm = p.emit("tensor",
                          lambda E: E.matmul(ps_r2[0:1, 256:256 + NT],
                                             ones_bf[0:64, 0:1],
                                             sq_buf[0:64, 0:NT],
                                             start=True, stop=True), inc="pe")
            t_pm_prev = t_pm
            p.wait("vector", t_pm)
            t_tile_vlast = p.emit(
                "vector",
                lambda E, tt=tt: E.tensor_copy(
                    R("pe2")[0:1, tt * NT:(tt + 1) * NT],
                    ps_r2[0:1, 256:256 + NT]),
                inc="dv")

        p.barrier()
        akv_scope.close()

        ab2_scope = es.enter_context(contextlib.ExitStack())
        AB2 = ab2_scope.enter_context
        qmid_bf = AB2(nc.sbuf_tensor("qmid_bf", [128, 12 * TQ], BF16))
        hw_buf = [AB2(nc.sbuf_tensor(f"hw_buf{i}", [128, 3328], BF16))
                  for i in range(2)]

        aq_scope = es.enter_context(contextlib.ExitStack())
        AQ = aq_scope.enter_context
        qa_buf = [AQ(nc.sbuf_tensor(f"qa_buf{i}", [128, 2048], BF16))
                  for i in range(2)]
        qmid_raw = AQ(nc.sbuf_tensor("qmid_raw", [128, 12 * TQ], F32))
        qsq_bf = AQ(nc.sbuf_tensor("qsq_bf", [128, TQ], BF16))
        qtmp_f = AQ(nc.sbuf_tensor("qtmp_f", [128, TQ], F32))
        qtmp2_f = AQ(nc.sbuf_tensor("qtmp2_f", [128, TQ], F32))
        qrow = AQ(nc.sbuf_tensor("qrow", [1, 1024], F32))

        t_qa_load = [None, None]
        t_qa_used = [None, None]
        t_qev = [None, None]
        t_qnmm_prev = None
        for mc in range(12):
            par = mc % 2
            PSQ = ps_main if par == 0 else ps_bc
            t_qa_load[par] = p.emit(
                "gpsimd",
                lambda E, mc=mc, par=par: E.dma_start(qa_buf[par][:, :], qa_pk[mc]),
                waits=[t_qa_used[par]], inc=f"dwq{par}", inc_by=16)
            p.wait("tensor", t_qa_load[par], t_qev[par],
                   ("dv", p.cnt["dv"]) if mc <= 1 else None)
            for kc in range(16):
                for nh in range(2):
                    p.emit("tensor",
                           lambda E, kc=kc, nh=nh, par=par, PSQ=PSQ: E.matmul(
                               PSQ[:, nh * 512:(nh + 1) * 512],
                               qa_buf[par][:, kc * 128:(kc + 1) * 128],
                               h1_own[:, kc * TQ + nh * 512:
                                      kc * TQ + (nh + 1) * 512],
                               start=(kc == 0), stop=(kc == 15)),
                           inc="pe" if (kc == 15 and nh == 1) else None)
            t_qa_used[par] = ("pe", p.cnt["pe"])
            p.wait("vector", t_qa_used[par])
            t_ev = p.emit("vector",
                          lambda E, mc=mc, PSQ=PSQ: E.tensor_copy(
                              qmid_raw[:, mc * TQ:(mc + 1) * TQ], PSQ[:, :]),
                          inc="dv")
            t_qev[par] = t_ev
            p.wait("scalar", t_ev, t_qnmm_prev if mc > 0 else t_stat)
            t_qsq = p.emit("scalar",
                           lambda E, mc=mc: E.activation(
                               qsq_bf[:, :], qmid_raw[:, mc * TQ:(mc + 1) * TQ],
                               AF.Square, bias=zerob[:, 0:1], scale=1.0), inc="ac")
            p.wait("tensor", t_qsq)
            for nh in range(2):
                p.emit("tensor",
                       lambda E, nh=nh, mc=mc: E.matmul(
                           (ps_r1 if nh == 0 else ps_r2)[0:1, 0:512],
                           ones_bf[:, 0:1], qsq_bf[:, nh * 512:(nh + 1) * 512],
                           start=(mc == 0), stop=(mc == 11)),
                       inc="pe" if nh == 1 else None)
            t_qnmm_prev = ("pe", p.cnt["pe"])
        t_qs2 = ("pe", p.cnt["pe"])
        p.wait("vector", t_qs2)

        def _qrms(E):
            E.tensor_scalar(qrow[0:1, 0:512], ps_r1[0:1, 0:512], 1.0 / QL,
                            EPS_RMS, ALU.mult, ALU.add)
            return E.tensor_scalar(qrow[0:1, 512:1024], ps_r2[0:1, 0:512], 1.0 / QL,
                            EPS_RMS, ALU.mult, ALU.add)
        t_qr = p.emit("vector", _qrms, inc="dv")
        p.wait("scalar", t_qr)
        p.emit("scalar", lambda E: E.activation(qrow[0:1, :], qrow[0:1, :],
                                                AF.Ln, bias=zerob[0:1, 0:1],
                                                scale=1.0))
        t_qr2 = p.emit("scalar", lambda E: E.activation(qrow[0:1, :],
                                                        qrow[0:1, :], AF.Exp,
                                                        bias=zerob[0:1, 0:1],
                                                        scale=-0.5), inc="ac")
        p.wait("tensor", t_qr2)
        for nh in range(2):
            p.emit("tensor",
                   lambda E, nh=nh: E.matmul(ps_bc[:, nh * 512:(nh + 1) * 512],
                                             ones_f[0:1, :],
                                             qrow[0:1, nh * 512:(nh + 1) * 512],
                                             start=True, stop=True),
                   inc="pe" if nh == 1 else None)
        t_qbc = ("pe", p.cnt["pe"])
        p.wait("vector", t_qbc)
        t_qapply = []

        def _q_apply_pair(E, mc):
            tmps = [qtmp_f, qtmp2_f]
            E.tensor_mul(tmps[mc % 2][:, :],
                         qmid_raw[:, mc * TQ:(mc + 1) * TQ], ps_bc[:, :])
            if mc > 0:
                return E.tensor_scalar(qmid_bf[:, (mc - 1) * TQ:mc * TQ],
                                       tmps[(mc - 1) % 2][:, :],
                                       qaln_sb[:, mc - 1:mc], None, ALU.mult)
            return SPC(E)

        for mc in range(12):
            t = p.emit("vector", lambda E, mc=mc: _q_apply_pair(E, mc), inc="dv")
            if mc > 0:
                t_qapply.append(t)
        t = p.emit("vector",
                   lambda E: (SPC(E),
                              E.tensor_scalar(qmid_bf[:, 11 * TQ:12 * TQ],
                                              qtmp2_f[:, :], qaln_sb[:, 11:12],
                                              None, ALU.mult))[-1], inc="dv")
        t_qapply.append(t)

        t_hw = [None] * H
        for h in range(2):
            t_hw[h] = p.emit("gpsimd",
                             lambda E, h=h: E.dma_start(hw_buf[h % 2][:, :],
                                                        hw_pk[h]),
                             inc=f"dwh{h % 2}", inc_by=16)

        if dbg:
            t_all = ("dv", p.cnt["dv"])
            p.wait("gpsimd", t_all)
            p.emit("gpsimd", lambda E: E.dma_start(dbg_out["d_qmid"], qmid_bf[:, :]),
                   inc="dso", inc_by=16)
            p.emit("gpsimd", lambda E: E.dma_start(dbg_out["d_ckv"], ckv_bf[:, :]),
                   inc="dso", inc_by=16)
            p.emit("gpsimd", lambda E: E.dma_start(dbg_out["d_kpe"], kpe_aug[:, :]),
                   inc="dso", inc_by=16)
            p.emit("gpsimd", lambda E: E.dma_start(dbg_out["d_pe2"], R("pe2")),
                   inc="dso", inc_by=16)
            p.emit("gpsimd", lambda E: E.dma_start(dbg_out["d_h1"], h1_own[:, :]),
                   inc="dso", inc_by=16)
            p.emit("gpsimd", lambda E: E.dma_start(dbg_out["d_cos"], cos_own[:, :]),
                   inc="dso", inc_by=16)
            p.emit("gpsimd", lambda E: E.dma_start(dbg_out["d_sin"], sin_own[:, :]),
                   inc="dso", inc_by=16)
            p.emit("gpsimd", lambda E: E.dma_start(dbg_out["d_rt"], rt_bf[:, :]),
                   inc="dso", inc_by=16)
            p.emit("gpsimd", lambda E: E.dma_start(dbg_out["d_lnw"][:, 0:12],
                                                   qaln_sb[:, :]),
                   inc="dso", inc_by=16)
            p.emit("gpsimd", lambda E: E.dma_start(dbg_out["d_lnw"][:, 12:16],
                                                   kvaln_sb[:, :]),
                   inc="dso", inc_by=16)
            p.emit("gpsimd", lambda E: E.dma_start(dbg_out["d_qraw"],
                                                   qmid_raw[:, 0:TQ]),
                   inc="dso", inc_by=16)
            p.emit("gpsimd", lambda E: E.dma_start(dbg_out["d_f2pi"], f2pi[:, :]),
                   inc="dso", inc_by=16)
            p.emit("gpsimd", lambda E: E.dma_start(dbg_out["d_qabuf"],
                                                   qa_buf[0][:, :]),
                   inc="dso", inc_by=16)

        p.enabled = phases >= 2
        aq_scope.close()
        aps_scope.close()
        h1r_scope.close()

        h2_bf = A(nc.sbuf_tensor("h2_bf", [128, 16 * TQ], BF16, side="right"))
        cat_scope = es.enter_context(contextlib.ExitStack())
        attn_cat = cat_scope.enter_context(
            nc.sbuf_tensor("attn_cat", [128, H * TQ], BF16, side="right"))

        b_scope = es.enter_context(contextlib.ExitStack())
        BS = b_scope.enter_context
        et_buf = [BS(nc.sbuf_tensor(f"et{i}", [128, TQ], BF16)) for i in range(2)]
        qn_bf = BS(nc.sbuf_tensor("qn_bf", [128, TQ], BF16))
        qpe_bf = BS(nc.sbuf_tensor("qpe_bf", [64, TQ], BF16))
        q_aug = BS(nc.sbuf_tensor("q_aug", [65, TQ], BF16))
        kn_bf = BS(nc.sbuf_tensor("kn_bf", [128, S], BF16))
        vt_bf = BS(nc.sbuf_tensor("vt_bf", [128, 16 * 128], BF16))
        sqb = BS(nc.sbuf_tensor("sqb", [128, 4096], BF16))
        rbc_sb = BS(nc.sbuf_tensor("rbc_sb", [128, TQ], F32))
        btmp1 = BS(nc.sbuf_tensor("btmp1", [64, TQ], F32))

        bps_scope = es.enter_context(contextlib.ExitStack())
        BPS = bps_scope.enter_context
        ps_proj = BPS(nc.psum_tensor("ps_proj", [128, 1024], F32))
        ps_sA = BPS(nc.psum_tensor("ps_sA", [128, 1024], F32))
        ps_sB = BPS(nc.psum_tensor("ps_sB", [128, 1024], F32))
        ps_av = BPS(nc.psum_tensor("ps_av", [128, 1024], F32))

        t_cat_prev = None
        t_hw_used = [None, None]
        t_rr_prev = None
        t_rbv_prev = None
        t_av_prev = None
        for h in range(H):
            par = h % 2
            HB = hw_buf[par]
            if h >= 2:
                t_hw[h] = p.emit("gpsimd",
                                 lambda E, h=h: E.dma_start(hw_buf[h % 2][:, :],
                                                            hw_pk[h]),
                                 waits=[t_hw_used[par]], inc=f"dwh{par}", inc_by=16)

            p.wait("tensor", t_hw[h], t_rr_prev)
            for kc in range(12):
                if h == 0:
                    p.wait("tensor", t_qapply[kc])
                for nh in range(2):
                    p.emit("tensor",
                           lambda E, kc=kc, nh=nh, HB=HB: E.matmul(
                               ps_proj[:, nh * 512:(nh + 1) * 512],
                               HB[:, kc * 128:(kc + 1) * 128],
                               qmid_bf[:, kc * TQ + nh * 512:kc * TQ + (nh + 1) * 512],
                               start=(kc == 0), stop=(kc == 11)),
                           inc="pe" if (kc == 11 and nh == 1) else None)
            t_qn = ("pe", p.cnt["pe"])
            p.wait("vector", t_qn)
            t_qnev = p.emit("vector",
                            lambda E: E.tensor_copy(qn_bf[:, :], ps_proj[:, :]),
                            inc="dv")
            p.wait("tensor", t_rbv_prev)
            for kc in range(12):
                for nh in range(2):
                    p.emit("tensor",
                           lambda E, kc=kc, nh=nh, HB=HB: E.matmul(
                               ps_sA[0:64, nh * 512:(nh + 1) * 512],
                               HB[:, 1536 + kc * 64:1536 + (kc + 1) * 64],
                               qmid_bf[:, kc * TQ + nh * 512:kc * TQ + (nh + 1) * 512],
                               start=(kc == 0), stop=(kc == 11)),
                           inc="pe" if (kc == 11 and nh == 1) else None)
            t_qpe = ("pe", p.cnt["pe"])
            p.wait("vector", t_qpe)
            t_qpev = p.emit("vector",
                            lambda E: E.tensor_copy(qpe_bf[:, :], ps_sA[0:64, :]),
                            inc="dv")
            p.wait("tensor", t_qpev)
            for nh in range(2):
                p.emit("tensor",
                       lambda E, nh=nh: E.matmul(
                           ps_sB[0:64, nh * 512:(nh + 1) * 512], rt_bf[:, :],
                           qpe_bf[:, nh * 512:(nh + 1) * 512],
                           start=True, stop=True),
                       inc="pe" if nh == 1 else None)
            t_rx = ("pe", p.cnt["pe"])
            p.wait("vector", t_rx)

            def _qrope(E):
                E.tensor_mul(q_aug[0:64, :], ps_sB[0:64, :], sin_own[:, :])
                E.tensor_mul(btmp1[:, :], ps_sA[0:64, :], cos_own[:, :])
                SPC(E)
                return E.tensor_tensor(q_aug[0:64, :], q_aug[0:64, :], btmp1[:, :],
                                ALU.add)
            t_qrope = p.emit("vector", _qrope, inc="dv")

            p.wait("tensor", t_cat_prev)
            for half, PSK in ((0, ps_av), (1, ps_proj)):
                if half == 1:
                    p.wait("tensor", t_qnev)
                for kc in range(4):
                    for nh in range(2):
                        p.emit("tensor",
                               lambda E, kc=kc, nh=nh, half=half, PSK=PSK, HB=HB:
                               E.matmul(
                                   PSK[:, nh * 512:(nh + 1) * 512],
                                   HB[:, 2304 + kc * 128:2304 + (kc + 1) * 128],
                                   ckv_bf[:, kc * S + half * TQ + nh * 512:
                                          kc * S + half * TQ + (nh + 1) * 512],
                                   start=(kc == 0), stop=(kc == 3)),
                               inc="pe" if (kc == 3 and nh == 1) else None)
            t_k = ("pe", p.cnt["pe"])
            p.wait("vector", t_k)
            t_kev = p.emit("vector",
                           lambda E: (E.tensor_copy(kn_bf[:, 0:TQ], ps_av[:, :]),
                                      E.tensor_copy(kn_bf[:, TQ:S],
                                                    ps_proj[:, :]))[-1],
                           inc="dv")

            p.wait("scalar", t_kev, t_qrope)
            t_sqq = p.emit("scalar",
                           lambda E: (E.activation(sqb[:, 2048:3072], qn_bf[:, :],
                                                   AF.Square, bias=zerob[:, 0:1],
                                                   scale=1.0),
                                      E.activation(sqb[0:64, 3072:4096],
                                                   q_aug[0:64, :], AF.Square,
                                                   bias=zerob[0:64, 0:1],
                                                   scale=1.0))[-1], inc="ac")
            t_sqk = p.emit("scalar",
                           lambda E: E.activation(sqb[:, 0:S], kn_bf[:, :],
                                                  AF.Square, bias=zerob[:, 0:1],
                                                  scale=1.0), inc="ac")
            p.wait("tensor", t_sqq)
            for nh in range(2):
                p.emit("tensor",
                       lambda E, nh=nh: E.matmul(
                           ps_sA[0:1, nh * 512:(nh + 1) * 512], ones_bf[:, 0:1],
                           sqb[:, 2048 + nh * 512:2048 + (nh + 1) * 512],
                           start=True, stop=True))
            for nh in range(2):
                p.emit("tensor",
                       lambda E, nh=nh: E.matmul(
                           ps_sA[0:1, nh * 512:(nh + 1) * 512],
                           ones_bf[0:64, 0:1],
                           sqb[0:64, 3072 + nh * 512:3072 + (nh + 1) * 512],
                           start=False, stop=True),
                       inc="pe" if nh == 1 else None)
            t_qn2 = ("pe", p.cnt["pe"])
            p.wait("tensor", t_sqk)
            for half, PSK in ((0, ps_av), (1, ps_proj)):
                for nh in range(2):
                    p.emit("tensor",
                           lambda E, half=half, nh=nh, PSK=PSK: E.matmul(
                               PSK[0:1, nh * 512:(nh + 1) * 512], ones_bf[:, 0:1],
                               sqb[:, half * TQ + nh * 512:
                                   half * TQ + (nh + 1) * 512],
                               start=True, stop=True),
                           inc="pe" if (half == 1 and nh == 1) else None)
            t_kn2 = ("pe", p.cnt["pe"])
            p.wait("vector", t_kn2)

            def _stats2(E):
                E.tensor_copy(R("qn2"), ps_sA[0:1, :])
                E.tensor_copy(R("kt2")[0:1, 0:TQ], ps_av[0:1, :])
                E.tensor_copy(R("kt2")[0:1, TQ:S], ps_proj[0:1, :])
                SPC(E)
                E.tensor_add(R("kt2"), R("kt2"), R("pe2"))
                SPC(E)
                SPC(E)
                E.tensor_reduce(km2[0:1, 0:1], R("kt2"), mybir.AxisListType.X,
                                ALU.max)
                SPC(E)
                return SPC(E)
            t_b1 = p.emit("vector", _stats2, inc="dv")
            p.wait("scalar", t_b1)
            p.emit("scalar", lambda E: E.activation(R("bias"), R("qn2"), AF.Ln,
                                                    bias=zerob[0:1, 0:1],
                                                    scale=km2[0:1, 0:1]))
            p.emit("scalar", lambda E: E.copy(fols[0:1, 0:256], fols[0:1, 0:256]))
            t_b3 = p.emit("scalar", lambda E: E.activation(R("bias"), R("bias"),
                                                           AF.Exp,
                                                           bias=zerob[0:1, 0:1],
                                                           scale=0.5), inc="ac")
            t_b4 = p.emit("vector", SPC, waits=[t_b3], inc="dv")
            t_bld = p.emit("gpsimd", lambda E: E.dma_start(q_aug[64:65, :],
                                                           R("bias")),
                           waits=[t_b4], inc="dwb", inc_by=16)
            if dbg and h == 0:
                p.wait("gpsimd", t_bld)
                p.emit("gpsimd", lambda E: E.dma_start(dbg_out["d_brow0"],
                                                       brow_scr),
                       inc="dso", inc_by=16)

            p.wait("tensor", t_b1)
            t_vt_ev_prev = [None, None]
            for tc in range(16):
                vpar = tc % 2
                po = vpar * 512
                p.wait("tensor", t_vt_ev_prev[vpar])
                for cc in range(4):
                    p.emit("tensor",
                           lambda E, tc=tc, cc=cc, po=po, HB=HB: E.matmul(
                               ps_proj[:, po:po + 128],
                               ckv_bf[:, cc * S + tc * 128:cc * S + (tc + 1) * 128],
                               HB[:, 2816 + cc * 128:2816 + (cc + 1) * 128],
                               start=(cc == 0), stop=(cc == 3)),
                           inc="pe" if cc == 3 else None)
                t_vt = ("pe", p.cnt["pe"])
                p.wait("vector", t_vt)
                t_vt_ev_prev[vpar] = p.emit(
                    "vector",
                    lambda E, tc=tc, po=po: E.tensor_copy(
                        vt_bf[:, tc * 128:(tc + 1) * 128], ps_proj[:, po:po + 128]),
                    inc="dv")

            t_expt = [None] * 16
            t_savc = [t_av_prev, t_av_prev]
            p.wait("tensor", t_bld)

            def _emit_scores(tc):
                spar = tc % 2
                PSS = ps_sA if spar == 0 else ps_sB
                if tc >= 2:
                    p.wait("tensor", (t_expt[tc - 2][0], t_expt[tc - 2][1] + 1))
                for nh in range(2):
                    p.emit("tensor",
                           lambda E, tc=tc, nh=nh, PSS=PSS: E.matmul(
                               PSS[:, nh * 512:(nh + 1) * 512],
                               kn_bf[:, tc * 128:(tc + 1) * 128],
                               qn_bf[:, nh * 512:(nh + 1) * 512],
                               start=True, stop=False))
                    p.emit("tensor",
                           lambda E, tc=tc, nh=nh, PSS=PSS: E.matmul(
                               PSS[:, nh * 512:(nh + 1) * 512],
                               kpe_aug[:, tc * 128:(tc + 1) * 128],
                               q_aug[:, nh * 512:(nh + 1) * 512],
                               start=False, stop=True),
                           inc="pe" if nh == 1 else None)
                t_sc = ("pe", p.cnt["pe"])
                p.wait("scalar", t_sc, t_savc[spar])
                PSS2 = PSS
                t_expt[tc] = p.emit("scalar",
                                    lambda E, spar=spar, PSS=PSS2:
                                    E.activation(et_buf[spar][:, :], PSS[:, :],
                                                 AF.Exp, bias=zerob[:, 0:1],
                                                 scale=SCALE),
                                    inc="ac", fol=False)

            def _emit_sa(tc):
                spar = tc % 2
                p.wait("tensor", (t_expt[tc][0], t_expt[tc][1] + 1))
                for nh in range(2):
                    p.emit("tensor",
                           lambda E, nh=nh, spar=spar, tc=tc: E.matmul(
                               ps_proj[0:1, nh * 512:(nh + 1) * 512],
                               ones_bf[:, 0:1],
                               et_buf[spar][:, nh * 512:(nh + 1) * 512],
                               start=(tc == 0), stop=(tc == 15)))
                for nh in range(2):
                    p.emit("tensor",
                           lambda E, tc=tc, nh=nh, spar=spar: E.matmul(
                               ps_av[:, nh * 512:(nh + 1) * 512],
                               vt_bf[:, tc * 128:(tc + 1) * 128],
                               et_buf[spar][:, nh * 512:(nh + 1) * 512],
                               start=(tc == 0), stop=(tc == 15)),
                           inc="pe" if nh == 1 else None)
                t_savc[spar] = ("pe", p.cnt["pe"])

            for tc in range(16):
                _emit_scores(tc)
                if tc >= 1:
                    _emit_sa(tc - 1)
            p.emit("scalar", lambda E: E.copy(fols[0:1, 0:8], fols[0:1, 0:8]),
                   inc="ac", fol=False)
            _emit_sa(15)

            t_sums = t_savc[1]
            p.wait("vector", t_sums)

            def _rsum(E):
                E.tensor_copy(R("sum"), ps_proj[0:1, :])
                SPC(E)
                return E.reciprocal(R("rsum"), R("sum"))
            t_rr = p.emit("vector", _rsum, inc="dv")
            t_rr_prev2 = t_rr
            p.wait("tensor", t_rr)
            for nh in range(2):
                p.emit("tensor",
                       lambda E, nh=nh: E.matmul(
                           ps_sA[:, nh * 512:(nh + 1) * 512], ones_f[0:1, :],
                           R("rsum")[0:1, nh * 512:(nh + 1) * 512],
                           start=True, stop=True),
                       inc="pe" if nh == 1 else None)
            t_rbc2 = ("pe", p.cnt["pe"])
            p.wait("vector", t_rbc2)
            t_rbv = p.emit("vector",
                           lambda E: E.tensor_copy(rbc_sb[:, :], ps_sA[:, :]),
                           inc="dv")
            t_cat_prev = p.emit("vector",
                                lambda E, h=h: E.tensor_mul(
                                    attn_cat[:, h * TQ:(h + 1) * TQ],
                                    ps_av[:, :], rbc_sb[:, :]),
                                inc="dv")
            t_hw_used[par] = t_savc[1]
            t_rr_prev = t_rr_prev2
            t_rbv_prev = t_rbv
            t_av_prev = t_savc[1]
            if dbg and h == 0:
                p.wait("gpsimd", t_cat_prev)
                for nm, src_ap in [("d_qn0", qn_bf[:, :]), ("d_kn0", kn_bf[:, :]),
                                   ("d_qaug0", q_aug[:, :]), ("d_vt0", vt_bf[:, :]),
                                   ("d_et0", et_buf[1][:, :]),
                                   ("d_rbc0", rbc_sb[:, :])]:
                    p.emit("gpsimd",
                           lambda E, nm=nm, src_ap=src_ap: E.dma_start(
                               dbg_out[nm], src_ap), inc="dso", inc_by=16)

        if dbg:
            p.wait("gpsimd", t_cat_prev)
            p.emit("gpsimd", lambda E: E.dma_start(dbg_out["d_cat"], attn_cat[:, :]),
                   inc="dso", inc_by=16)

        p.enabled = phases >= 3
        p.barrier()
        b_scope.close()
        bps_scope.close()
        ab2_scope.close()
        ab1_scope.close()

        cd_scope = es.enter_context(contextlib.ExitStack())
        CD = cd_scope.enter_context
        x1r_buf = [CD(nc.sbuf_tensor(f"x1r{i}", [128, TQ], F32)) for i in range(2)]
        o_buf = [CD(nc.sbuf_tensor(f"o_buf{i}", [128, 2048], BF16))
                 for i in range(2)]
        g_buf = [CD(nc.sbuf_tensor(f"g_buf{i}", [128, 2048], BF16))
                 for i in range(2)]
        u_buf = [CD(nc.sbuf_tensor(f"u_buf{i}", [128, 2048], BF16))
                 for i in range(2)]

        c_scope = es.enter_context(contextlib.ExitStack())
        CS = c_scope.enter_context
        xc_buf = [CS(nc.sbuf_tensor(f"xc{i}", [128, TQ], F32)) for i in range(2)]
        x1_buf = [CS(nc.sbuf_tensor(f"x1b{i}", [128, TQ], F32)) for i in range(2)]
        cb_bf = CS(nc.sbuf_tensor("cb_bf", [128, TQ], BF16))
        csq_bf = CS(nc.sbuf_tensor("csq_bf", [128, TQ], BF16))
        ctmp = CS(nc.sbuf_tensor("ctmp", [128, TQ], F32))

        cps_scope = es.enter_context(contextlib.ExitStack())
        CPS = cps_scope.enter_context
        ps_oA = CPS(nc.psum_tensor("ps_oA", [128, 1024], F32))
        ps_oB = CPS(nc.psum_tensor("ps_oB", [128, 1024], F32))
        ps_c1 = CPS(nc.psum_tensor("ps_c1", [1, 512], F32))
        ps_c2 = CPS(nc.psum_tensor("ps_c2", [1, 512], F32))
        ps_c3 = CPS(nc.psum_tensor("ps_c3", [1, 512], F32))
        ps_c4 = CPS(nc.psum_tensor("ps_c4", [1, 512], F32))

        t_o_load = [None, None]
        t_o_used = [None, None]
        t_xc_load = [None, None]
        t_x1_st = [None, None]
        t_oev = [None, None]
        t_cst_prev = None
        for mc in range(16):
            par = mc % 2
            PSO = ps_oA if par == 0 else ps_oB
            t_o_load[par] = p.emit("gpsimd",
                                   lambda E, mc=mc, par=par: E.dma_start(
                                       o_buf[par][:, :], o_pk[mc]),
                                   waits=[t_o_used[par]],
                                   inc=f"dwo{par}", inc_by=16)
            t_xc_load[par] = p.emit("sync",
                                    lambda E, mc=mc, par=par: E.dma_start(
                                        xc_buf[par][:, :], xT_r[:, mc, 0:TQ]),
                                    waits=[t_oev[par]], inc=f"dxc{par}", inc_by=16)
            p.wait("tensor", t_o_load[par], t_oev[par])
            for kc in range(16):
                for nh in range(2):
                    p.emit("tensor",
                           lambda E, kc=kc, nh=nh, par=par, PSO=PSO: E.matmul(
                               PSO[:, nh * 512:(nh + 1) * 512],
                               o_buf[par][:, kc * 128:(kc + 1) * 128],
                               attn_cat[:, kc * TQ + nh * 512:
                                        kc * TQ + (nh + 1) * 512],
                               start=(kc == 0), stop=(kc == 15)),
                           inc="pe" if (kc == 15 and nh == 1) else None)
            t_o_used[par] = ("pe", p.cnt["pe"])
            p.wait("vector", t_o_used[par], t_xc_load[par], t_x1_st[par])
            t_x1 = p.emit("vector",
                          lambda E, par=par, PSO=PSO: E.tensor_tensor(
                              x1_buf[par][:, :], PSO[:, :], xc_buf[par][:, :],
                              ALU.add), inc="dv")
            t_oev[par] = t_x1
            t_x1_st[par] = p.emit("sync",
                                  lambda E, mc=mc, par=par: E.dma_start(
                                      x1_scr[mc], x1_buf[par][:, :]),
                                  waits=[t_x1], inc=f"dx1w{par}", inc_by=16)
            p.wait("vector", t_cst_prev)
            t_cb = p.emit("vector",
                          lambda E, par=par: E.tensor_copy(cb_bf[:, :],
                                                           x1_buf[par][:, :]),
                          inc="dv")
            p.wait("scalar", t_x1, t_cst_prev)
            t_cs = p.emit("scalar",
                          lambda E, par=par: E.activation(csq_bf[:, :],
                                                          x1_buf[par][:, :],
                                                          AF.Square,
                                                          bias=zerob[:, 0:1],
                                                          scale=1.0), inc="ac")
            p.wait("tensor", t_cb, t_cs)
            for nh in range(2):
                p.emit("tensor",
                       lambda E, nh=nh, mc=mc: E.matmul(
                           (ps_c1 if nh == 0 else ps_c2)[0:1, 0:512],
                           ones_bf[:, 0:1], cb_bf[:, nh * 512:(nh + 1) * 512],
                           start=(mc == 0), stop=(mc == 15)))
            for nh in range(2):
                p.emit("tensor",
                       lambda E, nh=nh, mc=mc: E.matmul(
                           (ps_c3 if nh == 0 else ps_c4)[0:1, 0:512],
                           ones_bf[:, 0:1], csq_bf[:, nh * 512:(nh + 1) * 512],
                           start=(mc == 0), stop=(mc == 15)),
                       inc="pe" if nh == 1 else None)
            t_cst_prev = ("pe", p.cnt["pe"])
        t_cstat = ("pe", p.cnt["pe"])

        p.wait("vector", t_cstat)

        def _ln2rows(E):
            cr = R("crow")
            E.tensor_scalar(cr[0:1, 0:512], ps_c1[0:1, :], 1.0 / D, None, ALU.mult)
            E.tensor_scalar(cr[0:1, 512:1024], ps_c2[0:1, :], 1.0 / D, None,
                            ALU.mult)
            E.tensor_scalar(cr[0:1, 1024:1536], ps_c3[0:1, :], 1.0 / D, None,
                            ALU.mult)
            E.tensor_scalar(cr[0:1, 1536:2048], ps_c4[0:1, :], 1.0 / D, None,
                            ALU.mult)
            E.tensor_mul(ctmp[0:1, 0:1024], cr[0:1, 0:1024], cr[0:1, 0:1024])
            SPC(E)
            E.tensor_sub(cr[0:1, 1024:2048], cr[0:1, 1024:2048],
                         ctmp[0:1, 0:1024])
            SPC(E)
            return E.tensor_scalar(cr[0:1, 1024:2048], cr[0:1, 1024:2048], EPS_LN,
                            None, ALU.add)
        t_c1 = p.emit("vector", _ln2rows, inc="dv")
        p.wait("scalar", t_c1)
        p.emit("scalar", lambda E: E.activation(R("crow")[0:1, 1024:2048],
                                                R("crow")[0:1, 1024:2048], AF.Ln,
                                                bias=zerob[0:1, 0:1], scale=1.0))
        t_c2 = p.emit("scalar", lambda E: E.activation(R("crow")[0:1, 1024:2048],
                                                       R("crow")[0:1, 1024:2048],
                                                       AF.Exp,
                                                       bias=zerob[0:1, 0:1],
                                                       scale=-0.5), inc="ac")
        p.wait("vector", t_c2)

        def _c_b(E):
            cr = R("crow")
            E.tensor_mul(cr[0:1, 0:1024], cr[0:1, 0:1024], cr[0:1, 1024:2048])
            SPC(E)
            return E.tensor_scalar(cr[0:1, 0:1024], cr[0:1, 0:1024], -1.0, None,
                            ALU.mult)
        t_c3 = p.emit("vector", _c_b, inc="dv")
        p.wait("tensor", t_c3)
        for nh in range(2):
            p.emit("tensor",
                   lambda E, nh=nh: E.matmul(ps_oA[:, nh * 512:(nh + 1) * 512],
                                             ones_f[0:1, :],
                                             R("crow")[0:1, 1024 + nh * 512:
                                                       1024 + (nh + 1) * 512],
                                             start=True, stop=True))
        for nh in range(2):
            p.emit("tensor",
                   lambda E, nh=nh: E.matmul(ps_oB[:, nh * 512:(nh + 1) * 512],
                                             ones_f[0:1, :],
                                             R("crow")[0:1, nh * 512:
                                                       (nh + 1) * 512],
                                             start=True, stop=True),
                   inc="pe" if nh == 1 else None)
        t_cbc = ("pe", p.cnt["pe"])

        t_x1r = [None, None]
        t_h2v = [None, None]
        t_h2chunk = []
        for mc in range(16):
            par = mc % 2
            t_x1r[par] = p.emit("sync",
                                lambda E, mc=mc, par=par: E.dma_start(
                                    x1r_buf[par][:, :], x1_scr[mc]),
                                waits=[("dx1w0", p.cnt["dx1w0"]),
                                       ("dx1w1", p.cnt["dx1w1"]), t_h2v[par]],
                                inc=f"dx1r{par}", inc_by=16)
            p.wait("vector", t_x1r[par], t_cbc if mc == 0 else None)
            t_h2v[par] = p.emit(
                "vector",
                lambda E, mc=mc, par=par: (
                    E.tensor_mul(ctmp[:, :], x1r_buf[par][:, :], ps_oA[:, :]),
                    SPC(E),
                    E.tensor_tensor(h2_bf[:, mc * TQ:(mc + 1) * TQ], ctmp[:, :],
                                    ps_oB[:, :], ALU.add))[-1],
                inc="dv")
            t_h2chunk.append(t_h2v[par])

        if dbg:
            p.wait("gpsimd", t_h2v[1])
            p.emit("gpsimd", lambda E: E.dma_start(dbg_out["d_h2"], h2_bf[:, :]),
                   inc="dso", inc_by=16)

        p.enabled = phases >= 4
        if p.enabled:
            t_h2_all = ("dv", p.cnt["dv"])
            p.wait("gpsimd", t_h2_all)
            p.wait("sync", t_h2_all)
        cat_scope.close()
        c_scope.close()
        cps_scope.close()

        NOCT = 8
        OCT = 64 // NOCT
        d_scope = es.enter_context(contextlib.ExitStack())
        DS = d_scope.enter_context
        gated = DS(nc.sbuf_tensor("gated", [128, OCT * TQ], BF16))
        silu_bf = [DS(nc.sbuf_tensor(f"silu{i}", [128, TQ], BF16)) for i in range(2)]
        d_buf = [DS(nc.sbuf_tensor(f"d_buf{i}", [128, OCT * 128], BF16))
                 for i in range(2)]
        dtmp = [DS(nc.sbuf_tensor(f"dtmp{i}", [128, TQ], F32)) for i in range(2)]
        ffn_acc = DS(nc.sbuf_tensor("ffn_acc", [128, 16 * TQ], F32, side="right"))

        dps_scope = es.enter_context(contextlib.ExitStack())
        DPS = dps_scope.enter_context
        ps_d = [DPS(nc.psum_tensor(f"ps_d{i}", [128, 1024], F32))
                for i in range(2)]
        ps_g = DPS(nc.psum_tensor("ps_g", [128, 1024], F32))
        ps_u = DPS(nc.psum_tensor("ps_u", [128, 1024], F32))

        t_g_load = [None, None]
        t_u_load = [None, None]
        t_d_load = [None, None]
        t_d_used = [None, None]
        t_sil_prev = None
        t_mul_prev = None
        t_gated_wr = [None, None]
        t_gmm_prev = [None, None]
        t_umm_prev = [None, None]
        t_dp_done = None
        t_dev = [None, None]
        t_yst = [None, None]
        t_x1r2 = [None, None]

        for oo in range(NOCT):
            for hl in range(OCT):
                hc = oo * OCT + hl
                par = hc % 2
                t_g_load[par] = p.emit("gpsimd",
                                       lambda E, hc=hc, par=par: E.dma_start(
                                           g_buf[par][:, :], g_pk[hc]),
                                       waits=[t_gmm_prev[par]],
                                       inc=f"dwg{par}", inc_by=16)
                t_u_load[par] = p.emit("gpsimd",
                                       lambda E, hc=hc, par=par: E.dma_start(
                                           u_buf[par][:, :], u_pk[hc]),
                                       waits=[t_umm_prev[par]],
                                       inc=f"dwu{par}", inc_by=16)
                p.wait("tensor", t_g_load[par], t_sil_prev)
                for kc in range(16):
                    if hc == 0:
                        p.wait("tensor", t_h2chunk[kc])
                    for nh in range(2):
                        p.emit("tensor",
                               lambda E, kc=kc, nh=nh, par=par: E.matmul(
                                   ps_g[:, nh * 512:(nh + 1) * 512],
                                   g_buf[par][:, kc * 128:(kc + 1) * 128],
                                   h2_bf[:, kc * TQ + nh * 512:
                                         kc * TQ + (nh + 1) * 512],
                                   start=(kc == 0), stop=(kc == 15)),
                               inc="pe" if (kc == 15 and nh == 1) else None)
                t_gmm = ("pe", p.cnt["pe"])
                t_gmm_prev[par] = t_gmm
                p.wait("tensor", t_u_load[par], t_mul_prev)
                for kc in range(16):
                    for nh in range(2):
                        p.emit("tensor",
                               lambda E, kc=kc, nh=nh, par=par: E.matmul(
                                   ps_u[:, nh * 512:(nh + 1) * 512],
                                   u_buf[par][:, kc * 128:(kc + 1) * 128],
                                   h2_bf[:, kc * TQ + nh * 512:
                                         kc * TQ + (nh + 1) * 512],
                                   start=(kc == 0), stop=(kc == 15)),
                               inc="pe" if (kc == 15 and nh == 1) else None)
                t_umm = ("pe", p.cnt["pe"])
                t_umm_prev[par] = t_umm
                p.wait("scalar", t_gmm, t_gated_wr[par])
                t_sil = p.emit("scalar",
                               lambda E, par=par: E.activation(
                                   silu_bf[par][:, :], ps_g[:, :], AF.Silu,
                                   bias=zerob[:, 0:1], scale=1.0), inc="ac")
                t_sil_prev = t_sil
                p.wait("vector", t_sil, t_umm,
                       t_dp_done if (oo > 0 and hl < 2) else None)
                t_gated_wr[par] = p.emit(
                    "vector",
                    lambda E, hl=hl, par=par: E.tensor_mul(
                        gated[:, hl * TQ:(hl + 1) * TQ], silu_bf[par][:, :],
                        ps_u[:, :]), inc="dv")
                t_mul_prev = t_gated_wr[par]

            t_gated_all = ("dv", p.cnt["dv"])

            p.wait("tensor", t_gated_all)
            for mc in range(16):
                par = mc % 2
                t_d_load[par] = p.emit("gpsimd",
                                       lambda E, oo=oo, mc=mc, par=par:
                                       E.dma_start(
                                           d_buf[par][:, :],
                                           d_pk[mc][:, oo * OCT * 128:
                                                    (oo + 1) * OCT * 128]),
                                       waits=[t_d_used[par]],
                                       inc=f"dwd{par}", inc_by=16)
                if oo == NOCT - 1:
                    t_x1r2[par] = p.emit("sync",
                                         lambda E, mc=mc, par=par: E.dma_start(
                                             x1r_buf[par][:, :], x1_scr[mc]),
                                         waits=[t_yst[par]],
                                         inc=f"dx1r{par}", inc_by=16)
                p.wait("tensor", t_d_load[par], t_dev[par])
                for kc in range(OCT):
                    for nh in range(2):
                        p.emit("tensor",
                               lambda E, kc=kc, nh=nh, par=par: E.matmul(
                                   ps_d[par][:, nh * 512:(nh + 1) * 512],
                                   d_buf[par][:, kc * 128:(kc + 1) * 128],
                                   gated[:, kc * TQ + nh * 512:
                                         kc * TQ + (nh + 1) * 512],
                                   start=(kc == 0), stop=(kc == OCT - 1)),
                               inc="pe" if (kc == OCT - 1 and nh == 1) else None)
                t_dmm = ("pe", p.cnt["pe"])
                t_d_used[par] = t_dmm
                p.wait("vector", t_dmm)
                if oo == 0:
                    t_dev[par] = p.emit("vector",
                                        lambda E, mc=mc, par=par: E.tensor_copy(
                                            ffn_acc[:, mc * TQ:(mc + 1) * TQ],
                                            ps_d[par][:, :]), inc="dv")
                elif oo < NOCT - 1:
                    t_dev[par] = p.emit("vector",
                                        lambda E, mc=mc, par=par:
                                        E.tensor_tensor(
                                            ffn_acc[:, mc * TQ:(mc + 1) * TQ],
                                            ffn_acc[:, mc * TQ:(mc + 1) * TQ],
                                            ps_d[par][:, :], ALU.add), inc="dv")
                else:
                    p.wait("vector", t_x1r2[par])
                    t_dev[par] = p.emit(
                        "vector",
                        lambda E, mc=mc, par=par: (
                            E.tensor_tensor(dtmp[par][:, :], ps_d[par][:, :],
                                            ffn_acc[:, mc * TQ:(mc + 1) * TQ],
                                            ALU.add),
                            SPC(E),
                            E.tensor_tensor(dtmp[par][:, :], dtmp[par][:, :],
                                            x1r_buf[par][:, :], ALU.add))[-1],
                        inc="dv")
                    t_yst[par] = p.emit("sync",
                                        lambda E, mc=mc, par=par: E.dma_start(
                                            y_out[mc * 128:(mc + 1) * 128, :],
                                            dtmp[par][:, :]),
                                        waits=[t_dev[par]], inc="dso", inc_by=16)
            t_dp_done = ("pe", p.cnt["pe"])

        p.wait("sync", ("dso", p.cnt["dso"]))

        with nc.Block() as block:
            @block.tensor
            def _(E):
                for fn in p.q["tensor"]:
                    fn(E)

            @block.vector
            def _(E):
                for fn in p.q["vector"]:
                    fn(E)

            @block.scalar
            def _(E):
                for fn in p.q["scalar"]:
                    fn(E)

            @block.gpsimd
            def _(E):
                for fn in p.q["gpsimd"]:
                    fn(E)

            @block.sync
            def _(E):
                for fn in p.q["sync"]:
                    fn(E)

    return nc



_NC_CACHE = {}


def _get_nc(dbg=False):
    if dbg not in _NC_CACHE:
        _NC_CACHE[dbg] = build_nc(dbg)
    return _NC_CACHE[dbg]


def kernel(**inputs):
    x = np.asarray(inputs["L0_attn_x"], np.float32)
    w = _stage_weights(inputs)
    in_maps = []
    for c in range(8):
        b, half = c // 2, c % 2
        own = x[b, half * TQ:(half + 1) * TQ, :]
        other = x[b, (1 - half) * TQ:(2 - half) * TQ, :]
        xT = np.ascontiguousarray(np.concatenate([own, other], axis=0).T)
        m = {"xT": xT}
        m.update(w)
        in_maps.append(m)
    nc = _get_nc(False)
    res = run_bass_kernel_spmd(nc, in_maps, core_ids=list(range(8)))
    out = np.empty((NB, S, D), np.float32)
    for c in range(8):
        b, half = c // 2, c % 2
        out[b, half * TQ:(half + 1) * TQ, :] = res.results[c]["y"].T
    return out


# version 10 (final)
# speedup vs baseline: 1.0460x; 1.0071x over previous
import contextlib
import numpy as np
import concourse.bass as bass
import concourse.mybir as mybir
from concourse.bass_utils import run_bass_kernel_spmd

F32 = mybir.dt.float32
BF16 = mybir.dt.bfloat16
I32 = mybir.dt.int32
AF = mybir.ActivationFunctionType
ALU = mybir.AluOpType
ET = mybir.EngineType

D = 2048
S = 2048
NB = 4
H = 16
QL = 1536
KVL = 512
DN = 128
DR = 64
DV = 128
FH = 8192
TQ = 1024
SCALE = float((DN + DR) ** -0.5)
MAGIC = float(1.5 * 2 ** 23)
INV2PI = float(1.0 / (2.0 * np.pi))
TWOPI = float(2.0 * np.pi)
ROPE_ES = float(-np.log(10000.0) / 32.0)
EPS_LN = 1e-5
EPS_RMS = 1e-6
NT = 256
NTILES = S // NT

ENGS = ("tensor", "vector", "scalar", "gpsimd", "sync")



def _pack_lhsT(w, mb=128):
    Fd, Dk = w.shape
    a = w.reshape(Fd // mb, mb, Dk // 128, 128).transpose(0, 3, 2, 1)
    return np.ascontiguousarray(a.reshape(Fd // mb, 128, -1), dtype=np.float32)


_DEINT = np.concatenate([np.arange(0, 64, 2), np.arange(1, 64, 2)])


def _stage_weights(inp):
    qa = np.asarray(inp["L0_attn_q_a_w"], np.float32)
    qb = np.asarray(inp["L0_attn_q_b_w"], np.float32)
    kva = np.asarray(inp["L0_attn_kv_a_w"], np.float32)
    kvb = np.asarray(inp["L0_attn_kv_b_w"], np.float32)
    d = {
        "qa_pack": _pack_lhsT(qa, 128),
        "qaln_pack": np.ascontiguousarray(
            np.asarray(inp["L0_attn_q_a_ln_w"], np.float32).reshape(12, 128)),
        "kva_ckv": _pack_lhsT(kva[0:KVL, :], 128),
        "kva_pe": _pack_lhsT(kva[KVL + _DEINT, :], 64)[0],
        "kvaln_pack": np.ascontiguousarray(
            np.asarray(inp["L0_attn_kv_a_ln_w"], np.float32).reshape(4, 128)),
        "o_pack": _pack_lhsT(np.asarray(inp["L0_attn_o_w"], np.float32), 128),
        "g_pack": _pack_lhsT(np.asarray(inp["L0_ffn_w_gate"], np.float32), 128),
        "u_pack": _pack_lhsT(np.asarray(inp["L0_ffn_w_up"], np.float32), 128),
        "d_pack": _pack_lhsT(np.asarray(inp["L0_ffn_w_down"], np.float32), 128),
    }
    hw = np.empty((H, 128, 3328), np.float32)
    for h in range(H):
        r0 = h * (DN + DR)
        hw[h, :, 0:1536] = _pack_lhsT(qb[r0:r0 + DN, :], 128)[0]
        hw[h, :, 1536:2304] = _pack_lhsT(qb[r0 + DN + _DEINT, :], 64)[0]
        r1 = h * (DN + DV)
        hw[h, :, 2304:2816] = _pack_lhsT(kvb[r1:r1 + DN, :], 128)[0]
        hw[h, :, 2816:3328] = _pack_lhsT(kvb[r1 + DN:r1 + DN + DV, :], 128)[0]
    d["headw"] = hw
    return d



class Prog:
    def __init__(self, nc, es):
        self.nc = nc
        self.es = es
        self.q = {e: [] for e in ENGS}
        self.sems = {}
        self.cnt = {}
        self.enabled = True
        self.fol = {}

    def sem(self, name):
        self.sems[name] = self.es.enter_context(self.nc.semaphore(name))
        self.cnt[name] = 0

    def emit(self, eng, build_fn, waits=(), inc=None, inc_by=1, fol=True):
        if not self.enabled:
            return None
        tick = None
        if inc is not None:
            self.cnt[inc] += inc_by
            tick = (inc, self.cnt[inc])
        waits = tuple(w for w in waits if w is not None)
        sems = self.sems
        fol = self.fol.get(eng) if (inc is not None and fol) else None

        def fn(E, _w=waits, _i=inc, _b=build_fn, _n=inc_by, _f=fol):
            for (s, v) in _w:
                E.wait_ge(sems[s], v)
            ins = _b(E)
            if _f is not None:
                ins = _f(E)
            if _i is not None:
                ins.then_inc(sems[_i], _n)

        self.q[eng].append(fn)
        return tick

    def wait(self, eng, *ticks):
        if not self.enabled:
            return
        ticks = tuple(t for t in ticks if t is not None)
        if not ticks:
            return
        sems = self.sems

        def fn(E, _t=ticks):
            for (s, v) in _t:
                E.wait_ge(sems[s], v)

        self.q[eng].append(fn)

    def barrier(self):
        if not self.enabled:
            return None
        for e in ("tensor", "vector", "scalar"):
            self.emit(e, lambda E: E.nop(), inc="bar")
        t = ("bar", self.cnt["bar"])
        for e in ENGS:
            self.wait(e, t)
        return t



def build_nc(dbg=False, phases=4):
    nc = bass.Bass("TRN2", target_bir_lowering=False)

    xT = nc.dram_tensor("xT", [D, S], F32, kind="ExternalInput").ap()
    qa_pk = nc.dram_tensor("qa_pack", [12, 128, 2048], F32, kind="ExternalInput").ap()
    qaln_pk = nc.dram_tensor("qaln_pack", [12, 128], F32, kind="ExternalInput").ap()
    kva_ckv_pk = nc.dram_tensor("kva_ckv", [4, 128, 2048], F32, kind="ExternalInput").ap()
    kva_pe_pk = nc.dram_tensor("kva_pe", [128, 1024], F32, kind="ExternalInput").ap()
    kvaln_pk = nc.dram_tensor("kvaln_pack", [4, 128], F32, kind="ExternalInput").ap()
    hw_pk = nc.dram_tensor("headw", [H, 128, 3328], F32, kind="ExternalInput").ap()
    o_pk = nc.dram_tensor("o_pack", [16, 128, 2048], F32, kind="ExternalInput").ap()
    g_pk = nc.dram_tensor("g_pack", [64, 128, 2048], F32, kind="ExternalInput").ap()
    u_pk = nc.dram_tensor("u_pack", [64, 128, 2048], F32, kind="ExternalInput").ap()
    d_pk = nc.dram_tensor("d_pack", [16, 128, 8192], F32, kind="ExternalInput").ap()
    y_out = nc.dram_tensor("y", [D, TQ], F32, kind="ExternalOutput").ap()
    x1_scr = nc.dram_tensor("x1_scr", [16, 128, TQ], F32, kind="Internal").ap()
    brow_scr = nc.dram_tensor("brow_scr", [1, TQ], F32, kind="Internal").ap()

    dbg_out = {}
    if dbg:
        for nm, shp, dt in [("d_qmid", [128, 12 * TQ], F32),
                            ("d_ckv", [128, 4 * S], F32),
                            ("d_kpe", [65, S], F32),
                            ("d_pe2", [1, S], F32),
                            ("d_cat", [128, H * TQ], F32),
                            ("d_h2", [128, 16 * TQ], F32),
                            ("d_h1", [128, 16 * TQ], F32),
                            ("d_cos", [64, TQ], F32),
                            ("d_sin", [64, TQ], F32),
                            ("d_rt", [64, 64], F32),
                            ("d_lnw", [128, 16], F32),
                            ("d_qraw", [128, TQ], F32),
                            ("d_posk", [1, S], F32),
                            ("d_f2pi", [64, 1], F32),
                            ("d_qabuf", [128, 2048], F32),
                            ("d_qn0", [128, TQ], F32),
                            ("d_kn0", [128, S], F32),
                            ("d_qaug0", [65, TQ], F32),
                            ("d_vt0", [128, 2048], F32),
                            ("d_et0", [128, TQ], F32),
                            ("d_rbc0", [128, TQ], F32),
                            ("d_rows0", [1, 6144], F32),
                            ("d_brow0", [1, TQ], F32)]:
            dbg_out[nm] = nc.dram_tensor(nm, shp, dt, kind="ExternalOutput").ap()

    es_root = contextlib.ExitStack()
    with es_root as es:
        p = Prog(nc, es)
        for s in ("pe", "dv", "ac", "gp", "bar",
                  "dwk", "dwq0", "dwq1", "dwh0", "dwh1", "dwo0", "dwo1",
                  "dwg0", "dwg1", "dwu0", "dwu1", "dwd0", "dwd1", "dwb",
                  "dxx0", "dxx1", "dxc0", "dxc1", "dx1w0", "dx1w1",
                  "dx1r0", "dx1r1", "dlnw", "dxb", "dso"):
            p.sem(s)

        A = es.enter_context

        ones_bf = A(nc.sbuf_tensor("ones_bf", [128, 1], BF16))
        ones_f = A(nc.sbuf_tensor("ones_f", [1, 128], F32))
        zerob = A(nc.sbuf_tensor("zerob", [128, 1], F32))
        ident_f = A(nc.sbuf_tensor("ident_f", [128, 128], F32))
        iota_bc = A(nc.sbuf_tensor("iota_bc", [128, 128], F32))
        pidx128 = A(nc.sbuf_tensor("pidx128", [128, 1], F32))
        rt_bf = A(nc.sbuf_tensor("rt_bf", [64, 64], BF16))
        f2pi = A(nc.sbuf_tensor("f2pi", [64, 1], F32))
        pid_i = A(nc.sbuf_tensor("pid_i", [1, 1], I32))
        pid_f = A(nc.sbuf_tensor("pid_f", [1, 1], F32))
        halfv = A(nc.sbuf_tensor("halfv", [1, 1], F32))
        obase = A(nc.sbuf_tensor("obase", [1, 1], F32))
        scr64a = A(nc.sbuf_tensor("scr64a", [64, 64], F32))
        scr64b = A(nc.sbuf_tensor("scr64b", [64, 64], F32))
        scr64c = A(nc.sbuf_tensor("scr64c", [64, 64], F32))
        scr64d = A(nc.sbuf_tensor("scr64d", [64, 64], F32))
        qaln_sb = A(nc.sbuf_tensor("qaln_sb", [128, 12], F32))
        kvaln_sb = A(nc.sbuf_tensor("kvaln_sb", [128, 4], F32))
        lnw_raw = A(nc.sbuf_tensor("lnw_raw", [12, 128], F32))
        rows = A(nc.sbuf_tensor("rows", [1, 6144], F32))
        km2 = A(nc.sbuf_tensor("km2", [1, 1], F32))
        folv = A(nc.sbuf_tensor("folv", [1, 256], F32))
        fols = A(nc.sbuf_tensor("fols", [1, 256], F32))
        p.fol["vector"] = lambda E: E.tensor_copy(folv[0:1, 0:8], folv[0:1, 0:8])
        p.fol["scalar"] = lambda E: E.copy(fols[0:1, 0:8], fols[0:1, 0:8])

        def SPC(E):
            return E.tensor_copy(folv[0:1, 0:256], folv[0:1, 0:256])

        def R(name):
            a, b = {"pe2": (0, 2048), "kt2": (2048, 4096),
                    "qn2": (4096, 5120), "bias": (5120, 6144),
                    "sum": (4096, 5120), "rsum": (5120, 6144),
                    "crow": (0, 2048)}[name]
            return rows[0:1, a:b]

        ab1_scope = es.enter_context(contextlib.ExitStack())
        AB1 = ab1_scope.enter_context
        cos_own = AB1(nc.sbuf_tensor("cos_own", [64, TQ], F32))
        sin_own = AB1(nc.sbuf_tensor("sin_own", [64, TQ], F32))
        ckv_bf = AB1(nc.sbuf_tensor("ckv_bf", [128, 4 * S], BF16))
        kpe_aug = AB1(nc.sbuf_tensor("kpe_aug", [65, S], BF16))

        h1r_scope = es.enter_context(contextlib.ExitStack())
        h1_own = h1r_scope.enter_context(
            nc.sbuf_tensor("h1_own", [128, 16 * TQ], BF16, side="right"))

        akv_scope = es.enter_context(contextlib.ExitStack())
        AKV = akv_scope.enter_context
        cos_k = AKV(nc.sbuf_tensor("cos_k", [64, S], F32))
        sin_k = AKV(nc.sbuf_tensor("sin_k", [64, S], F32))
        kva_ckv_bf = AKV(nc.sbuf_tensor("kva_ckv_bf", [128, 4 * 2048], BF16))
        kva_pe_bf = AKV(nc.sbuf_tensor("kva_pe_bf", [128, 1024], BF16))
        kpe_bf_t = AKV(nc.sbuf_tensor("kpe_bf_t", [64, NT], BF16))
        xt_buf = [AKV(nc.sbuf_tensor(f"xt{i}", [128, 16 * NT], F32))
                  for i in range(2)]
        sq_buf = AKV(nc.sbuf_tensor("sq_buf", [128, 16 * NT], BF16))
        h1_tmp = AKV(nc.sbuf_tensor("h1_tmp", [128, 16 * NT], BF16))
        tmp_f = AKV(nc.sbuf_tensor("tmp_f", [128, 1024], F32))
        abrow = AKV(nc.sbuf_tensor("abrow", [1, 1024], F32))

        aps_scope = es.enter_context(contextlib.ExitStack())
        APS = aps_scope.enter_context
        ps_main = APS(nc.psum_tensor("ps_main", [128, 1024], F32))
        ps_bc = APS(nc.psum_tensor("ps_bc", [128, 1024], F32))
        ps_pe = APS(nc.psum_tensor("ps_pe", [64, 512], F32))
        ps_r1 = APS(nc.psum_tensor("ps_r1", [1, 512], F32))
        ps_r2 = APS(nc.psum_tensor("ps_r2", [1, 512], F32))

        pre_scope = contextlib.ExitStack()
        PRE = pre_scope.enter_context
        iota_row = PRE(nc.sbuf_tensor("iota_row", [1, 2048], F32))
        posk_row = PRE(nc.sbuf_tensor("posk_row", [1, 2048], F32))
        rope_t1 = PRE(nc.sbuf_tensor("rope_t1", [64, 1024], F32))
        rope_t2 = PRE(nc.sbuf_tensor("rope_t2", [64, 1024], F32))

        p.emit("gpsimd", lambda E: E.iota(iota_bc[:, :], pattern=[[1, 128]], base=0,
                                          channel_multiplier=0,
                                          allow_small_or_imprecise_dtypes=True))
        p.emit("gpsimd", lambda E: E.iota(pidx128[:, :], pattern=[[0, 1]], base=0,
                                          channel_multiplier=1,
                                          allow_small_or_imprecise_dtypes=True))
        p.emit("gpsimd",
               lambda E: E.iota(iota_row[0:1, :], pattern=[[1, 2048]], base=0,
                                channel_multiplier=0,
                                allow_small_or_imprecise_dtypes=True))
        t_gp = p.emit("gpsimd",
                      lambda E: E.iota(folv[0:1, 0:8], pattern=[[1, 8]], base=0,
                                       channel_multiplier=0,
                                       allow_small_or_imprecise_dtypes=True),
                      inc="gp")

        def _pre_a(E):
            E.store(pid_i[0:1, 0:1], nc.partition_id(engines=[ET.DVE]))
            E.memset(ones_bf[:, :], 1.0)
            E.memset(ones_f[:, :], 1.0)
            E.tensor_copy(pid_f[:, :], pid_i[:, :])
            E.memset(zerob[:, :], 0.0)
            E.memset(kpe_aug[64:65, :], -1.0)
            E.memset(scr64b[0:1, 0:8], 0.0)
        p.emit("vector", _pre_a)
        p.wait("vector", t_gp)

        def PP(vfn, prev=[None]):
            t = p.emit("vector", vfn, waits=[prev[0]], inc="dv")
            prev[0] = p.emit("scalar",
                             lambda E: E.copy(fols[0:1, 0:8], fols[0:1, 0:8]),
                             waits=[t], inc="ac")
            return prev[0]

        def _v1(E):
            E.tensor_scalar(ident_f[:, :], iota_bc[:, :], pidx128[:, 0:1], None,
                            ALU.is_equal)
            E.tensor_scalar(scr64a[:, :], iota_bc[0:64, 0:64],
                            pidx128[0:64, 0:1], None, ALU.subtract)
            E.tensor_scalar(scr64b[0:1, 1:2], pid_f[:, :], 1.0, None, ALU.is_equal)
            E.tensor_scalar(scr64b[0:1, 3:4], pid_f[:, :], 3.0, None, ALU.is_equal)
            E.tensor_scalar(scr64b[0:1, 5:6], pid_f[:, :], 5.0, None, ALU.is_equal)
            E.tensor_scalar(scr64b[0:1, 7:8], pid_f[:, :], 7.0, None, ALU.is_equal)
            return E.tensor_scalar(scr64d[0:64, 0:1], pidx128[0:64, 0:1], 32.0,
                                   None, ALU.is_ge)
        PP(_v1)

        def _v2(E):
            E.tensor_scalar(scr64c[:, :], scr64a[:, :], 32.0, None, ALU.is_equal)
            E.tensor_add(scr64b[0:1, 0:1], scr64b[0:1, 1:2], scr64b[0:1, 3:4])
            return E.tensor_scalar(scr64d[0:64, 1:2], scr64d[0:64, 0:1], 32.0,
                                   None, ALU.mult)
        PP(_v2)

        def _v3(E):
            E.tensor_scalar(scr64a[:, :], scr64a[:, :], -32.0, None, ALU.is_equal)
            E.tensor_add(scr64b[0:1, 2:3], scr64b[0:1, 5:6], scr64b[0:1, 7:8])
            return E.tensor_sub(f2pi[:, :], pidx128[0:64, 0:1],
                                scr64d[0:64, 1:2])
        PP(_v3)

        def _v4(E):
            E.tensor_sub(scr64d[:, :], scr64c[:, :], scr64a[:, :])
            return E.tensor_add(halfv[:, :], scr64b[0:1, 0:1], scr64b[0:1, 2:3])
        PP(_v4)

        def _v5(E):
            E.tensor_copy(rt_bf[:, :], scr64d[:, :])
            return E.tensor_scalar(halfv[:, :], halfv[:, :], 1024.0, None,
                                   ALU.mult)
        PP(_v5)

        def _v6(E):
            E.tensor_scalar(obase[:, :], halfv[:, :], -1.0, None, ALU.mult)
            return E.tensor_scalar(posk_row[0:1, 0:1024], iota_row[0:1, 0:1024],
                                   halfv[0:1, 0:1], None, ALU.add)
        PP(_v6)

        def _v7(E):
            return E.tensor_scalar(obase[:, :], obase[:, :], 1024.0, None,
                                   ALU.add)
        PP(_v7)

        def _v8(E):
            return E.tensor_scalar(posk_row[0:1, 1024:2048],
                                   iota_row[0:1, 0:1024], obase[0:1, 0:1], None,
                                   ALU.add)
        t_prev2 = PP(_v8)

        p.emit("scalar", lambda E: E.activation(f2pi[:, :], f2pi[:, :], AF.Exp,
                                                bias=zerob[0:64, 0:1],
                                                scale=ROPE_ES))
        p.emit("scalar", lambda E: E.copy(fols[0:1, 0:256], fols[0:1, 0:256]))
        t_f2pi = p.emit("scalar", lambda E: E.mul(f2pi[:, :], f2pi[:, :], INV2PI),
                        inc="ac")

        t_d = p.emit("sync", lambda E: E.dma_start(lnw_raw[0:12, :], qaln_pk),
                     inc="dlnw", inc_by=16)
        p.wait("tensor", t_d, t_prev2)
        t_pe = p.emit("tensor", lambda E: E.transpose(ps_main[0:128, 0:12],
                                                      lnw_raw[0:12, :],
                                                      ident_f[0:12, 0:12]),
                      inc="pe")
        p.wait("vector", t_pe)
        t_dv = p.emit("vector", lambda E: E.tensor_copy(qaln_sb[:, :],
                                                        ps_main[0:128, 0:12]),
                      inc="dv")
        t_d = p.emit("sync", lambda E: E.dma_start(lnw_raw[0:4, :], kvaln_pk),
                     waits=[t_dv], inc="dlnw", inc_by=16)
        p.wait("tensor", t_d)
        t_pe = p.emit("tensor", lambda E: E.transpose(ps_main[0:128, 16:20],
                                                      lnw_raw[0:4, :],
                                                      ident_f[0:4, 0:4]),
                      inc="pe")
        p.wait("vector", t_pe)
        t_kvln = p.emit("vector", lambda E: E.tensor_copy(kvaln_sb[:, :],
                                                          ps_main[0:128, 16:20]),
                        inc="dv")

        p.wait("tensor", t_prev2)
        t_tabv = None
        for tb in range(2):
            p.wait("tensor", t_tabv)
            for j in range(2):
                p.emit("tensor",
                       lambda E, tb=tb, j=j: E.matmul(
                           ps_bc[0:64, j * 512:(j + 1) * 512], ones_f[0:1, 0:64],
                           posk_row[0:1, tb * 1024 + j * 512:
                                    tb * 1024 + (j + 1) * 512],
                           start=True, stop=True),
                       inc="pe" if j == 1 else None)
            t_pb = ("pe", p.cnt["pe"])
            p.wait("vector", t_pb, t_f2pi)

            def _tab_v(E, tb=tb):
                E.tensor_scalar(rope_t1[:, :], ps_bc[0:64, :], f2pi[:, 0:1], None,
                                ALU.mult)
                SPC(E)
                E.tensor_scalar(rope_t2[:, :], rope_t1[:, :], MAGIC, -MAGIC,
                                ALU.add, ALU.add)
                SPC(E)
                E.tensor_sub(rope_t2[:, :], rope_t1[:, :], rope_t2[:, :])
                SPC(E)
                E.tensor_scalar(sin_k[:, tb * 1024:(tb + 1) * 1024],
                                rope_t2[:, :], TWOPI, None, ALU.mult)
                E.tensor_scalar(rope_t1[:, :], rope_t1[:, :], 0.25, None, ALU.add)
                SPC(E)
                E.tensor_scalar(rope_t2[:, :], rope_t1[:, :], MAGIC, -MAGIC,
                                ALU.add, ALU.add)
                SPC(E)
                E.tensor_sub(rope_t2[:, :], rope_t1[:, :], rope_t2[:, :])
                SPC(E)
                return E.tensor_scalar(cos_k[:, tb * 1024:(tb + 1) * 1024],
                                rope_t2[:, :], TWOPI, None, ALU.mult)
            t_tabv = p.emit("vector", _tab_v, inc="dv")
        p.wait("scalar", t_tabv)
        p.emit("scalar", lambda E: E.activation(sin_k[:, :], sin_k[:, :], AF.Sin,
                                                bias=zerob[0:64, 0:1], scale=1.0))
        t_tabs = p.emit("scalar", lambda E: E.activation(cos_k[:, :], cos_k[:, :],
                                                         AF.Sin,
                                                         bias=zerob[0:64, 0:1],
                                                         scale=1.0), inc="ac")
        p.wait("vector", t_tabs)

        def _tab_cp(E):
            E.tensor_copy(cos_own[:, :], cos_k[:, 0:TQ])
            return E.tensor_copy(sin_own[:, :], sin_k[:, 0:TQ])
        t_tabcp = p.emit("vector", _tab_cp, inc="dv")
        pre_scope.close()

        for mb in range(4):
            p.emit("gpsimd",
                   lambda E, mb=mb: E.dma_start(
                       kva_ckv_bf[:, mb * 2048:(mb + 1) * 2048], kva_ckv_pk[mb]),
                   inc="dwk", inc_by=16)
        p.emit("gpsimd", lambda E: E.dma_start(kva_pe_bf[:, :], kva_pe_pk),
               inc="dwk", inc_by=16)
        t_kva = ("dwk", p.cnt["dwk"])

        xT_r = xT.rearrange("(kc p) s -> p kc s", p=128)
        t_xload = [None, None]
        t_xfree = [None, None]
        t_sqfree = [None, None]

        def h1sl(tt, kc):
            if tt < 4:
                return h1_own[:, kc * TQ + tt * NT: kc * TQ + tt * NT + NT]
            return h1_tmp[:, kc * NT:(kc + 1) * NT]

        t_tile_vlast = None
        t_kv_prev = None
        t_pm_prev = None
        t_ck_prev = None
        t_stat = None
        for tt in range(NTILES):
            par = tt % 2
            sem = f"dxx{par}"
            t_xload[par] = p.emit(
                "sync",
                lambda E, par=par, tt=tt: E.dma_start(
                    xt_buf[par][:, :].rearrange("p (kc s) -> p kc s", kc=16),
                    xT_r[:, :, tt * NT:(tt + 1) * NT]),
                waits=[t_xfree[par], t_sqfree[par]], inc=sem, inc_by=16)

            p.wait("vector", t_xload[par],
                   t_tabcp if tt == 0 else t_kv_prev)
            t_cast = p.emit("vector",
                            lambda E, par=par: E.tensor_copy(h1_tmp[:, :],
                                                             xt_buf[par][:, :]),
                            inc="dv")
            p.wait("scalar", t_xload[par], t_pm_prev)
            t_sq = p.emit("scalar",
                          lambda E, par=par: E.activation(sq_buf[:, :],
                                                          xt_buf[par][:, :],
                                                          AF.Square,
                                                          bias=zerob[:, 0:1],
                                                          scale=1.0), inc="ac")
            t_sqfree[par] = t_sq
            p.wait("tensor", t_cast, t_sq,
                   t_tile_vlast if tt >= 1 else ("dv", p.cnt["dv"]))
            for kc in range(16):
                p.emit("tensor",
                       lambda E, kc=kc: E.matmul(ps_r1[0:1, 0:NT], ones_bf[:, 0:1],
                                                 h1_tmp[:, kc * NT:(kc + 1) * NT],
                                                 start=(kc == 0), stop=(kc == 15)))
            for kc in range(16):
                p.emit("tensor",
                       lambda E, kc=kc: E.matmul(ps_r2[0:1, 0:NT], ones_bf[:, 0:1],
                                                 sq_buf[:, kc * NT:(kc + 1) * NT],
                                                 start=(kc == 0), stop=(kc == 15)),
                       inc="pe" if kc == 15 else None)
            t_stat = ("pe", p.cnt["pe"])

            p.wait("vector", t_stat)

            def _ln_rows(E):
                E.tensor_scalar(abrow[0:1, 0:NT], ps_r1[0:1, 0:NT], 1.0 / D, None,
                                ALU.mult)
                E.tensor_scalar(abrow[0:1, NT:2 * NT], ps_r2[0:1, 0:NT], 1.0 / D,
                                None, ALU.mult)
                E.tensor_mul(tmp_f[0:1, 0:NT], abrow[0:1, 0:NT], abrow[0:1, 0:NT])
                SPC(E)
                E.tensor_sub(abrow[0:1, NT:2 * NT], abrow[0:1, NT:2 * NT],
                             tmp_f[0:1, 0:NT])
                SPC(E)
                return E.tensor_scalar(abrow[0:1, NT:2 * NT], abrow[0:1, NT:2 * NT],
                                EPS_LN, None, ALU.add)
            t_r1 = p.emit("vector", _ln_rows, inc="dv")
            p.wait("scalar", t_r1)
            p.emit("scalar", lambda E: E.activation(abrow[0:1, NT:2 * NT],
                                                    abrow[0:1, NT:2 * NT], AF.Ln,
                                                    bias=zerob[0:1, 0:1],
                                                    scale=1.0))
            t_r2 = p.emit("scalar", lambda E: E.activation(abrow[0:1, NT:2 * NT],
                                                           abrow[0:1, NT:2 * NT],
                                                           AF.Exp,
                                                           bias=zerob[0:1, 0:1],
                                                           scale=-0.5), inc="ac")
            p.wait("vector", t_r2)

            def _brow(E):
                E.tensor_mul(abrow[0:1, 0:NT], abrow[0:1, 0:NT],
                             abrow[0:1, NT:2 * NT])
                SPC(E)
                return E.tensor_scalar(abrow[0:1, 0:NT], abrow[0:1, 0:NT], -1.0, None,
                                ALU.mult)
            t_r3 = p.emit("vector", _brow, inc="dv")
            p.wait("tensor", t_r3)
            t_bc = p.emit("tensor",
                          lambda E: E.matmul(ps_bc[:, 0:512], ones_f[0:1, :],
                                             abrow[0:1, 0:512], start=True,
                                             stop=True), inc="pe")
            p.wait("vector", t_bc)

            def _h1_apply(E, par=par, tt=tt):
                ins = None
                for kc in range(16):
                    sl = kc % 2
                    E.tensor_mul(tmp_f[:, sl * NT:(sl + 1) * NT],
                                 xt_buf[par][:, kc * NT:(kc + 1) * NT],
                                 ps_bc[:, NT:2 * NT])
                    if kc > 0:
                        psl = (kc - 1) % 2
                        ins = E.tensor_tensor(h1sl(tt, kc - 1),
                                              tmp_f[:, psl * NT:(psl + 1) * NT],
                                              ps_bc[:, 0:NT], ALU.add)
                SPC(E)
                return E.tensor_tensor(h1sl(tt, 15), tmp_f[:, (15 % 2) * NT:
                                                           (15 % 2 + 1) * NT],
                                       ps_bc[:, 0:NT], ALU.add)
            p.emit("vector", _h1_apply, inc="dv")
            t_h1 = ("dv", p.cnt["dv"])
            t_xfree[par] = t_h1

            p.wait("tensor", t_h1, t_kva if tt == 0 else None, t_ck_prev)
            for cc in range(4):
                for kc in range(16):
                    p.emit("tensor",
                           lambda E, cc=cc, kc=kc, tt=tt: E.matmul(
                               ps_main[:, cc * NT:(cc + 1) * NT],
                               kva_ckv_bf[:, cc * 2048 + kc * 128:
                                          cc * 2048 + (kc + 1) * 128],
                               h1sl(tt, kc), start=(kc == 0), stop=(kc == 15)))
            for kc in range(16):
                p.emit("tensor",
                       lambda E, kc=kc, tt=tt: E.matmul(
                           ps_pe[0:64, 0:NT],
                           kva_pe_bf[:, kc * 64:(kc + 1) * 64],
                           h1sl(tt, kc), start=(kc == 0), stop=(kc == 15)),
                       inc="pe" if kc == 15 else None)
            t_kv = ("pe", p.cnt["pe"])
            t_kv_prev = t_kv

            p.wait("scalar", t_kv)
            for cc in range(4):
                p.emit("scalar",
                       lambda E, cc=cc: E.activation(
                           sq_buf[:, cc * NT:(cc + 1) * NT],
                           ps_main[:, cc * NT:(cc + 1) * NT], AF.Square,
                           bias=zerob[:, 0:1], scale=1.0),
                       inc="ac" if cc == 3 else None)
            t_csq = ("ac", p.cnt["ac"])
            p.wait("tensor", t_csq)
            for cc in range(4):
                p.emit("tensor",
                       lambda E, cc=cc: E.matmul(ps_r1[0:1, 0:NT], ones_bf[:, 0:1],
                                                 sq_buf[:, cc * NT:(cc + 1) * NT],
                                                 start=(cc == 0), stop=(cc == 3)),
                       inc="pe" if cc == 3 else None)
            t_cs2 = ("pe", p.cnt["pe"])
            p.wait("vector", t_cs2)
            t_rv = p.emit("vector",
                          lambda E: E.tensor_scalar(abrow[0:1, 0:NT],
                                                    ps_r1[0:1, 0:NT], 1.0 / KVL,
                                                    EPS_RMS, ALU.mult, ALU.add),
                          inc="dv")
            p.wait("scalar", t_rv)
            p.emit("scalar", lambda E: E.activation(abrow[0:1, 0:NT],
                                                    abrow[0:1, 0:NT], AF.Ln,
                                                    bias=zerob[0:1, 0:1],
                                                    scale=1.0))
            t_rs = p.emit("scalar", lambda E: E.activation(abrow[0:1, 0:NT],
                                                           abrow[0:1, 0:NT],
                                                           AF.Exp,
                                                           bias=zerob[0:1, 0:1],
                                                           scale=-0.5), inc="ac")
            p.wait("tensor", t_rs)
            t_rbc = p.emit("tensor",
                           lambda E: E.matmul(ps_bc[:, 512:768], ones_f[0:1, :],
                                              abrow[0:1, 0:NT], start=True,
                                              stop=True), inc="pe")
            p.wait("vector", t_rbc)

            def _ckv_norm(E, tt=tt):
                for cc in range(4):
                    sl = cc % 2
                    E.tensor_scalar(tmp_f[:, sl * NT:(sl + 1) * NT],
                                    ps_main[:, cc * NT:(cc + 1) * NT],
                                    kvaln_sb[:, cc:cc + 1], None, ALU.mult)
                    if cc > 0:
                        psl = (cc - 1) % 2
                        E.tensor_tensor(
                            ckv_bf[:, (cc - 1) * S + tt * NT:
                                   (cc - 1) * S + (tt + 1) * NT],
                            tmp_f[:, psl * NT:(psl + 1) * NT],
                            ps_bc[:, 512:768], ALU.mult)
                SPC(E)
                return E.tensor_tensor(
                    ckv_bf[:, 3 * S + tt * NT: 3 * S + (tt + 1) * NT],
                    tmp_f[:, NT:2 * NT], ps_bc[:, 512:768], ALU.mult)
            p.emit("vector", _ckv_norm, inc="dv")
            t_ck_prev = ("dv", p.cnt["dv"])

            t_kpev = p.emit("vector",
                            lambda E: E.tensor_copy(kpe_bf_t[:, :],
                                                    ps_pe[0:64, 0:NT]), inc="dv")
            p.wait("tensor", t_kpev)
            t_rx = p.emit("tensor",
                          lambda E: E.matmul(ps_pe[0:64, NT:2 * NT],
                                             rt_bf[:, :], kpe_bf_t[:, :],
                                             start=True, stop=True), inc="pe")
            p.wait("vector", t_rx)

            def _krope(E, tt=tt):
                E.tensor_mul(tmp_f[0:64, 0:NT], ps_pe[0:64, 0:NT],
                             cos_k[:, tt * NT:(tt + 1) * NT])
                E.tensor_mul(tmp_f[0:64, NT:2 * NT], ps_pe[0:64, NT:2 * NT],
                             sin_k[:, tt * NT:(tt + 1) * NT])
                SPC(E)
                return E.tensor_tensor(kpe_aug[0:64, tt * NT:(tt + 1) * NT],
                                tmp_f[0:64, 0:NT], tmp_f[0:64, NT:2 * NT], ALU.add)
            t_kr = p.emit("vector", _krope, inc="dv")
            p.wait("scalar", t_kr)
            t_ps = p.emit("scalar",
                          lambda E, tt=tt: E.activation(
                              sq_buf[0:64, 0:NT],
                              kpe_aug[0:64, tt * NT:(tt + 1) * NT], AF.Square,
                              bias=zerob[0:64, 0:1], scale=1.0), inc="ac")
            p.wait("tensor", t_ps)
            t_pm = p.emit("tensor",
                          lambda E: E.matmul(ps_r2[0:1, 256:256 + NT],
                                             ones_bf[0:64, 0:1],
                                             sq_buf[0:64, 0:NT],
                                             start=True, stop=True), inc="pe")
            t_pm_prev = t_pm
            p.wait("vector", t_pm)
            t_tile_vlast = p.emit(
                "vector",
                lambda E, tt=tt: E.tensor_copy(
                    R("pe2")[0:1, tt * NT:(tt + 1) * NT],
                    ps_r2[0:1, 256:256 + NT]),
                inc="dv")

        p.barrier()
        akv_scope.close()

        ab2_scope = es.enter_context(contextlib.ExitStack())
        AB2 = ab2_scope.enter_context
        qmid_bf = AB2(nc.sbuf_tensor("qmid_bf", [128, 12 * TQ], BF16))
        hw_buf = [AB2(nc.sbuf_tensor(f"hw_buf{i}", [128, 3328], BF16))
                  for i in range(2)]

        aq_scope = es.enter_context(contextlib.ExitStack())
        AQ = aq_scope.enter_context
        qa_buf = [AQ(nc.sbuf_tensor(f"qa_buf{i}", [128, 2048], BF16))
                  for i in range(2)]
        qmid_raw = AQ(nc.sbuf_tensor("qmid_raw", [128, 12 * TQ], F32))
        qsq_bf = AQ(nc.sbuf_tensor("qsq_bf", [128, TQ], BF16))
        qtmp_f = AQ(nc.sbuf_tensor("qtmp_f", [128, TQ], F32))
        qtmp2_f = AQ(nc.sbuf_tensor("qtmp2_f", [128, TQ], F32))
        qrow = AQ(nc.sbuf_tensor("qrow", [1, 1024], F32))

        t_qa_load = [None, None]
        t_qa_used = [None, None]
        t_qev = [None, None]
        t_qnmm_prev = None
        for mc in range(12):
            par = mc % 2
            PSQ = ps_main if par == 0 else ps_bc
            t_qa_load[par] = p.emit(
                "gpsimd",
                lambda E, mc=mc, par=par: E.dma_start(qa_buf[par][:, :], qa_pk[mc]),
                waits=[t_qa_used[par]], inc=f"dwq{par}", inc_by=16)
            p.wait("tensor", t_qa_load[par], t_qev[par],
                   ("dv", p.cnt["dv"]) if mc <= 1 else None)
            for kc in range(16):
                for nh in range(2):
                    p.emit("tensor",
                           lambda E, kc=kc, nh=nh, par=par, PSQ=PSQ: E.matmul(
                               PSQ[:, nh * 512:(nh + 1) * 512],
                               qa_buf[par][:, kc * 128:(kc + 1) * 128],
                               h1_own[:, kc * TQ + nh * 512:
                                      kc * TQ + (nh + 1) * 512],
                               start=(kc == 0), stop=(kc == 15)),
                           inc="pe" if (kc == 15 and nh == 1) else None)
            t_qa_used[par] = ("pe", p.cnt["pe"])
            p.wait("vector", t_qa_used[par])
            t_ev = p.emit("vector",
                          lambda E, mc=mc, PSQ=PSQ: E.tensor_copy(
                              qmid_raw[:, mc * TQ:(mc + 1) * TQ], PSQ[:, :]),
                          inc="dv")
            t_qev[par] = t_ev
            p.wait("scalar", t_ev, t_qnmm_prev if mc > 0 else t_stat)
            t_qsq = p.emit("scalar",
                           lambda E, mc=mc: E.activation(
                               qsq_bf[:, :], qmid_raw[:, mc * TQ:(mc + 1) * TQ],
                               AF.Square, bias=zerob[:, 0:1], scale=1.0), inc="ac")
            p.wait("tensor", t_qsq)
            for nh in range(2):
                p.emit("tensor",
                       lambda E, nh=nh, mc=mc: E.matmul(
                           (ps_r1 if nh == 0 else ps_r2)[0:1, 0:512],
                           ones_bf[:, 0:1], qsq_bf[:, nh * 512:(nh + 1) * 512],
                           start=(mc == 0), stop=(mc == 11)),
                       inc="pe" if nh == 1 else None)
            t_qnmm_prev = ("pe", p.cnt["pe"])
        t_qs2 = ("pe", p.cnt["pe"])
        p.wait("vector", t_qs2)

        def _qrms(E):
            E.tensor_scalar(qrow[0:1, 0:512], ps_r1[0:1, 0:512], 1.0 / QL,
                            EPS_RMS, ALU.mult, ALU.add)
            return E.tensor_scalar(qrow[0:1, 512:1024], ps_r2[0:1, 0:512], 1.0 / QL,
                            EPS_RMS, ALU.mult, ALU.add)
        t_qr = p.emit("vector", _qrms, inc="dv")
        p.wait("scalar", t_qr)
        p.emit("scalar", lambda E: E.activation(qrow[0:1, :], qrow[0:1, :],
                                                AF.Ln, bias=zerob[0:1, 0:1],
                                                scale=1.0))
        t_qr2 = p.emit("scalar", lambda E: E.activation(qrow[0:1, :],
                                                        qrow[0:1, :], AF.Exp,
                                                        bias=zerob[0:1, 0:1],
                                                        scale=-0.5), inc="ac")
        p.wait("tensor", t_qr2)
        for nh in range(2):
            p.emit("tensor",
                   lambda E, nh=nh: E.matmul(ps_bc[:, nh * 512:(nh + 1) * 512],
                                             ones_f[0:1, :],
                                             qrow[0:1, nh * 512:(nh + 1) * 512],
                                             start=True, stop=True),
                   inc="pe" if nh == 1 else None)
        t_qbc = ("pe", p.cnt["pe"])
        p.wait("vector", t_qbc)
        t_qapply = []

        def _q_apply_pair(E, mc):
            tmps = [qtmp_f, qtmp2_f]
            E.tensor_mul(tmps[mc % 2][:, :],
                         qmid_raw[:, mc * TQ:(mc + 1) * TQ], ps_bc[:, :])
            if mc > 0:
                return E.tensor_scalar(qmid_bf[:, (mc - 1) * TQ:mc * TQ],
                                       tmps[(mc - 1) % 2][:, :],
                                       qaln_sb[:, mc - 1:mc], None, ALU.mult)
            return SPC(E)

        for mc in range(12):
            t = p.emit("vector", lambda E, mc=mc: _q_apply_pair(E, mc), inc="dv")
            if mc > 0:
                t_qapply.append(t)
        t = p.emit("vector",
                   lambda E: (SPC(E),
                              E.tensor_scalar(qmid_bf[:, 11 * TQ:12 * TQ],
                                              qtmp2_f[:, :], qaln_sb[:, 11:12],
                                              None, ALU.mult))[-1], inc="dv")
        t_qapply.append(t)

        t_hw = [None] * H
        for h in range(2):
            t_hw[h] = p.emit("gpsimd",
                             lambda E, h=h: E.dma_start(hw_buf[h % 2][:, :],
                                                        hw_pk[h]),
                             inc=f"dwh{h % 2}", inc_by=16)

        if dbg:
            t_all = ("dv", p.cnt["dv"])
            p.wait("gpsimd", t_all)
            p.emit("gpsimd", lambda E: E.dma_start(dbg_out["d_qmid"], qmid_bf[:, :]),
                   inc="dso", inc_by=16)
            p.emit("gpsimd", lambda E: E.dma_start(dbg_out["d_ckv"], ckv_bf[:, :]),
                   inc="dso", inc_by=16)
            p.emit("gpsimd", lambda E: E.dma_start(dbg_out["d_kpe"], kpe_aug[:, :]),
                   inc="dso", inc_by=16)
            p.emit("gpsimd", lambda E: E.dma_start(dbg_out["d_pe2"], R("pe2")),
                   inc="dso", inc_by=16)
            p.emit("gpsimd", lambda E: E.dma_start(dbg_out["d_h1"], h1_own[:, :]),
                   inc="dso", inc_by=16)
            p.emit("gpsimd", lambda E: E.dma_start(dbg_out["d_cos"], cos_own[:, :]),
                   inc="dso", inc_by=16)
            p.emit("gpsimd", lambda E: E.dma_start(dbg_out["d_sin"], sin_own[:, :]),
                   inc="dso", inc_by=16)
            p.emit("gpsimd", lambda E: E.dma_start(dbg_out["d_rt"], rt_bf[:, :]),
                   inc="dso", inc_by=16)
            p.emit("gpsimd", lambda E: E.dma_start(dbg_out["d_lnw"][:, 0:12],
                                                   qaln_sb[:, :]),
                   inc="dso", inc_by=16)
            p.emit("gpsimd", lambda E: E.dma_start(dbg_out["d_lnw"][:, 12:16],
                                                   kvaln_sb[:, :]),
                   inc="dso", inc_by=16)
            p.emit("gpsimd", lambda E: E.dma_start(dbg_out["d_qraw"],
                                                   qmid_raw[:, 0:TQ]),
                   inc="dso", inc_by=16)
            p.emit("gpsimd", lambda E: E.dma_start(dbg_out["d_f2pi"], f2pi[:, :]),
                   inc="dso", inc_by=16)
            p.emit("gpsimd", lambda E: E.dma_start(dbg_out["d_qabuf"],
                                                   qa_buf[0][:, :]),
                   inc="dso", inc_by=16)

        p.enabled = phases >= 2
        aq_scope.close()
        aps_scope.close()
        h1r_scope.close()

        h2_bf = A(nc.sbuf_tensor("h2_bf", [128, 16 * TQ], BF16, side="right"))
        cat_scope = es.enter_context(contextlib.ExitStack())
        attn_cat = cat_scope.enter_context(
            nc.sbuf_tensor("attn_cat", [128, H * TQ], BF16, side="right"))

        b_scope = es.enter_context(contextlib.ExitStack())
        BS = b_scope.enter_context
        et_buf = [BS(nc.sbuf_tensor(f"et{i}", [128, TQ], BF16)) for i in range(2)]
        qn_bf = BS(nc.sbuf_tensor("qn_bf", [128, TQ], BF16))
        qpe_bf = BS(nc.sbuf_tensor("qpe_bf", [64, TQ], BF16))
        q_aug = BS(nc.sbuf_tensor("q_aug", [65, TQ], BF16))
        kn_bf = BS(nc.sbuf_tensor("kn_bf", [128, S], BF16))
        vt_bf = BS(nc.sbuf_tensor("vt_bf", [128, 16 * 128], BF16))
        sqb = BS(nc.sbuf_tensor("sqb", [128, 4096], BF16))
        rbc_sb = BS(nc.sbuf_tensor("rbc_sb", [128, TQ], F32))
        btmp1 = BS(nc.sbuf_tensor("btmp1", [64, TQ], F32))

        bps_scope = es.enter_context(contextlib.ExitStack())
        BPS = bps_scope.enter_context
        ps_proj = BPS(nc.psum_tensor("ps_proj", [128, 1024], F32))
        ps_sA = BPS(nc.psum_tensor("ps_sA", [128, 1024], F32))
        ps_sB = BPS(nc.psum_tensor("ps_sB", [128, 1024], F32))
        ps_av = BPS(nc.psum_tensor("ps_av", [128, 1024], F32))

        t_cat_prev = None
        t_hw_used = [None, None]
        t_rr_prev = None
        t_rbv_prev = None
        t_av_prev = None
        for h in range(H):
            par = h % 2
            HB = hw_buf[par]
            if h >= 2:
                t_hw[h] = p.emit("gpsimd",
                                 lambda E, h=h: E.dma_start(hw_buf[h % 2][:, :],
                                                            hw_pk[h]),
                                 waits=[t_hw_used[par]], inc=f"dwh{par}", inc_by=16)

            p.wait("tensor", t_hw[h], t_rr_prev)
            for kc in range(12):
                if h == 0:
                    p.wait("tensor", t_qapply[kc])
                for nh in range(2):
                    p.emit("tensor",
                           lambda E, kc=kc, nh=nh, HB=HB: E.matmul(
                               ps_proj[:, nh * 512:(nh + 1) * 512],
                               HB[:, kc * 128:(kc + 1) * 128],
                               qmid_bf[:, kc * TQ + nh * 512:kc * TQ + (nh + 1) * 512],
                               start=(kc == 0), stop=(kc == 11)),
                           inc="pe" if (kc == 11 and nh == 1) else None)
            t_qn = ("pe", p.cnt["pe"])
            p.wait("vector", t_qn)
            t_qnev = p.emit("vector",
                            lambda E: E.tensor_copy(qn_bf[:, :], ps_proj[:, :]),
                            inc="dv")
            p.wait("tensor", t_rbv_prev)
            for kc in range(12):
                for nh in range(2):
                    p.emit("tensor",
                           lambda E, kc=kc, nh=nh, HB=HB: E.matmul(
                               ps_sA[0:64, nh * 512:(nh + 1) * 512],
                               HB[:, 1536 + kc * 64:1536 + (kc + 1) * 64],
                               qmid_bf[:, kc * TQ + nh * 512:kc * TQ + (nh + 1) * 512],
                               start=(kc == 0), stop=(kc == 11)),
                           inc="pe" if (kc == 11 and nh == 1) else None)
            t_qpe = ("pe", p.cnt["pe"])
            p.wait("vector", t_qpe)
            t_qpev = p.emit("vector",
                            lambda E: E.tensor_copy(qpe_bf[:, :], ps_sA[0:64, :]),
                            inc="dv")
            p.wait("tensor", t_qpev)
            for nh in range(2):
                p.emit("tensor",
                       lambda E, nh=nh: E.matmul(
                           ps_sB[0:64, nh * 512:(nh + 1) * 512], rt_bf[:, :],
                           qpe_bf[:, nh * 512:(nh + 1) * 512],
                           start=True, stop=True),
                       inc="pe" if nh == 1 else None)
            t_rx = ("pe", p.cnt["pe"])
            p.wait("vector", t_rx)

            def _qrope(E):
                E.tensor_mul(q_aug[0:64, :], ps_sB[0:64, :], sin_own[:, :])
                E.tensor_mul(btmp1[:, :], ps_sA[0:64, :], cos_own[:, :])
                SPC(E)
                return E.tensor_tensor(q_aug[0:64, :], q_aug[0:64, :], btmp1[:, :],
                                ALU.add)
            t_qrope = p.emit("vector", _qrope, inc="dv")

            p.wait("tensor", t_cat_prev)
            for half, PSK in ((0, ps_av), (1, ps_proj)):
                if half == 1:
                    p.wait("tensor", t_qnev)
                for kc in range(4):
                    for nh in range(2):
                        p.emit("tensor",
                               lambda E, kc=kc, nh=nh, half=half, PSK=PSK, HB=HB:
                               E.matmul(
                                   PSK[:, nh * 512:(nh + 1) * 512],
                                   HB[:, 2304 + kc * 128:2304 + (kc + 1) * 128],
                                   ckv_bf[:, kc * S + half * TQ + nh * 512:
                                          kc * S + half * TQ + (nh + 1) * 512],
                                   start=(kc == 0), stop=(kc == 3)),
                               inc="pe" if (kc == 3 and nh == 1) else None)
            t_k = ("pe", p.cnt["pe"])
            p.wait("vector", t_k)
            t_kev = p.emit("vector",
                           lambda E: (E.tensor_copy(kn_bf[:, 0:TQ], ps_av[:, :]),
                                      E.tensor_copy(kn_bf[:, TQ:S],
                                                    ps_proj[:, :]))[-1],
                           inc="dv")

            p.wait("scalar", t_kev, t_qrope)
            t_sqq = p.emit("scalar",
                           lambda E: (E.activation(sqb[:, 2048:3072], qn_bf[:, :],
                                                   AF.Square, bias=zerob[:, 0:1],
                                                   scale=1.0),
                                      E.activation(sqb[0:64, 3072:4096],
                                                   q_aug[0:64, :], AF.Square,
                                                   bias=zerob[0:64, 0:1],
                                                   scale=1.0))[-1], inc="ac")
            t_sqk = p.emit("vector",
                           lambda E: (SPC(E),
                                      E.tensor_mul(sqb[:, 0:S], kn_bf[:, :],
                                                   kn_bf[:, :]))[-1], inc="dv")
            p.wait("tensor", t_sqq)
            for nh in range(2):
                p.emit("tensor",
                       lambda E, nh=nh: E.matmul(
                           ps_sA[0:1, nh * 512:(nh + 1) * 512], ones_bf[:, 0:1],
                           sqb[:, 2048 + nh * 512:2048 + (nh + 1) * 512],
                           start=True, stop=True))
            for nh in range(2):
                p.emit("tensor",
                       lambda E, nh=nh: E.matmul(
                           ps_sA[0:1, nh * 512:(nh + 1) * 512],
                           ones_bf[0:64, 0:1],
                           sqb[0:64, 3072 + nh * 512:3072 + (nh + 1) * 512],
                           start=False, stop=True),
                       inc="pe" if nh == 1 else None)
            t_qn2 = ("pe", p.cnt["pe"])
            p.wait("tensor", t_sqk)
            for half, PSK in ((0, ps_av), (1, ps_proj)):
                for nh in range(2):
                    p.emit("tensor",
                           lambda E, half=half, nh=nh, PSK=PSK: E.matmul(
                               PSK[0:1, nh * 512:(nh + 1) * 512], ones_bf[:, 0:1],
                               sqb[:, half * TQ + nh * 512:
                                   half * TQ + (nh + 1) * 512],
                               start=True, stop=True),
                           inc="pe" if (half == 1 and nh == 1) else None)
            t_kn2 = ("pe", p.cnt["pe"])
            p.wait("vector", t_kn2)

            def _stats2(E):
                E.tensor_copy(R("qn2"), ps_sA[0:1, :])
                E.tensor_copy(R("kt2")[0:1, 0:TQ], ps_av[0:1, :])
                E.tensor_copy(R("kt2")[0:1, TQ:S], ps_proj[0:1, :])
                SPC(E)
                E.tensor_add(R("kt2"), R("kt2"), R("pe2"))
                SPC(E)
                SPC(E)
                E.tensor_reduce(km2[0:1, 0:1], R("kt2"), mybir.AxisListType.X,
                                ALU.max)
                SPC(E)
                return SPC(E)
            t_b1 = p.emit("vector", _stats2, inc="dv")
            p.wait("scalar", t_b1)
            p.emit("scalar", lambda E: E.activation(R("bias"), R("qn2"), AF.Ln,
                                                    bias=zerob[0:1, 0:1],
                                                    scale=km2[0:1, 0:1]))
            p.emit("scalar", lambda E: E.copy(fols[0:1, 0:256], fols[0:1, 0:256]))
            t_b3 = p.emit("scalar", lambda E: E.activation(R("bias"), R("bias"),
                                                           AF.Exp,
                                                           bias=zerob[0:1, 0:1],
                                                           scale=0.5), inc="ac")
            t_b4 = p.emit("vector", SPC, waits=[t_b3], inc="dv")
            t_bld = p.emit("gpsimd", lambda E: E.dma_start(q_aug[64:65, :],
                                                           R("bias")),
                           waits=[t_b4], inc="dwb", inc_by=16)
            if dbg and h == 0:
                p.wait("gpsimd", t_bld)
                p.emit("gpsimd", lambda E: E.dma_start(dbg_out["d_brow0"],
                                                       brow_scr),
                       inc="dso", inc_by=16)

            p.wait("tensor", t_b1)
            t_vt_ev_prev = [None, None]
            for tc in range(16):
                vpar = tc % 2
                po = vpar * 512
                p.wait("tensor", t_vt_ev_prev[vpar])
                for cc in range(4):
                    p.emit("tensor",
                           lambda E, tc=tc, cc=cc, po=po, HB=HB: E.matmul(
                               ps_proj[:, po:po + 128],
                               ckv_bf[:, cc * S + tc * 128:cc * S + (tc + 1) * 128],
                               HB[:, 2816 + cc * 128:2816 + (cc + 1) * 128],
                               start=(cc == 0), stop=(cc == 3)),
                           inc="pe" if cc == 3 else None)
                t_vt = ("pe", p.cnt["pe"])
                p.wait("vector", t_vt)
                t_vt_ev_prev[vpar] = p.emit(
                    "vector",
                    lambda E, tc=tc, po=po: E.tensor_copy(
                        vt_bf[:, tc * 128:(tc + 1) * 128], ps_proj[:, po:po + 128]),
                    inc="dv")

            t_expt = [None] * 16
            t_savc = [t_av_prev, t_av_prev]
            p.wait("tensor", t_bld)

            def _emit_scores(tc):
                spar = tc % 2
                PSS = ps_sA if spar == 0 else ps_sB
                if tc >= 2:
                    p.wait("tensor", (t_expt[tc - 2][0], t_expt[tc - 2][1] + 1))
                for nh in range(2):
                    p.emit("tensor",
                           lambda E, tc=tc, nh=nh, PSS=PSS: E.matmul(
                               PSS[:, nh * 512:(nh + 1) * 512],
                               kn_bf[:, tc * 128:(tc + 1) * 128],
                               qn_bf[:, nh * 512:(nh + 1) * 512],
                               start=True, stop=False))
                    p.emit("tensor",
                           lambda E, tc=tc, nh=nh, PSS=PSS: E.matmul(
                               PSS[:, nh * 512:(nh + 1) * 512],
                               kpe_aug[:, tc * 128:(tc + 1) * 128],
                               q_aug[:, nh * 512:(nh + 1) * 512],
                               start=False, stop=True),
                           inc="pe" if nh == 1 else None)
                t_sc = ("pe", p.cnt["pe"])
                p.wait("scalar", t_sc, t_savc[spar])
                PSS2 = PSS
                t_expt[tc] = p.emit("scalar",
                                    lambda E, spar=spar, PSS=PSS2:
                                    E.activation(et_buf[spar][:, :], PSS[:, :],
                                                 AF.Exp, bias=zerob[:, 0:1],
                                                 scale=SCALE),
                                    inc="ac", fol=False)

            def _emit_sa(tc):
                spar = tc % 2
                p.wait("tensor", (t_expt[tc][0], t_expt[tc][1] + 1))
                for nh in range(2):
                    p.emit("tensor",
                           lambda E, nh=nh, spar=spar, tc=tc: E.matmul(
                               ps_proj[0:1, nh * 512:(nh + 1) * 512],
                               ones_bf[:, 0:1],
                               et_buf[spar][:, nh * 512:(nh + 1) * 512],
                               start=(tc == 0), stop=(tc == 15)))
                for nh in range(2):
                    p.emit("tensor",
                           lambda E, tc=tc, nh=nh, spar=spar: E.matmul(
                               ps_av[:, nh * 512:(nh + 1) * 512],
                               vt_bf[:, tc * 128:(tc + 1) * 128],
                               et_buf[spar][:, nh * 512:(nh + 1) * 512],
                               start=(tc == 0), stop=(tc == 15)),
                           inc="pe" if nh == 1 else None)
                t_savc[spar] = ("pe", p.cnt["pe"])

            for tc in range(16):
                _emit_scores(tc)
                if tc >= 1:
                    _emit_sa(tc - 1)
            p.emit("scalar", lambda E: E.copy(fols[0:1, 0:8], fols[0:1, 0:8]),
                   inc="ac", fol=False)
            _emit_sa(15)

            t_sums = t_savc[1]
            p.wait("vector", t_sums)

            def _rsum(E):
                E.tensor_copy(R("sum"), ps_proj[0:1, :])
                SPC(E)
                return E.reciprocal(R("rsum"), R("sum"))
            t_rr = p.emit("vector", _rsum, inc="dv")
            t_rr_prev2 = t_rr
            p.wait("tensor", t_rr)
            for nh in range(2):
                p.emit("tensor",
                       lambda E, nh=nh: E.matmul(
                           ps_sA[:, nh * 512:(nh + 1) * 512], ones_f[0:1, :],
                           R("rsum")[0:1, nh * 512:(nh + 1) * 512],
                           start=True, stop=True),
                       inc="pe" if nh == 1 else None)
            t_rbc2 = ("pe", p.cnt["pe"])
            p.wait("vector", t_rbc2)
            t_rbv = p.emit("vector",
                           lambda E: E.tensor_copy(rbc_sb[:, :], ps_sA[:, :]),
                           inc="dv")
            t_cat_prev = p.emit("vector",
                                lambda E, h=h: E.tensor_mul(
                                    attn_cat[:, h * TQ:(h + 1) * TQ],
                                    ps_av[:, :], rbc_sb[:, :]),
                                inc="dv")
            t_hw_used[par] = t_savc[1]
            t_rr_prev = t_rr_prev2
            t_rbv_prev = t_rbv
            t_av_prev = t_savc[1]
            if dbg and h == 0:
                p.wait("gpsimd", t_cat_prev)
                for nm, src_ap in [("d_qn0", qn_bf[:, :]), ("d_kn0", kn_bf[:, :]),
                                   ("d_qaug0", q_aug[:, :]), ("d_vt0", vt_bf[:, :]),
                                   ("d_et0", et_buf[1][:, :]),
                                   ("d_rbc0", rbc_sb[:, :])]:
                    p.emit("gpsimd",
                           lambda E, nm=nm, src_ap=src_ap: E.dma_start(
                               dbg_out[nm], src_ap), inc="dso", inc_by=16)

        if dbg:
            p.wait("gpsimd", t_cat_prev)
            p.emit("gpsimd", lambda E: E.dma_start(dbg_out["d_cat"], attn_cat[:, :]),
                   inc="dso", inc_by=16)

        p.enabled = phases >= 3
        p.barrier()
        b_scope.close()
        bps_scope.close()
        ab2_scope.close()
        ab1_scope.close()

        cd_scope = es.enter_context(contextlib.ExitStack())
        CD = cd_scope.enter_context
        x1r_buf = [CD(nc.sbuf_tensor(f"x1r{i}", [128, TQ], F32)) for i in range(2)]
        o_buf = [CD(nc.sbuf_tensor(f"o_buf{i}", [128, 2048], BF16))
                 for i in range(2)]
        g_buf = [CD(nc.sbuf_tensor(f"g_buf{i}", [128, 2048], BF16))
                 for i in range(2)]
        u_buf = [CD(nc.sbuf_tensor(f"u_buf{i}", [128, 2048], BF16))
                 for i in range(2)]

        c_scope = es.enter_context(contextlib.ExitStack())
        CS = c_scope.enter_context
        xc_buf = [CS(nc.sbuf_tensor(f"xc{i}", [128, TQ], F32)) for i in range(2)]
        x1_buf = [CS(nc.sbuf_tensor(f"x1b{i}", [128, TQ], F32)) for i in range(2)]
        cb_bf = CS(nc.sbuf_tensor("cb_bf", [128, TQ], BF16))
        csq_bf = CS(nc.sbuf_tensor("csq_bf", [128, TQ], BF16))
        ctmp = CS(nc.sbuf_tensor("ctmp", [128, TQ], F32))

        cps_scope = es.enter_context(contextlib.ExitStack())
        CPS = cps_scope.enter_context
        ps_oA = CPS(nc.psum_tensor("ps_oA", [128, 1024], F32))
        ps_oB = CPS(nc.psum_tensor("ps_oB", [128, 1024], F32))
        ps_c1 = CPS(nc.psum_tensor("ps_c1", [1, 512], F32))
        ps_c2 = CPS(nc.psum_tensor("ps_c2", [1, 512], F32))
        ps_c3 = CPS(nc.psum_tensor("ps_c3", [1, 512], F32))
        ps_c4 = CPS(nc.psum_tensor("ps_c4", [1, 512], F32))

        t_o_load = [None, None]
        t_o_used = [None, None]
        t_xc_load = [None, None]
        t_x1_st = [None, None]
        t_oev = [None, None]
        t_cst_prev = None
        for mc in range(16):
            par = mc % 2
            PSO = ps_oA if par == 0 else ps_oB
            t_o_load[par] = p.emit("gpsimd",
                                   lambda E, mc=mc, par=par: E.dma_start(
                                       o_buf[par][:, :], o_pk[mc]),
                                   waits=[t_o_used[par]],
                                   inc=f"dwo{par}", inc_by=16)
            t_xc_load[par] = p.emit("sync",
                                    lambda E, mc=mc, par=par: E.dma_start(
                                        xc_buf[par][:, :], xT_r[:, mc, 0:TQ]),
                                    waits=[t_oev[par]], inc=f"dxc{par}", inc_by=16)
            p.wait("tensor", t_o_load[par], t_oev[par])
            for kc in range(16):
                for nh in range(2):
                    p.emit("tensor",
                           lambda E, kc=kc, nh=nh, par=par, PSO=PSO: E.matmul(
                               PSO[:, nh * 512:(nh + 1) * 512],
                               o_buf[par][:, kc * 128:(kc + 1) * 128],
                               attn_cat[:, kc * TQ + nh * 512:
                                        kc * TQ + (nh + 1) * 512],
                               start=(kc == 0), stop=(kc == 15)),
                           inc="pe" if (kc == 15 and nh == 1) else None)
            t_o_used[par] = ("pe", p.cnt["pe"])
            p.wait("vector", t_o_used[par], t_xc_load[par], t_x1_st[par])
            t_x1 = p.emit("vector",
                          lambda E, par=par, PSO=PSO: E.tensor_tensor(
                              x1_buf[par][:, :], PSO[:, :], xc_buf[par][:, :],
                              ALU.add), inc="dv")
            t_oev[par] = t_x1
            t_x1_st[par] = p.emit("sync",
                                  lambda E, mc=mc, par=par: E.dma_start(
                                      x1_scr[mc], x1_buf[par][:, :]),
                                  waits=[t_x1], inc=f"dx1w{par}", inc_by=16)
            p.wait("vector", t_cst_prev)
            t_cb = p.emit("vector",
                          lambda E, par=par: E.tensor_copy(cb_bf[:, :],
                                                           x1_buf[par][:, :]),
                          inc="dv")
            p.wait("scalar", t_x1, t_cst_prev)
            t_cs = p.emit("scalar",
                          lambda E, par=par: E.activation(csq_bf[:, :],
                                                          x1_buf[par][:, :],
                                                          AF.Square,
                                                          bias=zerob[:, 0:1],
                                                          scale=1.0), inc="ac")
            p.wait("tensor", t_cb, t_cs)
            for nh in range(2):
                p.emit("tensor",
                       lambda E, nh=nh, mc=mc: E.matmul(
                           (ps_c1 if nh == 0 else ps_c2)[0:1, 0:512],
                           ones_bf[:, 0:1], cb_bf[:, nh * 512:(nh + 1) * 512],
                           start=(mc == 0), stop=(mc == 15)))
            for nh in range(2):
                p.emit("tensor",
                       lambda E, nh=nh, mc=mc: E.matmul(
                           (ps_c3 if nh == 0 else ps_c4)[0:1, 0:512],
                           ones_bf[:, 0:1], csq_bf[:, nh * 512:(nh + 1) * 512],
                           start=(mc == 0), stop=(mc == 15)),
                       inc="pe" if nh == 1 else None)
            t_cst_prev = ("pe", p.cnt["pe"])
        t_cstat = ("pe", p.cnt["pe"])

        p.wait("vector", t_cstat)

        def _ln2rows(E):
            cr = R("crow")
            E.tensor_scalar(cr[0:1, 0:512], ps_c1[0:1, :], 1.0 / D, None, ALU.mult)
            E.tensor_scalar(cr[0:1, 512:1024], ps_c2[0:1, :], 1.0 / D, None,
                            ALU.mult)
            E.tensor_scalar(cr[0:1, 1024:1536], ps_c3[0:1, :], 1.0 / D, None,
                            ALU.mult)
            E.tensor_scalar(cr[0:1, 1536:2048], ps_c4[0:1, :], 1.0 / D, None,
                            ALU.mult)
            E.tensor_mul(ctmp[0:1, 0:1024], cr[0:1, 0:1024], cr[0:1, 0:1024])
            SPC(E)
            E.tensor_sub(cr[0:1, 1024:2048], cr[0:1, 1024:2048],
                         ctmp[0:1, 0:1024])
            SPC(E)
            return E.tensor_scalar(cr[0:1, 1024:2048], cr[0:1, 1024:2048], EPS_LN,
                            None, ALU.add)
        t_c1 = p.emit("vector", _ln2rows, inc="dv")
        p.wait("scalar", t_c1)
        p.emit("scalar", lambda E: E.activation(R("crow")[0:1, 1024:2048],
                                                R("crow")[0:1, 1024:2048], AF.Ln,
                                                bias=zerob[0:1, 0:1], scale=1.0))
        t_c2 = p.emit("scalar", lambda E: E.activation(R("crow")[0:1, 1024:2048],
                                                       R("crow")[0:1, 1024:2048],
                                                       AF.Exp,
                                                       bias=zerob[0:1, 0:1],
                                                       scale=-0.5), inc="ac")
        p.wait("vector", t_c2)

        def _c_b(E):
            cr = R("crow")
            E.tensor_mul(cr[0:1, 0:1024], cr[0:1, 0:1024], cr[0:1, 1024:2048])
            SPC(E)
            return E.tensor_scalar(cr[0:1, 0:1024], cr[0:1, 0:1024], -1.0, None,
                            ALU.mult)
        t_c3 = p.emit("vector", _c_b, inc="dv")
        p.wait("tensor", t_c3)
        for nh in range(2):
            p.emit("tensor",
                   lambda E, nh=nh: E.matmul(ps_oA[:, nh * 512:(nh + 1) * 512],
                                             ones_f[0:1, :],
                                             R("crow")[0:1, 1024 + nh * 512:
                                                       1024 + (nh + 1) * 512],
                                             start=True, stop=True))
        for nh in range(2):
            p.emit("tensor",
                   lambda E, nh=nh: E.matmul(ps_oB[:, nh * 512:(nh + 1) * 512],
                                             ones_f[0:1, :],
                                             R("crow")[0:1, nh * 512:
                                                       (nh + 1) * 512],
                                             start=True, stop=True),
                   inc="pe" if nh == 1 else None)
        t_cbc = ("pe", p.cnt["pe"])

        t_x1r = [None, None]
        t_h2v = [None, None]
        t_h2chunk = []
        for mc in range(16):
            par = mc % 2
            t_x1r[par] = p.emit("sync",
                                lambda E, mc=mc, par=par: E.dma_start(
                                    x1r_buf[par][:, :], x1_scr[mc]),
                                waits=[("dx1w0", p.cnt["dx1w0"]),
                                       ("dx1w1", p.cnt["dx1w1"]), t_h2v[par]],
                                inc=f"dx1r{par}", inc_by=16)
            p.wait("vector", t_x1r[par], t_cbc if mc == 0 else None)
            t_h2v[par] = p.emit(
                "vector",
                lambda E, mc=mc, par=par: (
                    E.tensor_mul(ctmp[:, :], x1r_buf[par][:, :], ps_oA[:, :]),
                    SPC(E),
                    E.tensor_tensor(h2_bf[:, mc * TQ:(mc + 1) * TQ], ctmp[:, :],
                                    ps_oB[:, :], ALU.add))[-1],
                inc="dv")
            t_h2chunk.append(t_h2v[par])

        if dbg:
            p.wait("gpsimd", t_h2v[1])
            p.emit("gpsimd", lambda E: E.dma_start(dbg_out["d_h2"], h2_bf[:, :]),
                   inc="dso", inc_by=16)

        p.enabled = phases >= 4
        if p.enabled:
            t_h2_all = ("dv", p.cnt["dv"])
            p.wait("gpsimd", t_h2_all)
            p.wait("sync", t_h2_all)
        cat_scope.close()
        c_scope.close()
        cps_scope.close()

        NOCT = 8
        OCT = 64 // NOCT
        d_scope = es.enter_context(contextlib.ExitStack())
        DS = d_scope.enter_context
        gated = DS(nc.sbuf_tensor("gated", [128, OCT * TQ], BF16))
        silu_bf = [DS(nc.sbuf_tensor(f"silu{i}", [128, TQ], BF16)) for i in range(2)]
        d_buf = [DS(nc.sbuf_tensor(f"d_buf{i}", [128, OCT * 128], BF16))
                 for i in range(2)]
        dtmp = [DS(nc.sbuf_tensor(f"dtmp{i}", [128, TQ], F32)) for i in range(2)]
        ffn_acc = DS(nc.sbuf_tensor("ffn_acc", [128, 16 * TQ], F32, side="right"))

        dps_scope = es.enter_context(contextlib.ExitStack())
        DPS = dps_scope.enter_context
        ps_d = [DPS(nc.psum_tensor(f"ps_d{i}", [128, 1024], F32))
                for i in range(2)]
        ps_g = DPS(nc.psum_tensor("ps_g", [128, 1024], F32))
        ps_u = DPS(nc.psum_tensor("ps_u", [128, 1024], F32))

        t_g_load = [None, None]
        t_u_load = [None, None]
        t_d_load = [None, None]
        t_d_used = [None, None]
        t_sil_prev = None
        t_mul_prev = None
        t_gated_wr = [None, None]
        t_gmm_prev = [None, None]
        t_umm_prev = [None, None]
        t_dp_done = None
        t_dev = [None, None]
        t_yst = [None, None]
        t_x1r2 = [None, None]

        for oo in range(NOCT):
            for hl in range(OCT):
                hc = oo * OCT + hl
                par = hc % 2
                t_g_load[par] = p.emit("gpsimd",
                                       lambda E, hc=hc, par=par: E.dma_start(
                                           g_buf[par][:, :], g_pk[hc]),
                                       waits=[t_gmm_prev[par]],
                                       inc=f"dwg{par}", inc_by=16)
                t_u_load[par] = p.emit("gpsimd",
                                       lambda E, hc=hc, par=par: E.dma_start(
                                           u_buf[par][:, :], u_pk[hc]),
                                       waits=[t_umm_prev[par]],
                                       inc=f"dwu{par}", inc_by=16)
                p.wait("tensor", t_g_load[par], t_sil_prev)
                for kc in range(16):
                    if hc == 0:
                        p.wait("tensor", t_h2chunk[kc])
                    for nh in range(2):
                        p.emit("tensor",
                               lambda E, kc=kc, nh=nh, par=par: E.matmul(
                                   ps_g[:, nh * 512:(nh + 1) * 512],
                                   g_buf[par][:, kc * 128:(kc + 1) * 128],
                                   h2_bf[:, kc * TQ + nh * 512:
                                         kc * TQ + (nh + 1) * 512],
                                   start=(kc == 0), stop=(kc == 15)),
                               inc="pe" if (kc == 15 and nh == 1) else None)
                t_gmm = ("pe", p.cnt["pe"])
                t_gmm_prev[par] = t_gmm
                p.wait("tensor", t_u_load[par], t_mul_prev)
                for kc in range(16):
                    for nh in range(2):
                        p.emit("tensor",
                               lambda E, kc=kc, nh=nh, par=par: E.matmul(
                                   ps_u[:, nh * 512:(nh + 1) * 512],
                                   u_buf[par][:, kc * 128:(kc + 1) * 128],
                                   h2_bf[:, kc * TQ + nh * 512:
                                         kc * TQ + (nh + 1) * 512],
                                   start=(kc == 0), stop=(kc == 15)),
                               inc="pe" if (kc == 15 and nh == 1) else None)
                t_umm = ("pe", p.cnt["pe"])
                t_umm_prev[par] = t_umm
                p.wait("scalar", t_gmm, t_gated_wr[par])
                t_sil = p.emit("scalar",
                               lambda E, par=par: E.activation(
                                   silu_bf[par][:, :], ps_g[:, :], AF.Silu,
                                   bias=zerob[:, 0:1], scale=1.0), inc="ac")
                t_sil_prev = t_sil
                p.wait("vector", t_sil, t_umm,
                       t_dp_done if (oo > 0 and hl < 2) else None)
                t_gated_wr[par] = p.emit(
                    "vector",
                    lambda E, hl=hl, par=par: E.tensor_mul(
                        gated[:, hl * TQ:(hl + 1) * TQ], silu_bf[par][:, :],
                        ps_u[:, :]), inc="dv")
                t_mul_prev = t_gated_wr[par]

            t_gated_all = ("dv", p.cnt["dv"])

            p.wait("tensor", t_gated_all)
            for mc in range(16):
                par = mc % 2
                t_d_load[par] = p.emit("gpsimd",
                                       lambda E, oo=oo, mc=mc, par=par:
                                       E.dma_start(
                                           d_buf[par][:, :],
                                           d_pk[mc][:, oo * OCT * 128:
                                                    (oo + 1) * OCT * 128]),
                                       waits=[t_d_used[par]],
                                       inc=f"dwd{par}", inc_by=16)
                if oo == NOCT - 1:
                    t_x1r2[par] = p.emit("sync",
                                         lambda E, mc=mc, par=par: E.dma_start(
                                             x1r_buf[par][:, :], x1_scr[mc]),
                                         waits=[t_yst[par]],
                                         inc=f"dx1r{par}", inc_by=16)
                p.wait("tensor", t_d_load[par], t_dev[par])
                for kc in range(OCT):
                    for nh in range(2):
                        p.emit("tensor",
                               lambda E, kc=kc, nh=nh, par=par: E.matmul(
                                   ps_d[par][:, nh * 512:(nh + 1) * 512],
                                   d_buf[par][:, kc * 128:(kc + 1) * 128],
                                   gated[:, kc * TQ + nh * 512:
                                         kc * TQ + (nh + 1) * 512],
                                   start=(kc == 0), stop=(kc == OCT - 1)),
                               inc="pe" if (kc == OCT - 1 and nh == 1) else None)
                t_dmm = ("pe", p.cnt["pe"])
                t_d_used[par] = t_dmm
                p.wait("vector", t_dmm)
                if oo == 0:
                    t_dev[par] = p.emit("vector",
                                        lambda E, mc=mc, par=par: E.tensor_copy(
                                            ffn_acc[:, mc * TQ:(mc + 1) * TQ],
                                            ps_d[par][:, :]), inc="dv")
                elif oo < NOCT - 1:
                    t_dev[par] = p.emit("vector",
                                        lambda E, mc=mc, par=par:
                                        E.tensor_tensor(
                                            ffn_acc[:, mc * TQ:(mc + 1) * TQ],
                                            ffn_acc[:, mc * TQ:(mc + 1) * TQ],
                                            ps_d[par][:, :], ALU.add), inc="dv")
                else:
                    p.wait("vector", t_x1r2[par])
                    t_dev[par] = p.emit(
                        "vector",
                        lambda E, mc=mc, par=par: (
                            E.tensor_tensor(dtmp[par][:, :], ps_d[par][:, :],
                                            ffn_acc[:, mc * TQ:(mc + 1) * TQ],
                                            ALU.add),
                            SPC(E),
                            E.tensor_tensor(dtmp[par][:, :], dtmp[par][:, :],
                                            x1r_buf[par][:, :], ALU.add))[-1],
                        inc="dv")
                    t_yst[par] = p.emit("sync",
                                        lambda E, mc=mc, par=par: E.dma_start(
                                            y_out[mc * 128:(mc + 1) * 128, :],
                                            dtmp[par][:, :]),
                                        waits=[t_dev[par]], inc="dso", inc_by=16)
            t_dp_done = ("pe", p.cnt["pe"])

        p.wait("sync", ("dso", p.cnt["dso"]))

        with nc.Block() as block:
            @block.tensor
            def _(E):
                for fn in p.q["tensor"]:
                    fn(E)

            @block.vector
            def _(E):
                for fn in p.q["vector"]:
                    fn(E)

            @block.scalar
            def _(E):
                for fn in p.q["scalar"]:
                    fn(E)

            @block.gpsimd
            def _(E):
                for fn in p.q["gpsimd"]:
                    fn(E)

            @block.sync
            def _(E):
                for fn in p.q["sync"]:
                    fn(E)

    return nc



_NC_CACHE = {}


def _get_nc(dbg=False):
    if dbg not in _NC_CACHE:
        _NC_CACHE[dbg] = build_nc(dbg)
    return _NC_CACHE[dbg]


def kernel(**inputs):
    x = np.asarray(inputs["L0_attn_x"], np.float32)
    w = _stage_weights(inputs)
    in_maps = []
    for c in range(8):
        b, half = c // 2, c % 2
        own = x[b, half * TQ:(half + 1) * TQ, :]
        other = x[b, (1 - half) * TQ:(2 - half) * TQ, :]
        xT = np.ascontiguousarray(np.concatenate([own, other], axis=0).T)
        m = {"xT": xT}
        m.update(w)
        in_maps.append(m)
    nc = _get_nc(False)
    res = run_bass_kernel_spmd(nc, in_maps, core_ids=list(range(8)))
    out = np.empty((NB, S, D), np.float32)
    for c in range(8):
        b, half = c // 2, c % 2
        out[b, half * TQ:(half + 1) * TQ, :] = res.results[c]["y"].T
    return out
